# version 10 (final)
# baseline (speedup 1.0000x reference)
import numpy as np
from contextlib import ExitStack

import concourse.bass as bass
import concourse.mybir as mybir
from concourse.bass_utils import run_bass_kernel_spmd

F32 = mybir.dt.float32
BF16 = mybir.dt.bfloat16
I8 = mybir.dt.int8
I16 = mybir.dt.int16
I32 = mybir.dt.int32
OP = mybir.AluOpType
ACT = mybir.ActivationFunctionType
AX = mybir.AxisListType
MAGIC = 12582912.0

T = 2048
D = 1024
DIP = 4240
DI = 2048
CD = 2176
NH = 16
HD = 128
DS = 64
CH = 256
NCH = 8
DFF = 3072
NKD = 8
NKI = 16
NCF = 24
NTT = 4
NTP = 16
EPS = 1e-5
QA = 32767.0
QW = 7.0

SBUF_BYTES = 229376
BASE = 27648


class Sem:
    __slots__ = ("h", "n", "nc")

    def __init__(self, nc, stack, name):
        self.h = stack.enter_context(nc.semaphore(name))
        self.n = 0
        self.nc = nc

    def inc_raw_cc(self, instr, k=1):
        instr.then_inc(self.h, k)
        self.n += k
        return instr

    def inc(self, instr, k=1, drain=True):
        if k == 1 and drain and instr.ins.engine != mybir.EngineType.PE:
            eng = self.nc.engines[instr.ins.engine]
            eng.maybe_drain_then_inc((self.h, k), fusable=True)
        else:
            instr.then_inc(self.h, k)
        self.n += k
        return instr


class Ctx:
    pass


class Arena:

    def __init__(self, nc):
        self.nc = nc
        self.items = []

    def at(self, name, off, shape, dt, p0, p1):
        per = int(np.prod(shape[1:])) * mybir.dt.size(dt)
        assert BASE <= off and off + per <= SBUF_BYTES, (name, off, per)
        for (n2, o2, s2, q0, q1) in self.items:
            if not (p1 < q0 or q1 < p0):
                if not (off + per <= o2 or o2 + s2 <= off):
                    raise AssertionError(f"overlap {name} vs {n2}")
        self.items.append((name, off, per, p0, p1))
        return self.nc.alloc_sbuf_tensor_at(name, shape, dt, offset=off).ap()


def build(stage_limit=99):
    nc = bass.Bass("TRN2", target_bir_lowering=False, detect_race_conditions=False)
    c = Ctx()
    c.nc = nc
    c.stage_limit = stage_limit

    c.x_hbm = nc.dram_tensor("x", [T, D], F32, kind="ExternalInput").ap()
    c.w_in_hbm = nc.dram_tensor("w_in", [D, DIP], F32, kind="ExternalInput").ap()
    c.conv_w_hbm = nc.dram_tensor("conv_w", [4, CD], F32, kind="ExternalInput").ap()
    c.conv_b_hbm = nc.dram_tensor("conv_b", [CD], F32, kind="ExternalInput").ap()
    c.dt_bias_hbm = nc.dram_tensor("dt_bias", [NH], F32, kind="ExternalInput").ap()
    c.a_log_hbm = nc.dram_tensor("a_log", [NH], F32, kind="ExternalInput").ap()
    c.dvec_hbm = nc.dram_tensor("dvec", [NH], F32, kind="ExternalInput").ap()
    c.ns_hbm = nc.dram_tensor("norm_scale", [DI], F32, kind="ExternalInput").ap()
    c.w_out_hbm = nc.dram_tensor("w_out", [DI, D], F32, kind="ExternalInput").ap()
    c.w_gate_hbm = nc.dram_tensor("w_gate", [DFF, D], F32, kind="ExternalInput").ap()
    c.w_up_hbm = nc.dram_tensor("w_up", [DFF, D], F32, kind="ExternalInput").ap()
    c.w_down_hbm = nc.dram_tensor("w_down", [D, DFF], F32, kind="ExternalInput").ap()
    c.out_hbm = nc.dram_tensor("out", [T, D], F32, kind="ExternalOutput").ap()

    c.z_sp = nc.dram_tensor("z_sp", [DI, T], BF16)
    c.up_sp = nc.dram_tensor("up_sp", [DFF, T], F32)
    c.gate_sp = nc.dram_tensor("gate_sp", [DFF, T], F32)
    c.sn_sp = nc.dram_tensor("sn_sp", [D, T], BF16)
    c.dt_dram = nc.dram_tensor("dt_dram", [NH, T], F32)
    c.acs_dram = nc.dram_tensor("acs_dram", [NH, T], F32)
    c.e_dram = nc.dram_tensor("e_dram", [NH, T], F32)
    c.eal_dram = nc.dram_tensor("eal_dram", [NH, NCH], F32)
    c.row_dram = nc.dram_tensor("row_dram", [4, T], F32)
    c.sc_dram = nc.dram_tensor("sc_dram", [1, 16], F32)
    c.cc_in = [nc.dram_tensor(f"cc_in{i}", [1, 8], F32) for i in range(4)]
    c.cc_out = [nc.dram_tensor(f"cc_out{i}", [1, 8], F32) for i in range(4)]

    c.dbg = {}

    def tap(name, shape):
        c.dbg[name] = nc.dram_tensor("dbg_" + name, shape, F32,
                                     kind="ExternalOutput").ap()

    if stage_limit < 99:
        tap("s128", [128, 128])
        tap("xn_tm", [T, D])
        tap("gmax", [1, 16])
        tap("lnst", [128, 12])
        tap("red", [128, 8])
        if stage_limit >= 2:
            tap("z", [DI, T])
            tap("xbc", [CD, T])
            tap("dtr", [NH, T])
        if stage_limit >= 3:
            tap("conv", [CD, T])
            tap("acs", [NH, T])
            tap("dtv", [NH, T])
        if stage_limit >= 4:
            tap("upt", [DFF, T])
            tap("gt", [DFF, T])
            tap("xqv", [D, T])
            tap("wq0", [D, 1536])
        if stage_limit >= 5:
            tap("y", [DI, T])
            tap("xsc2", [T, DI])
        if stage_limit >= 6:
            tap("sn", [D, T])
            tap("rsq", [128, T])
            tap("rrow", [1, T])
        if stage_limit >= 7:
            tap("scal2", [1, 32])
            tap("wqd", [DFF, D])
            tap("g2d", [DFF, T])

    with ExitStack() as ST:
        E = ST.enter_context
        c.ST = ST

        def sb(name, shape, dt=F32):
            t = E(nc.sbuf_tensor(name, shape, dt))
            setattr(c, name, t)
            return t

        sb("sid_bf", [128, 128], BF16)
        sb("sid_f", [128, 128], F32)
        sb("small", [NH, 16])
        sb("dcol", [128, NH])
        sb("cwt", [128, 17 * 5])
        sb("scal", [1, 32])
        sb("scbc", [128, 16])
        sb("red", [128, 8])
        sb("scr128", [128, 128])
        sb("rowt", [1, 128])
        sb("dt_tm", [128, NTP, NH])
        sb("acs_tm", [128, NTP, NH])
        sb("df_tm", [128, NTP, NH])
        sb("rsq", [128, T], BF16)
        sb("ln_st", [128, 12])
        sb("epsc", [128, 1])
        sb("wbm", [128, 32])
        sb("wrs", [128, 32])
        sb("nwt1", [128, 32])
        sb("nwt2", [128, 32])
        sb("wsc", [128, 32])
        sb("wdmx", [128, 32])
        sb("wlna", [128, 32])
        sb("wlnb", [128, 32])
        sb("c_e12", [128, 1])
        sb("onescol_bf", [128, 1], BF16)
        sb("c_e7", [128, 1])

        A = Arena(nc)
        S1 = BASE
        S2 = S1 + 67840
        S3 = S2 + 65536
        GQ0 = SBUF_BYTES - 98304 - 32
        STG = S2
        assert S3 + 65536 <= SBUF_BYTES

        def cur_alloc(base):
            cur = [base]

            def nxt(name, shape, dt, p0, p1):
                t = A.at(name, cur[0], shape, dt, p0, p1)
                setattr(c, name, t)
                cur[0] += int(np.prod(shape[1:])) * mybir.dt.size(dt)
                return cur[0]
            return nxt

        c.w_in_bf = A.at("w_in_bf", S1, [128, NKD, DIP], BF16, 0, 3)
        n1 = cur_alloc(S1)
        n1("conv_acc", [128, 2, T], F32, 4, 4)
        n1("dt_fm", [NH, T], F32, 4, 4)
        n1("acs_fm", [NH, T], F32, 4, 4)
        n1("scr16", [NH, T], F32, 4, 4)
        n1("scr16b", [NH, T], F32, 4, 4)
        n1("ones16", [NH, CH], F32, 4, 4)
        n1("scr16c", [NH, T], F32, 4, 4)
        c.wq_h = [A.at(f"wq_h{i}", S1 + i * 24576, [128, NKD, DFF // 2], BF16, 5, 6)
                  for i in range(2)]
        n2 = cur_alloc(S1 + 49152)
        n2("wdq_scr", [128, 2, D], F32, 5, 6)
        n2("wdq_t", [128, D], F32, 5, 6)
        c.bcf = A.at("bcf", S1 + 61440, [128, T], BF16, 4, 8)
        c.btm = A.at("btm", S1 + 65536, [128, NTP, DS], BF16, 4, 8)
        n3 = cur_alloc(S1)
        n3("acsbc", [128, 2, T], F32, 7, 8)
        n3("zbuf", [128, 2, T], BF16, 7, 8)
        n3("s_st", [64, NH * HD], F32, 7, 8)
        n3("gtm", [128, NCH, 2, CH], BF16, 7, 8)
        n3("d_i", [128, NH * 128], BF16, 7, 8)
        n3("wt", [128, 2, 2, CH], BF16, 7, 8)
        n3("dscr", [128, 2, CH], F32, 7, 8)
        n3("escr", [128, 2, CH], BF16, 7, 8)
        n3("csc", [64, 2, CH], BF16, 7, 8)
        n3("bsc", [128, 2, DS], BF16, 7, 8)
        n3("sbf", [64, HD], BF16, 7, 8)
        n3("e2", [64, NH * NCH], F32, 7, 8)
        n3("cfm", [64, T], BF16, 7, 8)
        n3("e64f", [64, CH], F32, 7, 8)
        n4 = cur_alloc(S1)
        n4("w_out_bf", [128, NKI, D], BF16, 9, 9)
        n4("wo_scr", [128, 2, D], F32, 9, 9)
        n4("ns_col", [128, NKI], F32, 9, 9)
        n4("ysq", [128, 2, 512], BF16, 9, 9)
        n4("snst", [128, 2, 512], BF16, 9, 9)
        n4("rrow", [1, T], F32, 9, 9)
        n4("rrow2", [1, T], F32, 9, 9)
        n5 = cur_alloc(S1)
        n5("wq_d", [128, NCF, D], BF16, 11, 12)
        n5("ofm", [128, NCF, 256], BF16, 11, 12)
        n5("gug", [128, 8, 256], BF16, 12, 12)

        c.ln_a = A.at("ln_a", S2, [128, 2, D], F32, 1, 1)
        c.ln_scr = A.at("ln_scr", S2 + 8192, [128, D], F32, 1, 1)
        c.cw5 = A.at("cw5", S2 + 12288, [5, CD], F32, 0, 2)
        c.xn_tm = A.at("xn_tm", S2 + 21504, [128, NTP, D], BF16, 1, 2)
        c.xbc = A.at("xbc", S2, [128, 16, T], BF16, 3, 7)
        c.yb = A.at("yb", S2, [128, NH, T], BF16, 8, 9)
        n6 = cur_alloc(STG)
        n6("ffn_a", [128, 512], F32, 10, 12)
        n6("ffn_b", [128, 512], F32, 10, 12)
        n6("ffn_c", [128, 512], I32, 10, 12)
        n6("ffn_d", [128, 512], I32, 10, 12)
        n6("wdq_scr2", [128, D], F32, 10, 12)
        n6("wdq_q2", [128, D], I8, 10, 12)
        n6("wdq_n2", [128, D], BF16, 10, 12)
        n6("ffn_p", [128, 512], F32, 10, 12)
        n6("fin_a", [128, 512], F32, 12, 12)
        n6("fin_x", [128, D], F32, 12, 12)
        n6("fin_o", [128, D], F32, 12, 12)
        end6 = n6("snbuf", [128, 2, 256], BF16, 12, 12)
        assert end6 <= GQ0, end6

        c.xn_fm = A.at("xn_fm", S3, [128, NKD, T], BF16, 2, 6)
        c.bc_tmp = A.at("bc_tmp", S3 + 32768, [128, T], BF16, 3, 6)
        c.st16 = A.at("st16", S3 + 36864, [16, 512], F32, 3, 3)
        c.zst = A.at("zst", S3 + 38912, [128, 2, 512], BF16, 3, 3)
        c.ffn_st = A.at("ffn_st", S3 + 36864, [128, 2, 512], F32, 5, 6)
        c.wdq_q = A.at("wdq_q", S3 + 40960, [128, D], I8, 5, 6)
        c.wdq_n = A.at("wdq_n", S3 + 41984, [128, D], BF16, 5, 6)
        c.xsc = A.at("xsc", S3, [128, NTP, DI], BF16, 7, 8)

        c.gq = A.at("gq", GQ0, [128, NCF, T], I16, 10, 12)

        c.pA = E(nc.psum_tensor("pA", [128, 512], F32))
        c.pB = E(nc.psum_tensor("pB", [128, 512], F32))
        c.pC = E(nc.psum_tensor("pC", [128, 512], F32))
        c.pD = E(nc.psum_tensor("pD", [128, 512], F32))
        c.pT = E(nc.psum_tensor("pT", [128, 1024], BF16))
        c.pU = E(nc.psum_tensor("pU", [128, 512], F32))
        c.pV = E(nc.psum_tensor("pV", [128, 512], F32))
        c.pW = E(nc.psum_tensor("pW", [128, 512], F32))

        names = ["ld_x", "ld_w", "ld_sm", "idn", "mz", "ln_sum", "ln_ap",
                 "xn_mx", "mm", "tp_e", "tp2", "ar_l", "ar_d", "sc_bc",
                 "cv", "dtp", "dts", "xq_v", "wdv", "wdg", "wdt",
                 "spl", "fmx", "ar2", "prep", "gt_e", "bc_d", "zr",
                 "ssd_p", "yev", "sup", "sq", "rs_r", "wos", "snv",
                 "rq", "sg2", "ar4", "oq", "ar5", "dwn", "fg", "fx",
                 "fout", "sw", "dbgs", "ip_e", "ln_iv", "cvs", "dtw", "ld_ffw", "spl2", "fev", "sws", "wds", "ssd_d", "ssd_e", "ysqs", "ld_wo", "rsqd", "rq_d", "rq_s", "fsg", "ar4s", "ar5s", "oq2", "rqr_a", "rqr_b", "rqr"]
        c.S = {n: Sem(nc, ST, n) for n in names}
        c.bank_ev = [Sem(nc, ST, f"bk{i}") for i in range(4)]
        c.banks = [c.pA, c.pB, c.pC, c.pD]
        c.bank_uses = [0, 0, 0, 0]
        c.mm_n = 0

        c.block = E(nc.Block())

        _p0_consts(c)
        _p1_ln(c)
        _p2_xn_transpose(c)
        _p2b_ar1(c)
        if stage_limit >= 2:
            _p3_in_proj(c)
        if stage_limit >= 3:
            _p4_conv_dtprep(c)
        if stage_limit >= 4:
            _p5_xq(c)
            _p6_upgate(c)
        if stage_limit >= 5:
            _p7_ssdprep(c)
            _p8_ssd(c)
        if stage_limit >= 6:
            _p9_rsqrt_outproj(c)
        if stage_limit >= 7:
            _p10_requant(c)
            _p11_ophase(c)
            _p12_final(c)
        _taps(c)
        _finish(c)
        left = nc._sbuf_addr_for_side("left")
        assert left <= BASE, f"auto sbuf {left} > BASE {BASE}"
        return nc, c.dbg


def _p0_consts(c):
    S, block = c.S, c.block

    @block.vector
    def _(v):
        v.memset(c.sid_bf[:], 1.0)
        v.memset(c.sid_f[:], 1.0)
        v.memset(c.epsc[:], EPS)
        v.memset(c.c_e12[:], 1e-12)
        v.memset(c.onescol_bf[:], 1.0)
        v.memset(c.c_e7[:], 7e-12)
        v.memset(c.scal[:], 0.0)
        v.memset(c.ln_st[:], 0.0)
        v.memset(c.red[:], 0.0)
        S["mz"].inc(v.memset(c.scr128[:], 0.0))

    @block.gpsimd
    def _(g):
        g.wait_ge(S["mz"].h, 1)
        g.affine_select(out=c.sid_bf[:], in_=c.sid_bf[:], pattern=[[-1, 128]],
                        compare_op=OP.is_equal, fill=0.0, base=0,
                        channel_multiplier=1)
        g.affine_select(out=c.sid_f[:], in_=c.sid_f[:], pattern=[[-1, 128]],
                        compare_op=OP.is_equal, fill=0.0, base=0,
                        channel_multiplier=1)
        S["idn"].inc(g.engine_nop())
        for k in range(NKD):
            S["ld_w"].inc(g.dma_start(out=c.w_in_bf[:, k, :],
                                      in_=c.w_in_hbm[k * 128:(k + 1) * 128, :]), 16)

    @block.sync
    def _(sy):
        S["ld_sm"].inc(sy.dma_start(out=c.small[:, 0:1],
                                    in_=c.dt_bias_hbm.unsqueeze(1)), 16)
        S["ld_sm"].inc(sy.dma_start(out=c.small[:, 2:3],
                                    in_=c.a_log_hbm.unsqueeze(1)), 16)
        S["ld_sm"].inc(sy.dma_start(out=c.cw5[0:4, :], in_=c.conv_w_hbm[:, :]), 16)
        S["ld_sm"].inc(sy.dma_start(out=c.cw5[4:5, :],
                                    in_=c.conv_b_hbm.unsqueeze(0)), 16)
        S["ld_sm"].inc(sy.dma_start(out=c.row_dram[3:4, 0:NH],
                                    in_=c.dvec_hbm.unsqueeze(0)), 16)
        sy.wait_ge(S["ld_sm"].h, S["ld_sm"].n)
        S["ld_sm"].inc(sy.dma_start(
            out=c.dcol[:, :],
            in_=c.row_dram[3:4, 0:NH].broadcast_to([128, NH])), 16)

    @block.scalar
    def _(s):
        s.wait_ge(S["ld_sm"].h, 96)
        s.activation(out=c.small[:, 1:2], in_=c.small[:, 2:3], func=ACT.Exp)
        S["ld_sm"].inc(s.mul(out=c.small[:, 1:2], in_=c.small[:, 1:2], mul=-1.0), 1)

    @block.vector
    def _(v):
        v.wait_ge(S["idn"].h, 1)
        S["idn"].inc(v.engine_nop())

    @block.tensor
    def _(t):
        t.wait_ge(S["ld_sm"].h, 64)
        t.wait_ge(S["idn"].h, 1)
        for cc in range(17):
            ins = t.transpose(c.pU[0:128, cc * 5:cc * 5 + 5],
                              c.cw5[0:5, cc * 128:(cc + 1) * 128],
                              c.sid_f[0:5, 0:5])
        S["dtp"].inc(ins)

    @block.vector
    def _(v):
        v.wait_ge(S["dtp"].h, 1)
        v.tensor_copy(out=c.scr128[0:1, 126:127], in_=c.scr128[0:1, 125:126])
        v.tensor_copy(out=c.scr128[0:1, 126:127], in_=c.scr128[0:1, 125:126])
        S["dtp"].inc(v.tensor_copy(out=c.cwt[:, 0:85], in_=c.pU[:, 0:85]))


def _p1_ln(c):
    S, block = c.S, c.block

    @block.sync
    def _(sy):
        for i in range(NTP):
            if i >= 2:
                sy.wait_ge(S["ln_ap"].h, i - 1)
            S["ld_x"].inc(sy.dma_start(out=c.ln_a[:, i % 2, :],
                                       in_=c.x_hbm[i * 128:(i + 1) * 128, :]), 16)

    @block.vector
    def _(v):
        for i in range(NTP):
            v.wait_ge(S["ld_x"].h, (i + 1) * 16)
            S["ln_sum"].inc(v.reduce_sum(out=c.ln_st[:, 0:1],
                                         in_=c.ln_a[:, i % 2, :], axis=AX.X))
            v.wait_ge(S["ln_iv"].h, i + 1)
            S["ln_ap"].inc(v.tensor_scalar(out=c.xn_tm[:, i, :],
                                           in0=c.ln_a[:, i % 2, :],
                                           scalar1=c.ln_st[:, 2:3],
                                           scalar2=c.ln_st[:, 8:9],
                                           op0=OP.subtract, op1=OP.mult))
            v.tensor_reduce(out=c.red[:, 0:1], in_=c.xn_tm[:, i, :], axis=AX.X,
                            op=OP.max, apply_absolute_value=True)
            if i == 0:
                ins = v.tensor_copy(out=c.red[:, 1:2], in_=c.red[:, 0:1])
            else:
                ins = v.tensor_tensor(out=c.red[:, 1:2], in0=c.red[:, 1:2],
                                      in1=c.red[:, 0:1], op=OP.max)
            S["xn_mx"].inc(ins)

    @block.scalar
    def _(s):
        for i in range(NTP):
            s.wait_ge(S["ld_x"].h, (i + 1) * 16)
            if i >= 1:
                s.wait_ge(S["ln_ap"].h, i)
            s.activation(out=c.ln_scr[:], in_=c.ln_a[:, i % 2, :],
                         func=ACT.Square, accum_out=c.ln_st[:, 1:2])
            s.wait_ge(S["ln_sum"].h, i + 1)
            s.mul(out=c.ln_st[:, 2:3], in_=c.ln_st[:, 0:1], mul=1.0 / D)
            s.wait_ge(S["mz"].h, 1)
            s.activation(out=c.ln_st[:, 3:4], in_=c.ln_st[:, 1:2], func=ACT.Identity,
                         scale=1.0 / D, bias=c.epsc[:, 0:1])
            s.activation(out=c.ln_st[:, 4:5], in_=c.ln_st[:, 2:3], func=ACT.Square)
            s.activation(out=c.ln_st[:, 5:6], in_=c.ln_st[:, 4:5], func=ACT.Identity,
                         scale=-1.0, bias=c.ln_st[:, 3:4])
            s.activation(out=c.ln_st[:, 7:8], in_=c.ln_st[:, 5:6], func=ACT.Ln)
            S["ln_iv"].inc(s.activation(out=c.ln_st[:, 8:9], in_=c.ln_st[:, 7:8],
                                        func=ACT.Exp, scale=-0.5))


def _p2_xn_transpose(c):
    S, block = c.S, c.block

    @block.tensor
    def _(t):
        t.wait_ge(S["idn"].h, 2)
        for i in range(NTP):
            t.wait_ge(S["ln_ap"].h, i + 1)
            if i >= 1:
                t.wait_ge(S["tp_e"].h, i)
            for j in range(NKD):
                ins = t.transpose(c.pT[:, j * 128:(j + 1) * 128],
                                  c.xn_tm[:, i, j * 128:(j + 1) * 128],
                                  c.sid_bf[:])
            S["mm"].inc(ins)
            c.mm_n += 1

    @block.vector
    def _(v):
        for i in range(NTP):
            v.wait_ge(S["mm"].h, i + 1)
            v.tensor_copy(out=c.scr128[0:1, 126:127], in_=c.scr128[0:1, 125:126])
            v.tensor_copy(out=c.scr128[0:1, 126:127], in_=c.scr128[0:1, 125:126])
            S["tp_e"].inc(v.tensor_copy(
                out=c.xn_fm[:, :, i * 128:(i + 1) * 128],
                in_=c.pT[:].rearrange("p (a b) -> p a b", a=NKD)))


def _p2b_ar1(c):
    S, block = c.S, c.block

    @block.vector
    def _(v):
        v.wait_ge(S["xn_mx"].h, NTP)
        S["xn_mx"].inc(v.tensor_copy(out=c.scr128[:, 0:1], in_=c.red[:, 1:2]))

    @block.tensor
    def _(t):
        t.wait_ge(S["xn_mx"].h, NTP + 1)
        S["tp2"].inc(t.transpose(c.pU[:, 0:128], c.scr128[:, :], c.sid_f[:]))

    @block.scalar
    def _(s):
        s.wait_ge(S["tp2"].h, 1)
        s.copy(out=c.scr128[0:1, 124:125], in_=c.scr128[0:1, 123:124])
        s.copy(out=c.scr128[0:1, 124:125], in_=c.scr128[0:1, 123:124])
        S["tp2"].inc(s.copy(out=c.rowt[0:1, :], in_=c.pU[0:1, 0:128]))

    @block.vector
    def _(v):
        v.wait_ge(S["tp2"].h, 2)
        S["ar_l"].inc(v.tensor_reduce(out=c.scal[0:1, 0:1], in_=c.rowt[0:1, :],
                                      axis=AX.X, op=OP.max))

    @block.gpsimd
    def _(g):
        g.wait_ge(S["ar_l"].h, 1)
        S["sw"].inc(g.dma_start(out=c.cc_in[0][:, :], in_=c.scal[0:1, 0:8]), 16)
        g.wait_ge(S["sw"].h, S["sw"].n)
        S["ar_d"].inc_raw_cc(g.collective_compute(
            "AllReduce", OP.max, replica_groups=[list(range(8))],
            ins=[c.cc_in[0][:, :].opt()], outs=[c.cc_out[0][:, :].opt()]))
        g.wait_ge(S["ar_d"].h, 1)
        S["sw"].inc(g.dma_start(out=c.scal[0:1, 8:9], in_=c.cc_out[0][0:1, 0:1]), 16)
        g.wait_ge(S["sw"].h, S["sw"].n)
        S["ar_d"].inc(g.engine_nop())

    @block.vector
    def _(v):
        v.wait_ge(S["ar_d"].h, 2)
        S["ar_d"].inc(v.tensor_scalar(out=c.scal[0:1, 9:10], in0=c.scal[0:1, 8:9],
                                      scalar1=1.0 / QA, scalar2=c.c_e12[0:1, 0:1],
                                      op0=OP.mult, op1=OP.max))

    @block.scalar
    def _(s):
        s.wait_ge(S["ar_d"].h, 3)
        s.activation(out=c.scal[0:1, 11:12], in_=c.scal[0:1, 9:10], func=ACT.Ln)
        S["ar_d"].inc(s.activation(out=c.scal[0:1, 10:11], in_=c.scal[0:1, 11:12],
                                   func=ACT.Exp, scale=-1.0))

    @block.sync
    def _(sy):
        sy.wait_ge(S["ar_d"].h, 4)
        S["sws"].inc(sy.dma_start(out=c.sc_dram[0:1, 0:2], in_=c.scal[0:1, 9:11]), 16)
        sy.wait_ge(S["sws"].h, S["sws"].n)
        S["sc_bc"].inc(sy.dma_start(out=c.scbc[:, 0:1],
                                    in_=c.sc_dram[0:1, 1:2].broadcast_to([128, 1])), 16)
        S["sc_bc"].inc(sy.dma_start(out=c.scbc[:, 1:2],
                                    in_=c.sc_dram[0:1, 0:1].broadcast_to([128, 1])), 16)


def _p3_in_proj(c):
    S, block = c.S, c.block
    groups = [(m, tt) for m in range(34) for tt in range(NTT)]
    c.ip_groups = groups
    mm_base = c.mm_n

    @block.tensor
    def _(t):
        t.wait_ge(S["ld_w"].h, NKD * 16)
        t.wait_ge(S["tp_e"].h, NTP)
        for gi, (m, tt) in enumerate(groups):
            bi = gi % 4
            if gi >= 4:
                t.wait_ge(S["ip_e"].h, gi - 3)
            mw = 16 if m == 33 else 128
            for k in range(NKD):
                ins = t.matmul(c.banks[bi][0:mw, 0:512],
                               c.w_in_bf[:, k, m * 128:m * 128 + mw],
                               c.xn_fm[:, k, tt * 512:(tt + 1) * 512],
                               start=(k == 0), stop=(k == NKD - 1))
            S["mm"].inc(ins)
            c.mm_n += 1

    @block.vector
    def _(v):
        zi = 0
        for gi, (m, tt) in enumerate(groups):
            bi = gi % 4
            v.wait_ge(S["mm"].h, mm_base + gi + 1)
            v.tensor_copy(out=c.scr128[0:1, 126:127], in_=c.scr128[0:1, 125:126])
            v.tensor_copy(out=c.scr128[0:1, 126:127], in_=c.scr128[0:1, 125:126])
            if m < 16:
                if zi >= 2:
                    v.wait_ge(S["spl"].h, 16 * (zi - 1))
                ins = v.tensor_copy(out=c.zst[:, zi % 2, :],
                                    in_=c.banks[bi][0:128, 0:512])
                zi += 1
            elif m < 32:
                ins = v.tensor_copy(out=c.xbc[:, m - 16, tt * 512:(tt + 1) * 512],
                                    in_=c.banks[bi][0:128, 0:512])
            elif m == 32:
                ins = v.tensor_copy(out=c.bc_tmp[:, tt * 512:(tt + 1) * 512],
                                    in_=c.banks[bi][0:128, 0:512])
            else:
                if tt >= 1:
                    v.wait_ge(S["spl"].h, 16 * (64 + tt))
                ins = v.tensor_copy(out=c.st16[0:16, :],
                                    in_=c.banks[bi][0:16, 0:512])
            S["ip_e"].inc(ins)

    @block.sync
    def _(sy):
        zi = 0
        for gi, (m, tt) in enumerate(groups):
            if m < 16:
                sy.wait_ge(S["ip_e"].h, gi + 1)
                S["spl"].inc(sy.dma_start(
                    out=c.z_sp[m * 128:(m + 1) * 128, tt * 512:(tt + 1) * 512],
                    in_=c.zst[:, zi % 2, :]), 16)
                zi += 1
            elif m == 33:
                sy.wait_ge(S["ip_e"].h, gi + 1)
                S["spl"].inc(sy.dma_start(
                    out=c.dt_dram[0:16, tt * 512:(tt + 1) * 512],
                    in_=c.st16[0:16, :]), 16)


def _p4_conv_dtprep(c):
    S, block = c.S, c.block

    def src_tile(cc):
        return c.xbc[:, cc, :] if cc < 16 else c.bc_tmp[:, :]

    @block.sync
    def _(sy):
        sy.wait_ge(S["spl"].h, S["spl"].n)
        S["spl"].inc(sy.dma_start(out=c.dt_fm[:, :], in_=c.dt_dram[:, :]), 16)

    dt_ld = S["spl"].n

    @block.vector
    def _(v):
        v.wait_ge(S["dtp"].h, 2)
        for cc in range(17):
            need = min((17 + cc) * 4, 132) if cc < 16 else 132
            v.wait_ge(S["ip_e"].h, need)
            if cc >= 2:
                v.wait_ge(S["cvs"].h, cc - 1)
            src = src_tile(cc)
            acc = c.conv_acc[:, cc % 2, :]
            w = lambda k: c.cwt[:, cc * 5 + k:cc * 5 + k + 1]
            v.tensor_scalar(out=acc, in0=src, scalar1=w(3), scalar2=w(4),
                            op0=OP.mult, op1=OP.add)
            for sh in (1, 2, 3):
                v.scalar_tensor_tensor(out=acc[:, sh:T], in0=src[:, 0:T - sh],
                                       scalar=w(3 - sh), in1=acc[:, sh:T],
                                       op0=OP.mult, op1=OP.add)
            S["cv"].inc(v.engine_nop())

    @block.scalar
    def _(s):
        for cc in range(17):
            s.wait_ge(S["cv"].h, cc + 1)
            S["cvs"].inc(s.activation(out=src_tile(cc), in_=c.conv_acc[:, cc % 2, :],
                                      func=ACT.Silu))

    @block.vector
    def _(v):
        v.wait_ge(S["spl"].h, dt_ld)
        S["dtw"].inc(v.tensor_scalar(out=c.scr16[:, :], in0=c.dt_fm[:, :],
                                     scalar1=c.small[:, 0:1], scalar2=None,
                                     op0=OP.add))

    @block.scalar
    def _(s):
        s.wait_ge(S["dtw"].h, 1)
        s.activation(out=c.acs_fm[:, :], in_=c.scr16[:, :], func=ACT.Abs)
        S["dts"].inc(s.activation(out=c.scr16b[:, :], in_=c.acs_fm[:, :],
                                  func=ACT.Exp, scale=-1.0))

    @block.vector
    def _(v):
        v.wait_ge(S["dts"].h, 1)
        S["dtw"].inc(v.tensor_scalar(out=c.scr16b[:, :], in0=c.scr16b[:, :],
                                     scalar1=1.0, scalar2=None, op0=OP.add))

    @block.scalar
    def _(s):
        s.wait_ge(S["dtw"].h, 2)
        S["dts"].inc(s.activation(out=c.scr16b[:, :], in_=c.scr16b[:, :],
                                  func=ACT.Ln))

    @block.vector
    def _(v):
        v.wait_ge(S["dts"].h, 2)
        v.memset(c.ones16[:], 1.0)
        v.scalar_tensor_tensor(out=c.dt_fm[:, :], in0=c.scr16[:, :], scalar=0.0,
                               in1=c.scr16b[:, :], op0=OP.max, op1=OP.add)
        v.tensor_scalar(out=c.scr16[:, :], in0=c.dt_fm[:, :],
                        scalar1=c.small[:, 1:2], scalar2=None, op0=OP.mult)
        for ch in range(NCH):
            sl = slice(ch * CH, (ch + 1) * CH)
            v.tensor_tensor_scan(out=c.acs_fm[:, sl], data0=c.ones16[:, :],
                                 data1=c.scr16[:, sl], initial=0.0,
                                 op0=OP.mult, op1=OP.add)
        S["dtw"].inc(v.engine_nop())

    @block.scalar
    def _(s):
        s.wait_ge(S["dtw"].h, 3)
        s.activation(out=c.scr16b[:, :], in_=c.acs_fm[:, :], func=ACT.Exp)
        S["dts"].inc(s.activation(out=c.small[:, 3:11],
                                  in_=c.acs_fm[:, CH - 1::CH], func=ACT.Exp))

    @block.sync
    def _(sy):
        sy.wait_ge(S["dts"].h, 3)
        S["spl"].inc(sy.dma_start(out=c.acs_dram[:, :], in_=c.acs_fm[:, :]), 16)
        S["spl"].inc(sy.dma_start(out=c.e_dram[:, :], in_=c.scr16b[:, :]), 16)
        S["spl"].inc(sy.dma_start(out=c.eal_dram[:, :], in_=c.small[:, 3:11]), 16)

    @block.vector
    def _(v):
        v.wait_ge(S["dtw"].h, 3)
        for ch in range(NCH):
            sl = slice(ch * CH, (ch + 1) * CH)
            last = c.acs_fm[:, ch * CH + CH - 1:ch * CH + CH]
            v.tensor_scalar(out=c.scr16[:, sl], in0=c.acs_fm[:, sl],
                            scalar1=last, scalar2=None, op0=OP.subtract)
        S["dtw"].inc(v.engine_nop())

    @block.scalar
    def _(s):
        s.wait_ge(S["dtw"].h, 4)
        S["dts"].inc(s.activation(out=c.scr16c[:, :], in_=c.scr16[:, :],
                                  func=ACT.Exp, scale=-1.0))

    @block.tensor
    def _(t):
        t.wait_ge(S["dtw"].h, 3)
        for i in range(NTP):
            t.transpose(c.pU[0:128, i * 16:(i + 1) * 16],
                        c.dt_fm[0:16, i * 128:(i + 1) * 128], c.sid_f[0:16, 0:16])
        for i in range(NTP):
            ins = t.transpose(c.pU[0:128, 256 + i * 16:256 + (i + 1) * 16],
                              c.acs_fm[0:16, i * 128:(i + 1) * 128],
                              c.sid_f[0:16, 0:16])
        S["tp2"].inc(ins)
        t.wait_ge(S["dts"].h, 4)
        for i in range(NTP):
            ins = t.transpose(c.pV[0:128, i * 16:(i + 1) * 16],
                              c.scr16c[0:16, i * 128:(i + 1) * 128],
                              c.sid_f[0:16, 0:16])
        S["tp2"].inc(ins)
        t.wait_ge(S["cvs"].h, 17)
        for i in range(NTP):
            ins = t.transpose(c.pT[:, i * 64:(i + 1) * 64],
                              c.bc_tmp[0:64, i * 128:(i + 1) * 128],
                              c.sid_bf[0:64, 0:64])
        S["tp2"].inc(ins)

    tp_base = 2

    @block.vector
    def _(v):
        v.wait_ge(S["tp2"].h, tp_base + 1)
        v.tensor_copy(out=c.scr128[0:1, 126:127], in_=c.scr128[0:1, 125:126])
        v.tensor_copy(out=c.scr128[0:1, 126:127], in_=c.scr128[0:1, 125:126])
        v.tensor_copy(out=c.dt_tm[:].rearrange("p a b -> p (a b)"),
                      in_=c.pU[:, 0:256])
        v.tensor_copy(out=c.acs_tm[:].rearrange("p a b -> p (a b)"),
                      in_=c.pU[:, 256:512])
        v.wait_ge(S["tp2"].h, tp_base + 2)
        v.tensor_copy(out=c.scr128[0:1, 126:127], in_=c.scr128[0:1, 125:126])
        v.tensor_copy(out=c.scr128[0:1, 126:127], in_=c.scr128[0:1, 125:126])
        v.tensor_copy(out=c.df_tm[:].rearrange("p a b -> p (a b)"),
                      in_=c.pV[:, 0:256])
        v.wait_ge(S["tp2"].h, tp_base + 3)
        v.tensor_copy(out=c.scr128[0:1, 126:127], in_=c.scr128[0:1, 125:126])
        v.tensor_copy(out=c.scr128[0:1, 126:127], in_=c.scr128[0:1, 125:126])
        v.tensor_copy(out=c.btm[:].rearrange("p a b -> p (a b)"),
                      in_=c.pT[:, 0:1024])
        S["prep"].inc(v.tensor_copy(out=c.bcf[:, :], in_=c.bc_tmp[:, :]))


def _p5_xq(c):
    S, block = c.S, c.block
    mm_after_p3 = c.mm_n
    qi16 = c.ffn_st[:, 0, :].bitcast(I16)

    @block.vector
    def _(v):
        v.wait_ge(S["sc_bc"].h, 32)
        v.wait_ge(S["mm"].h, mm_after_p3)
        v.wait_ge(S["dts"].h, 4)
        v.wait_ge(S["cvs"].h, 17)
        for k in range(NKD):
            for hf in range(2):
                sl = slice(hf * 1024, (hf + 1) * 1024)
                v.tensor_scalar(out=c.wdq_t[:, :], in0=c.xn_fm[:, k, sl],
                                scalar1=c.scbc[:, 0:1], scalar2=MAGIC,
                                op0=OP.mult, op1=OP.add)
                v.tensor_scalar(out=qi16, in0=c.wdq_t[:, :],
                                scalar1=MAGIC, scalar2=None, op0=OP.subtract)
                v.tensor_scalar(out=c.xn_fm[:, k, sl], in0=qi16,
                                scalar1=c.scbc[:, 1:2], scalar2=None, op0=OP.mult)
        S["xq_v"].inc(v.engine_nop())


def _p6_upgate(c):
    S, block = c.S, c.block
    HALVES = [("up", 0), ("up", 1), ("gate", 0), ("gate", 1)]
    whbm = {"up": c.w_up_hbm, "gate": c.w_gate_hbm}
    wsp = {"up": c.up_sp, "gate": c.gate_sp}

    items = []
    for ch in range(12):
        items.append(("deq", 0, ch))
    for H in range(4):
        mmq = [("mm", H, m, tt) for m in range(12) for tt in range(NTT)]
        deq = [("deq", H + 1, ch) for ch in range(12)] if H < 3 else []
        di = 0
        for i, it in enumerate(mmq):
            items.append(it)
            if i % 4 == 3 and di < len(deq):
                items.append(deq[di]); di += 1
        items.extend(deq[di:])
    c.p6_items = items

    mm_base = c.mm_n
    nitems = len(items)

    def ld_chunk(sy, ci):
        H, ch = divmod(ci, 12)
        w, hf = HALVES[H]
        if ci >= 2:
            sy.wait_ge(S["wdv"].h, 2 * (ci - 2) + 2)
        S["ld_ffw"].inc(sy.dma_start(
            out=c.wdq_scr[:, ci % 2, :],
            in_=whbm[w][hf * 1536 + ch * 128: hf * 1536 + (ch + 1) * 128, :]), 16)

    @block.sync
    def _(sy):
        sy.wait_ge(S["tp2"].h, 5)
        sy.wait_ge(S["dts"].h, 4)
        sy.wait_ge(S["cvs"].h, 17)
        ld_chunk(sy, 0)
        ld_chunk(sy, 1)
        seq = 0
        for it in items:
            if it[0] == "deq":
                ci = it[1] * 12 + it[2]
                if ci + 2 < 48:
                    ld_chunk(sy, ci + 2)
            else:
                _, H, m, tt = it
                w, hf = HALVES[H]
                sy.wait_ge(S["fev"].h, seq + 1)
                S["spl2"].inc(sy.dma_start(
                    out=wsp[w][hf * 1536 + m * 128: hf * 1536 + (m + 1) * 128,
                               tt * 512:(tt + 1) * 512],
                    in_=c.ffn_st[:, seq % 2, :]), 16)
                seq += 1

    @block.vector
    def _(v):
        v.wait_ge(S["xq_v"].h, 1)
        seq = 0
        idx = 0
        for it in items:
            idx += 1
            if it[0] == "deq":
                H, ch = it[1], it[2]
                ci = H * 12 + ch
                scr = c.wdq_scr[:, ci % 2, :]
                v.wait_ge(S["ld_ffw"].h, (ci + 1) * 16)
                if ci >= 1:
                    v.wait_ge(S["wdg"].h, 2 * ci)
                    v.wait_ge(S["wds"].h, ci)
                v.tensor_reduce(out=c.wbm[:].unsqueeze(2),
                                in_=scr.rearrange("p (b k) -> p b k", k=32),
                                axis=AX.X, op=OP.max, apply_absolute_value=True)
                S["wdv"].inc(v.tensor_scalar(out=c.wdmx[:], in0=c.wbm[:],
                                             scalar1=c.c_e7[:, 0:1],
                                             scalar2=None, op0=OP.max))
                v.wait_ge(S["wdg"].h, 2 * ci + 1)
                v.tensor_scalar(out=scr, in0=c.wdq_t[:, :], scalar1=MAGIC,
                                scalar2=None, op0=OP.add)
                S["wdv"].inc(v.tensor_scalar(out=c.wdq_q[:, :], in0=scr,
                                             scalar1=MAGIC, scalar2=None,
                                             op0=OP.subtract))
                v.wait_ge(S["mm"].h, mm_base + idx)
                v.tensor_copy(out=c.scr128[0:1, 126:127], in_=c.scr128[0:1, 125:126])
                v.tensor_copy(out=c.scr128[0:1, 126:127], in_=c.scr128[0:1, 125:126])
                S["wdt"].inc(v.tensor_copy(
                    out=c.wq_h[H % 2][:, :, ch * 128:(ch + 1) * 128],
                    in_=c.pT[:].rearrange("p (a b) -> p a b", a=NKD)))
            else:
                v.wait_ge(S["mm"].h, mm_base + idx)
                v.tensor_copy(out=c.scr128[0:1, 126:127], in_=c.scr128[0:1, 125:126])
                v.tensor_copy(out=c.scr128[0:1, 126:127], in_=c.scr128[0:1, 125:126])
                slot = seq % 2
                if seq >= 2:
                    v.wait_ge(S["spl2"].h, (seq - 1) * 16)
                v.tensor_copy(out=c.ffn_st[:, slot, :],
                              in_=c.banks[seq % 4][0:128, 0:512])
                col = 2 if it[1] < 2 else 3
                v.tensor_reduce(out=c.red[:, 4:5], in_=c.ffn_st[:, slot, :],
                                axis=AX.X, op=OP.max, apply_absolute_value=True)
                if seq == 0 or (col == 3 and seq == 96):
                    ins = v.tensor_copy(out=c.red[:, col:col + 1], in_=c.red[:, 4:5])
                else:
                    ins = v.tensor_tensor(out=c.red[:, col:col + 1],
                                          in0=c.red[:, col:col + 1],
                                          in1=c.red[:, 4:5], op=OP.max)
                S["fev"].inc(ins)
                S["fmx"].inc(v.engine_nop())
                seq += 1

    @block.gpsimd
    def _(g):
        seq = 0
        for it in items:
            if it[0] == "deq":
                H, ch = it[1], it[2]
                ci = H * 12 + ch
                scr = c.wdq_scr[:, ci % 2, :]
                g.wait_ge(S["wds"].h, ci + 1)
                g.tensor_tensor(out=c.nwt1[:], in0=c.wdmx[:], in1=c.wlnb[:],
                                op=OP.mult)
                g.tensor_scalar(out=c.nwt2[:], in0=c.nwt1[:], scalar1=-1.0,
                                scalar2=2.0, op0=OP.mult, op1=OP.add)
                g.tensor_tensor(out=c.nwt1[:], in0=c.wlnb[:], in1=c.nwt2[:],
                                op=OP.mult)
                g.tensor_scalar(out=c.wrs[:], in0=c.nwt1[:], scalar1=QW,
                                scalar2=None, op0=OP.mult)
                S["wdg"].inc(g.tensor_tensor(
                    out=c.wdq_t[:].rearrange("p (b k) -> p b k", k=32),
                    in0=scr.rearrange("p (b k) -> p b k", k=32),
                    in1=c.wrs[:].unsqueeze(2).broadcast_to([128, 32, 32]),
                    op=OP.mult))
                g.wait_ge(S["wdv"].h, 2 * ci + 2)
                if ci >= 1:
                    g.wait_ge(S["wdt"].h, ci)
                S["wdg"].inc(g.tensor_tensor(
                    out=c.wdq_n[:].rearrange("p (b k) -> p b k", k=32),
                    in0=c.wdq_q[:].rearrange("p (b k) -> p b k", k=32),
                    in1=c.wsc[:].unsqueeze(2).broadcast_to([128, 32, 32]),
                    op=OP.mult))
            else:
                seq += 1

    @block.scalar
    def _(sc_):
        for it in items:
            if it[0] != "deq":
                continue
            H, ch = it[1], it[2]
            ci = H * 12 + ch
            sc_.wait_ge(S["wdv"].h, 2 * ci + 1)
            if ci >= 1:
                sc_.wait_ge(S["wdg"].h, 2 * ci)
            sc_.activation(out=c.wlna[:], in_=c.wdmx[:], func=ACT.Ln)
            sc_.activation(out=c.wlnb[:], in_=c.wlna[:], func=ACT.Exp, scale=-1.0)
            sc_.mul(out=c.wrs[:], in_=c.wlnb[:], mul=QW)
            S["wds"].inc(sc_.mul(out=c.wsc[:], in_=c.wdmx[:], mul=1.0 / QW))

    @block.tensor
    def _(t):
        t.wait_ge(S["ip_e"].h, 136)
        seq = 0
        for it in items:
            if it[0] == "deq":
                H, ch = it[1], it[2]
                ci = H * 12 + ch
                t.wait_ge(S["wdg"].h, 2 * ci + 2)
                for k in range(NKD):
                    ins = t.transpose(c.pT[:, k * 128:(k + 1) * 128],
                                      c.wdq_n[:, k * 128:(k + 1) * 128],
                                      c.sid_bf[:])
                S["mm"].inc(ins)
            else:
                _, H, m, tt = it
                w, hf = HALVES[H]
                slab = c.wq_h[H % 2]
                t.wait_ge(S["wdt"].h, (H + 1) * 12)
                bi = seq % 4
                if seq >= 4:
                    t.wait_ge(S["fev"].h, seq - 3)
                for k in range(NKD):
                    ins = t.matmul(c.banks[bi][0:128, 0:512],
                                   slab[:, k, m * 128:(m + 1) * 128],
                                   c.xn_fm[:, k, tt * 512:(tt + 1) * 512],
                                   start=(k == 0), stop=(k == NKD - 1))
                S["mm"].inc(ins)
                seq += 1
            c.mm_n += 1

    @block.vector
    def _(v):
        v.wait_ge(S["fmx"].h, 192)
        v.tensor_copy(out=c.scr128[:, 0:1], in_=c.red[:, 2:3])
        S["ar2"].inc(v.tensor_copy(out=c.scr128[:, 1:2], in_=c.red[:, 3:4]))

    @block.tensor
    def _(t):
        t.wait_ge(S["ar2"].h, 1)
        S["mm"].inc(t.transpose(c.pU[:, 0:128], c.scr128[:, :], c.sid_f[:]))
        c.mm_n += 1

    mm_tp = c.mm_n

    @block.vector
    def _(v):
        v.wait_ge(S["mm"].h, mm_tp)
        v.tensor_copy(out=c.scr128[0:1, 126:127], in_=c.scr128[0:1, 125:126])
        v.tensor_copy(out=c.scr128[0:1, 126:127], in_=c.scr128[0:1, 125:126])
        S["ar2"].inc(v.tensor_reduce(out=c.red[0:2, 5:6], in_=c.pU[0:2, 0:128],
                                     axis=AX.X, op=OP.max))

    @block.gpsimd
    def _(g):
        g.wait_ge(S["ar2"].h, 2)
        S["sw"].inc(g.dma_start(out=c.cc_in[1][0:1, 0:2], in_=c.red[0:2, 5:6]), 16)
        S["sw"].inc(g.dma_start(out=c.cc_in[1][0:1, 2:8], in_=c.scal[0:1, 2:8]), 16)
        g.wait_ge(S["sw"].h, S["sw"].n)
        S["ar2"].inc_raw_cc(g.collective_compute(
            "AllReduce", OP.max, replica_groups=[list(range(8))],
            ins=[c.cc_in[1][:, :].opt()], outs=[c.cc_out[1][:, :].opt()]))
        g.wait_ge(S["ar2"].h, 3)
        S["sw"].inc(g.dma_start(out=c.scal[0:1, 12:14], in_=c.cc_out[1][0:1, 0:2]), 16)
        g.wait_ge(S["sw"].h, S["sw"].n)
        S["ar2"].inc(g.engine_nop())

    @block.vector
    def _(v):
        v.wait_ge(S["ar2"].h, 4)
        v.tensor_scalar(out=c.scal[0:1, 14:15], in0=c.scal[0:1, 12:13],
                        scalar1=1.0 / QA, scalar2=c.c_e12[0:1, 0:1],
                        op0=OP.mult, op1=OP.max)
        S["ar2"].inc(v.tensor_scalar(out=c.scal[0:1, 15:16], in0=c.scal[0:1, 13:14],
                                     scalar1=1.0 / QA, scalar2=c.c_e12[0:1, 0:1],
                                     op0=OP.mult, op1=OP.max))

    @block.scalar
    def _(s):
        s.wait_ge(S["ar2"].h, 5)
        s.activation(out=c.scal[0:1, 18:19], in_=c.scal[0:1, 14:15], func=ACT.Ln)
        s.activation(out=c.scal[0:1, 16:17], in_=c.scal[0:1, 18:19], func=ACT.Exp,
                     scale=-1.0)
        s.activation(out=c.scal[0:1, 18:19], in_=c.scal[0:1, 15:16], func=ACT.Ln)
        S["ar2"].inc(s.activation(out=c.scal[0:1, 17:18], in_=c.scal[0:1, 18:19],
                                  func=ACT.Exp, scale=-1.0))

    @block.sync
    def _(sy):
        sy.wait_ge(S["ar2"].h, 6)
        S["sws"].inc(sy.dma_start(out=c.sc_dram[0:1, 2:4], in_=c.scal[0:1, 16:18]), 16)
        S["sws"].inc(sy.dma_start(out=c.sc_dram[0:1, 4:5], in_=c.scal[0:1, 15:16]), 16)
        S["sws"].inc(sy.dma_start(out=c.sc_dram[0:1, 5:6], in_=c.scal[0:1, 14:15]), 16)
        sy.wait_ge(S["sws"].h, S["sws"].n)
        S["sc_bc"].inc(sy.dma_start(out=c.scbc[:, 2:3],
                                    in_=c.sc_dram[0:1, 2:3].broadcast_to([128, 1])), 16)
        S["sc_bc"].inc(sy.dma_start(out=c.scbc[:, 3:4],
                                    in_=c.sc_dram[0:1, 3:4].broadcast_to([128, 1])), 16)
        S["sc_bc"].inc(sy.dma_start(out=c.scbc[:, 4:5],
                                    in_=c.sc_dram[0:1, 4:5].broadcast_to([128, 1])), 16)


def _p7_ssdprep(c):
    S, block = c.S, c.block
    mm_b = c.mm_n
    c.bc_ready = {}
    c.cfm_ready = [0]

    @block.sync
    def _(sy):
        sy.wait_ge(S["fev"].h, 192)
        S["bc_d"].inc(sy.dma_start(out=c.cfm[0:64, :], in_=c.bcf[64:128, :]), 16)
        c.cfm_ready[0] = S["bc_d"].n
        for h in range(NH):
            S["bc_d"].inc(sy.dma_start(
                out=c.e2[:, h * NCH:(h + 1) * NCH],
                in_=c.eal_dram[h:h + 1, :].broadcast_to([64, NCH])), 16)
        for h in range(2):
            S["bc_d"].inc(sy.dma_start(
                out=c.acsbc[:, h, :],
                in_=c.acs_dram[h:h + 1, :].broadcast_to([128, T])), 16)
            S["bc_d"].inc(sy.dma_start(
                out=c.zbuf[:, h, :],
                in_=c.z_sp[h * 128:(h + 1) * 128, :]), 16)
            c.bc_ready[h] = S["bc_d"].n

    @block.tensor
    def _(t):
        for b in range(32):
            i, jj = divmod(b, 2)
            if b >= 1:
                t.wait_ge(S["tp_e"].h, NTP + b)
            for j in range(8):
                ins = t.transpose(c.pT[:, j * 128:(j + 1) * 128],
                                  c.xbc[:, jj * 8 + j, i * 128:(i + 1) * 128],
                                  c.sid_bf[:])
            S["mm"].inc(ins)
            c.mm_n += 1

    @block.vector
    def _(v):
        for b in range(32):
            i, jj = divmod(b, 2)
            v.wait_ge(S["mm"].h, mm_b + b + 1)
            v.tensor_copy(out=c.scr128[0:1, 126:127], in_=c.scr128[0:1, 125:126])
            v.tensor_copy(out=c.scr128[0:1, 126:127], in_=c.scr128[0:1, 125:126])
            S["tp_e"].inc(v.tensor_copy(
                out=c.xsc[:, i, jj * 1024:(jj + 1) * 1024]
                    .rearrange("p (a b) -> p a b", b=128),
                in_=c.pT[:].rearrange("p (a b) -> p a b", a=8)))
        for i in range(NTP):
            for h in range(NH):
                ins = v.tensor_scalar(out=c.xsc[:, i, h * 128:(h + 1) * 128],
                                      in0=c.xsc[:, i, h * 128:(h + 1) * 128],
                                      scalar1=c.dt_tm[:, i, h:h + 1],
                                      scalar2=None, op0=OP.mult)
        v.memset(c.s_st[:], 0.0)
        for h in range(NH):
            ins = v.tensor_scalar(out=c.d_i[:, h * 128:(h + 1) * 128],
                                  in0=c.sid_bf[:], scalar1=c.dcol[:, h:h + 1],
                                  scalar2=None, op0=OP.mult)
        S["prep"].inc(ins)

    mm_g = [0]

    @block.tensor
    def _(t):
        t.wait_ge(S["bc_d"].h, c.cfm_ready[0])
        t.wait_ge(S["tp_e"].h, NTP + 32)
        for ch in range(NCH):
            bank = c.pU if ch % 2 == 0 else c.pV
            if ch >= 2:
                t.wait_ge(S["gt_e"].h, 3 * (ch - 1))
            for sg in range(2):
                ins = t.matmul(bank[0:128, sg * 256:(sg + 1) * 256],
                               c.bcf[0:64, ch * 256 + sg * 128:ch * 256 + (sg + 1) * 128],
                               c.cfm[0:64, ch * 256:(ch + 1) * 256],
                               start=(sg == 0), stop=(sg == 1))
            S["mm"].inc(ins)
            c.mm_n += 1
    mm_g[0] = c.mm_n

    @block.vector
    def _(v):
        for ch in range(NCH):
            bank = c.pU if ch % 2 == 0 else c.pV
            v.wait_ge(S["mm"].h, mm_g[0] - NCH + ch + 1)
            v.tensor_copy(out=c.scr128[0:1, 126:127], in_=c.scr128[0:1, 125:126])
            v.tensor_copy(out=c.scr128[0:1, 126:127], in_=c.scr128[0:1, 125:126])
            S["gt_e"].inc(v.tensor_copy(
                out=c.gtm[:, ch, :, :].rearrange("p a b -> p (a b)"),
                in_=bank[0:128, 0:512]))

    @block.gpsimd
    def _(g):
        for ch in range(NCH):
            g.wait_ge(S["gt_e"].h, 3 * ch + 1)
            for sg in range(2):
                ins = g.affine_select(out=c.gtm[:, ch, sg, :],
                                      in_=c.gtm[:, ch, sg, :],
                                      pattern=[[1, 256]], compare_op=OP.is_ge,
                                      fill=0.0, base=-128 * sg,
                                      channel_multiplier=-1)
                S["gt_e"].inc(ins)


def _p8_ssd(c):
    S, block = c.S, c.block
    NIT = NH * NCH
    mm_b = c.mm_n

    @block.sync
    def _(sy):
        for h in range(2, NH):
            sy.wait_ge(S["ssd_p"].h, (h - 1) * NCH)
            sy.wait_ge(S["yev"].h, (h - 1) * NCH)
            S["bc_d"].inc(sy.dma_start(
                out=c.acsbc[:, h % 2, :],
                in_=c.acs_dram[h:h + 1, :].broadcast_to([128, T])), 16)
            S["bc_d"].inc(sy.dma_start(
                out=c.zbuf[:, h % 2, :],
                in_=c.z_sp[h * 128:(h + 1) * 128, :]), 16)
            c.bc_ready[h] = S["bc_d"].n

    @block.vector
    def _(v):
        v.wait_ge(S["prep"].h, S["prep"].n)
        v.wait_ge(S["gt_e"].h, 3 * NCH)
        for idx in range(NIT + 1):
            if idx >= 1:
                pidx = idx - 1
                h, ch = divmod(pidx, NCH)
                ybank = c.banks[pidx % 2]
                v.wait_ge(S["mm"].h, mm_b + 2 * pidx + 1)
                v.tensor_copy(out=c.scr128[0:1, 126:127], in_=c.scr128[0:1, 125:126])
                v.tensor_copy(out=c.scr128[0:1, 126:127], in_=c.scr128[0:1, 125:126])
                S["yev"].inc(v.tensor_tensor(
                    out=c.yb[:, h, ch * 256:(ch + 1) * 256],
                    in0=ybank[0:128, 0:256],
                    in1=c.zbuf[:, h % 2, ch * 256:(ch + 1) * 256], op=OP.mult))
                sbank = c.banks[2 + pidx % 2]
                v.wait_ge(S["mm"].h, mm_b + 2 * pidx + 2)
                v.tensor_copy(out=c.scr128[0:1, 126:127], in_=c.scr128[0:1, 125:126])
                v.tensor_copy(out=c.scr128[0:1, 126:127], in_=c.scr128[0:1, 125:126])
                S["sup"].inc(v.scalar_tensor_tensor(
                    out=c.s_st[:, h * HD:(h + 1) * HD],
                    in0=c.s_st[:, h * HD:(h + 1) * HD],
                    scalar=c.e2[:, h * NCH + ch:h * NCH + ch + 1],
                    in1=sbank[0:64, 0:128], op0=OP.mult, op1=OP.add))
            if idx < NIT:
                h, ch = divmod(idx, NCH)
                if ch == 0:
                    v.wait_ge(S["bc_d"].h, c.bc_ready[h])
                for sg in range(2):
                    ins = v.tensor_scalar(
                        out=c.dscr[:, sg, :],
                        in0=c.acsbc[:, h % 2, ch * 256:(ch + 1) * 256],
                        scalar1=c.acs_tm[:, ch * 2 + sg, h:h + 1],
                        scalar2=0.0, op0=OP.subtract, op1=OP.min)
                S["ssd_d"].inc(ins)
                v.wait_ge(S["ssd_e"].h, idx + 1)
                for sg in range(2):
                    v.tensor_tensor(out=c.wt[:, idx % 2, sg, :],
                                    in0=c.escr[:, sg, :],
                                    in1=c.gtm[:, ch, sg, :], op=OP.mult)
                v.tensor_tensor(out=c.csc[0:64, idx % 2, :],
                                in0=c.cfm[0:64, ch * 256:(ch + 1) * 256],
                                in1=c.e64f[0:64, :], op=OP.mult)
                for sg in range(2):
                    v.tensor_scalar(out=c.bsc[:, sg, :],
                                    in0=c.btm[:, ch * 2 + sg, :],
                                    scalar1=c.df_tm[:, ch * 2 + sg, h:h + 1],
                                    scalar2=None, op0=OP.mult)
                S["ssd_p"].inc(v.tensor_copy(out=c.sbf[:, :],
                                             in_=c.s_st[:, h * HD:(h + 1) * HD]))

    @block.scalar
    def _(sc_):
        for idx in range(NIT):
            h, ch = divmod(idx, NCH)
            sc_.wait_ge(S["ssd_d"].h, idx + 1)
            if idx >= 1:
                sc_.wait_ge(S["ssd_p"].h, idx)
            for sg in range(2):
                sc_.activation(out=c.escr[:, sg, :], in_=c.dscr[:, sg, :],
                               func=ACT.Exp)
            if ch == 0:
                sc_.wait_ge(S["bc_d"].h, c.bc_ready[h])
            S["ssd_e"].inc(sc_.activation(
                out=c.e64f[0:64, :], in_=c.acsbc[0:64, h % 2, ch * 256:(ch + 1) * 256],
                func=ACT.Exp))

    @block.tensor
    def _(t):
        for idx in range(NIT):
            h, ch = divmod(idx, NCH)
            t.wait_ge(S["ssd_p"].h, idx + 1)
            ybank = c.banks[idx % 2]
            if idx >= 2:
                t.wait_ge(S["yev"].h, idx - 1)
            xs = [c.xsc[:, ch * 2 + sg, h * 128:(h + 1) * 128] for sg in range(2)]
            t.matmul(ybank[0:128, 0:256], xs[0], c.wt[:, idx % 2, 0, :],
                     start=True, stop=False)
            t.matmul(ybank[0:128, 0:256], xs[1], c.wt[:, idx % 2, 1, :],
                     start=False, stop=False)
            t.matmul(ybank[0:128, 0:128], xs[0], c.d_i[:, h * 128:(h + 1) * 128],
                     start=False, stop=False)
            t.matmul(ybank[0:128, 128:256], xs[1], c.d_i[:, h * 128:(h + 1) * 128],
                     start=False, stop=False)
            ins = t.matmul(ybank[0:128, 0:256], c.sbf[:, :],
                           c.csc[0:64, idx % 2, :], start=False, stop=True)
            S["mm"].inc(ins)
            sbank = c.banks[2 + idx % 2]
            if idx >= 2:
                t.wait_ge(S["sup"].h, idx - 1)
            for sg in range(2):
                ins = t.matmul(sbank[0:64, 0:128], c.bsc[:, sg, :], xs[sg],
                               start=(sg == 0), stop=(sg == 1))
            S["mm"].inc(ins)
            c.mm_n += 2


def _p9_rsqrt_outproj(c):
    S, block = c.S, c.block
    NIT = NH * NCH

    @block.sync
    def _(sy):
        sy.wait_ge(S["yev"].h, NIT)
        sy.wait_ge(S["sup"].h, NIT)
        S["ld_wo"].inc(sy.dma_start(
            out=c.ns_col[:, :], in_=c.ns_hbm.rearrange("(a p) -> p a", p=128),
            allow_slow_non_contiguous=True), 16)
        for k in range(NKI):
            if k >= 2:
                sy.wait_ge(S["wos"].h, k - 1)
            S["ld_wo"].inc(sy.dma_start(
                out=c.wo_scr[:, k % 2, :],
                in_=c.w_out_hbm[k * 128:(k + 1) * 128, :]), 16)

    @block.vector
    def _(v):
        v.wait_ge(S["ld_wo"].h, 16)
        for k in range(NKI):
            v.wait_ge(S["ld_wo"].h, (k + 2) * 16)
            S["wos"].inc(v.tensor_scalar(out=c.w_out_bf[:, k, :],
                                         in0=c.wo_scr[:, k % 2, :],
                                         scalar1=c.ns_col[:, k:k + 1],
                                         scalar2=None, op0=OP.mult))

    mm_b = [c.mm_n]

    @block.scalar
    def _(sc_):
        sc_.wait_ge(S["yev"].h, NIT)
        for tt in range(NTT):
            for h in range(NH):
                i = tt * NH + h
                if i >= 2:
                    sc_.wait_ge(S["mm"].h, mm_b[0] + i - 1)
                S["ysqs"].inc(sc_.activation(
                    out=c.ysq[:, i % 2, :], in_=c.yb[:, h, tt * 512:(tt + 1) * 512],
                    func=ACT.Square))

    @block.tensor
    def _(t):
        assert c.mm_n == mm_b[0]
        for tt in range(NTT):
            if tt >= 1:
                t.wait_ge(S["rs_r"].h, tt)
            for h in range(NH):
                i = tt * NH + h
                t.wait_ge(S["ysqs"].h, i + 1)
                ins = t.matmul(c.pW[0:1, 0:512], c.onescol_bf[:, :],
                               c.ysq[:, i % 2, :], start=(h == 0), stop=(h == NH - 1))
                S["mm"].inc(ins)
                c.mm_n += 1

    @block.vector
    def _(v):
        for tt in range(NTT):
            v.wait_ge(S["mm"].h, mm_b[0] + (tt + 1) * NH)
            v.tensor_copy(out=c.scr128[0:1, 126:127], in_=c.scr128[0:1, 125:126])
            v.tensor_copy(out=c.scr128[0:1, 126:127], in_=c.scr128[0:1, 125:126])
            S["rs_r"].inc(v.tensor_copy(out=c.rrow[0:1, tt * 512:(tt + 1) * 512],
                                        in_=c.pW[0:1, 0:512]))
        S["rs_r"].inc(v.tensor_scalar(out=c.rrow2[0:1, :], in0=c.rrow[0:1, :],
                                      scalar1=1.0 / DI, scalar2=c.epsc[0:1, 0:1],
                                      op0=OP.mult, op1=OP.add))

    @block.scalar
    def _(sc_):
        sc_.wait_ge(S["rs_r"].h, 5)
        sc_.activation(out=c.rrow[0:1, :], in_=c.rrow2[0:1, :], func=ACT.Ln)
        S["rs_r"].inc(sc_.activation(out=c.rrow2[0:1, :], in_=c.rrow[0:1, :],
                                     func=ACT.Exp, scale=-0.5))

    @block.sync
    def _(sy):
        sy.wait_ge(S["rs_r"].h, 6)
        S["sws"].inc(sy.dma_start(out=c.row_dram[0:1, :], in_=c.rrow2[0:1, :]), 16)

    @block.gpsimd
    def _(g):
        g.wait_ge(S["sws"].h, S["sws"].n)
        S["rsqd"].inc(g.dma_start(out=c.rsq[:, :],
                                  in_=c.row_dram[0:1, :].broadcast_to([128, T])), 16)

    mm_o = [0]

    @block.tensor
    def _(t):
        mm_o[0] = c.mm_n
        t.wait_ge(S["wos"].h, NKI)
        seq = 0
        for j in range(NKD):
            for tt in range(NTT):
                if seq >= 4:
                    t.wait_ge(S["snv"].h, seq - 3)
                bi = seq % 4
                for k in range(NKI):
                    ins = t.matmul(c.banks[bi][0:128, 0:512],
                                   c.w_out_bf[:, k, j * 128:(j + 1) * 128],
                                   c.yb[:, k, tt * 512:(tt + 1) * 512],
                                   start=(k == 0), stop=(k == NKI - 1))
                S["mm"].inc(ins)
                c.mm_n += 1
                seq += 1

    @block.vector
    def _(v):
        v.wait_ge(S["rsqd"].h, 16)
        seq = 0
        for j in range(NKD):
            for tt in range(NTT):
                v.wait_ge(S["mm"].h, mm_o[0] + seq + 1)
                v.tensor_copy(out=c.scr128[0:1, 126:127], in_=c.scr128[0:1, 125:126])
                v.tensor_copy(out=c.scr128[0:1, 126:127], in_=c.scr128[0:1, 125:126])
                if seq >= 2:
                    v.wait_ge(S["spl2"].h, (192 + seq - 1) * 16)
                ins = v.tensor_tensor(out=c.snst[:, seq % 2, :],
                                      in0=c.banks[seq % 4][0:128, 0:512],
                                      in1=c.rsq[:, tt * 512:(tt + 1) * 512],
                                      op=OP.mult)
                S["snv"].inc(ins)
                seq += 1

    @block.sync
    def _(sy):
        seq = 0
        for j in range(NKD):
            for tt in range(NTT):
                sy.wait_ge(S["snv"].h, seq + 1)
                S["spl2"].inc(sy.dma_start(
                    out=c.sn_sp[j * 128:(j + 1) * 128, tt * 512:(tt + 1) * 512],
                    in_=c.snst[:, seq % 2, :]), 16)
                seq += 1


def _p10_requant(c):
    S, block = c.S, c.block
    sc2 = c.wdq_scr2

    @block.sync
    def _(sy):
        for i in range(96):
            ic, tt = divmod(i, NTT)
            if i >= 2:
                sy.wait_ge(S["rq"].h, i - 1)
            S["ld_ffw"].inc(sy.dma_start(
                out=(c.ffn_a if i % 2 == 0 else c.ffn_b)[:, :],
                in_=c.gate_sp[ic * 128:(ic + 1) * 128, tt * 512:(tt + 1) * 512]), 16)

    ldb = S["ld_ffw"].n - 96 * 16

    @block.vector
    def _(v):
        v.wait_ge(S["sc_bc"].h, 80)
        for _d in range(24):
            v.memset(c.gug[:, 0:4, :], 0.0)
        for i in range(96):
            ic, tt = divmod(i, NTT)
            gqs = c.gq[:, ic, tt * 512:(tt + 1) * 512]
            v.wait_ge(S["ld_ffw"].h, ldb + (i + 1) * 16)
            slot = (c.ffn_a if i % 2 == 0 else c.ffn_b)[:, :]
            v.tensor_scalar(out=sc2[:, 0:512], in0=slot, scalar1=c.scbc[:, 3:4],
                            scalar2=MAGIC, op0=OP.mult, op1=OP.add)
            S["rq_d"].inc(v.tensor_scalar(out=gqs, in0=sc2[:, 0:512],
                                          scalar1=MAGIC, scalar2=None,
                                          op0=OP.subtract))
            v.wait_ge(S["rq_s"].h, i + 1)
            v.tensor_scalar(out=c.ffn_p[:, :], in0=sc2[:, 512:1024],
                            scalar1=65536.0, scalar2=MAGIC, op0=OP.mult, op1=OP.add)
            v.tensor_scalar(out=c.ffn_c[:, :], in0=c.ffn_p[:, :], scalar1=MAGIC,
                            scalar2=65535.0, op0=OP.subtract, op1=OP.min)
            v.tensor_tensor(out=c.ffn_p[:, :], in0=gqs, in1=c.ffn_c[:, :],
                            op=OP.mult)
            v.tensor_reduce(out=c.red[:, 6:7], in_=c.ffn_p[:, :], axis=AX.X,
                            op=OP.max, apply_absolute_value=True)
            if i == 0:
                ins = v.tensor_copy(out=c.red[:, 5:6], in_=c.red[:, 6:7])
            else:
                ins = v.tensor_tensor(out=c.red[:, 5:6], in0=c.red[:, 5:6],
                                      in1=c.red[:, 6:7], op=OP.max)
            S["rq"].inc(ins)

    @block.scalar
    def _(sc_):
        for _d in range(8):
            sc_.copy(out=c.scr128[0:1, 124:125], in_=c.scr128[0:1, 123:124])
        for i in range(96):
            ic, tt = divmod(i, NTT)
            sc_.wait_ge(S["rq_d"].h, i + 1)
            if i >= 1:
                sc_.wait_ge(S["rq"].h, i)
            S["rq_s"].inc(sc_.activation(
                out=sc2[:, 512:1024], in_=c.gq[:, ic, tt * 512:(tt + 1) * 512],
                func=ACT.Sigmoid, scale=c.scbc[:, 4:5]))

    @block.vector
    def _(v):
        v.wait_ge(S["rq"].h, 96)
        S["ar4s"].inc(v.tensor_copy(out=c.scr128[:, 0:1], in_=c.red[:, 5:6]))

    @block.tensor
    def _(t):
        t.wait_ge(S["ar4s"].h, 1)
        S["mm"].inc(t.transpose(c.pU[:, 0:128], c.scr128[:, :], c.sid_f[:]))
        c.mm_n += 1
    mm_t4 = c.mm_n

    @block.vector
    def _(v):
        v.wait_ge(S["mm"].h, mm_t4)
        v.tensor_copy(out=c.scr128[0:1, 126:127], in_=c.scr128[0:1, 125:126])
        v.tensor_copy(out=c.scr128[0:1, 126:127], in_=c.scr128[0:1, 125:126])
        S["ar4s"].inc(v.tensor_reduce(out=c.red[0:1, 6:7], in_=c.pU[0:1, 0:128],
                                      axis=AX.X, op=OP.max))

    @block.gpsimd
    def _(g):
        g.wait_ge(S["ar4s"].h, 2)
        S["sw"].inc(g.dma_start(out=c.cc_in[2][0:1, 0:1], in_=c.red[0:1, 6:7]), 16)
        S["sw"].inc(g.dma_start(out=c.cc_in[2][0:1, 1:8], in_=c.scal[0:1, 1:8]), 16)
        g.wait_ge(S["sw"].h, S["sw"].n)
        S["ar4s"].inc_raw_cc(g.collective_compute(
            "AllReduce", OP.max, replica_groups=[list(range(8))],
            ins=[c.cc_in[2][:, :].opt()], outs=[c.cc_out[2][:, :].opt()]))
        g.wait_ge(S["ar4s"].h, 3)
        S["sw"].inc(g.dma_start(out=c.scal[0:1, 19:20], in_=c.cc_out[2][0:1, 0:1]), 16)
        g.wait_ge(S["sw"].h, S["sw"].n)
        S["ar4s"].inc(g.engine_nop())

    @block.scalar
    def _(sc_):
        sc_.wait_ge(S["ar4s"].h, 4)
        sc_.activation(out=c.scal[0:1, 23:24], in_=c.scal[0:1, 19:20], func=ACT.Ln)
        sc_.activation(out=c.scal[0:1, 24:25], in_=c.scal[0:1, 23:24], func=ACT.Exp,
                       scale=-1.0)
        sc_.mul(out=c.scal[0:1, 20:21], in_=c.scal[0:1, 24:25], mul=QA)
        sc_.mul(out=c.scal[0:1, 25:26], in_=c.scal[0:1, 19:20],
                mul=1.0 / (65536.0 * QA))
        S["ar4s"].inc(sc_.mul(out=c.scal[0:1, 21:22], in_=c.scal[0:1, 25:26],
                              mul=c.scal[0:1, 15:16]))

    @block.sync
    def _(sy):
        sy.wait_ge(S["ar4s"].h, 5)
        S["sws"].inc(sy.dma_start(out=c.sc_dram[0:1, 6:7], in_=c.scal[0:1, 20:21]), 16)
        sy.wait_ge(S["sws"].h, S["sws"].n)
        S["sc_bc"].inc(sy.dma_start(out=c.scbc[:, 5:6],
                                    in_=c.sc_dram[0:1, 6:7].broadcast_to([128, 1])), 16)


def _p11_ophase(c):
    S, block = c.S, c.block
    sc2 = c.wdq_scr2
    wdv0, wds0, wdg0, wdt0 = 96, 48, 96, 48
    p11_mm0 = c.mm_n
    ldb2 = [0]

    @block.sync
    def _(sy):
        ldb2[0] = S["ld_ffw"].n
        sy.wait_ge(S["rq"].h, 96)
        for ci in range(48):
            oc, c4 = divmod(ci, 6)
            if ci >= 2:
                sy.wait_ge(S["wdv"].h, wdv0 + 2 * (ci - 2) + 2)
            S["ld_ffw"].inc(sy.dma_start(
                out=sc2[:, (ci % 2) * 512:(ci % 2) * 512 + 512],
                in_=c.w_down_hbm[oc * 128:(oc + 1) * 128,
                                 c4 * 512:(c4 + 1) * 512]), 16)

    @block.vector
    def _(v):
        v.wait_ge(S["rq"].h, 96)
        for ci in range(48):
            slot = sc2[:, (ci % 2) * 512:(ci % 2) * 512 + 512]
            v.wait_ge(S["ld_ffw"].h, ldb2[0] + (ci + 1) * 16)
            if ci >= 1:
                v.wait_ge(S["wdg"].h, wdg0 + 2 * ci)
                v.wait_ge(S["wds"].h, wds0 + ci)
            v.tensor_reduce(out=c.wbm[:, 0:16].unsqueeze(2),
                            in_=slot.rearrange("p (b k) -> p b k", k=32),
                            axis=AX.X, op=OP.max, apply_absolute_value=True)
            S["wdv"].inc(v.tensor_scalar(out=c.wdmx[:, 0:16], in0=c.wbm[:, 0:16],
                                         scalar1=c.c_e7[:, 0:1], scalar2=None,
                                         op0=OP.max))
            v.wait_ge(S["wdg"].h, wdg0 + 2 * ci + 1)
            v.tensor_scalar(out=c.ffn_p[:, :], in0=c.ffn_p[:, :], scalar1=MAGIC,
                            scalar2=None, op0=OP.add)
            S["wdv"].inc(v.tensor_scalar(out=c.wdq_q2[:, 0:512], in0=c.ffn_p[:, :],
                                         scalar1=MAGIC, scalar2=None,
                                         op0=OP.subtract))
            oc, c4 = divmod(ci, 6)
            v.wait_ge(S["mm"].h, p11_mm0 + ci + 1)
            v.tensor_copy(out=c.scr128[0:1, 126:127], in_=c.scr128[0:1, 125:126])
            v.tensor_copy(out=c.scr128[0:1, 126:127], in_=c.scr128[0:1, 125:126])
            S["wdt"].inc(v.tensor_copy(
                out=c.wq_d[:, c4 * 4:(c4 + 1) * 4, oc * 128:(oc + 1) * 128],
                in_=c.pT[:, 0:512].rearrange("p (a b) -> p a b", a=4)))

    @block.scalar
    def _(sc_):
        for ci in range(48):
            sc_.wait_ge(S["wdv"].h, wdv0 + 2 * ci + 1)
            if ci >= 1:
                sc_.wait_ge(S["wdg"].h, wdg0 + 2 * ci)
            sc_.activation(out=c.wlna[:, 0:16], in_=c.wdmx[:, 0:16], func=ACT.Ln)
            sc_.activation(out=c.wlnb[:, 0:16], in_=c.wlna[:, 0:16], func=ACT.Exp,
                           scale=-1.0)
            sc_.mul(out=c.wrs[:, 0:16], in_=c.wlnb[:, 0:16], mul=QW)
            S["wds"].inc(sc_.mul(out=c.wsc[:, 0:16], in_=c.wdmx[:, 0:16],
                                 mul=1.0 / QW))

    @block.gpsimd
    def _(g):
        for ci in range(48):
            slot = sc2[:, (ci % 2) * 512:(ci % 2) * 512 + 512]
            g.wait_ge(S["wds"].h, wds0 + ci + 1)
            g.tensor_tensor(out=c.nwt1[:, 0:16], in0=c.wdmx[:, 0:16],
                            in1=c.wlnb[:, 0:16], op=OP.mult)
            g.tensor_scalar(out=c.nwt2[:, 0:16], in0=c.nwt1[:, 0:16],
                            scalar1=-1.0, scalar2=2.0, op0=OP.mult, op1=OP.add)
            g.tensor_tensor(out=c.nwt1[:, 0:16], in0=c.wlnb[:, 0:16],
                            in1=c.nwt2[:, 0:16], op=OP.mult)
            g.tensor_scalar(out=c.wrs[:, 0:16], in0=c.nwt1[:, 0:16],
                            scalar1=QW, scalar2=None, op0=OP.mult)
            S["wdg"].inc(g.tensor_tensor(
                out=c.ffn_p[:].rearrange("p (b k) -> p b k", k=32),
                in0=slot.rearrange("p (b k) -> p b k", k=32),
                in1=c.wrs[:, 0:16].unsqueeze(2).broadcast_to([128, 16, 32]),
                op=OP.mult))
            g.wait_ge(S["wdv"].h, wdv0 + 2 * ci + 2)
            if ci >= 1:
                g.wait_ge(S["wdt"].h, wdt0 + ci)
            S["wdg"].inc(g.tensor_tensor(
                out=c.wdq_n2[:, 0:512].rearrange("p (b k) -> p b k", k=32),
                in0=c.wdq_q2[:, 0:512].rearrange("p (b k) -> p b k", k=32),
                in1=c.wsc[:, 0:16].unsqueeze(2).broadcast_to([128, 16, 32]),
                op=OP.mult))

    @block.tensor
    def _(t):
        assert c.mm_n == p11_mm0
        for ci in range(48):
            t.wait_ge(S["wdg"].h, wdg0 + 2 * ci + 2)
            if ci >= 1:
                t.wait_ge(S["wdt"].h, wdt0 + ci)
            for k in range(4):
                ins = t.transpose(c.pT[:, k * 128:(k + 1) * 128],
                                  c.wdq_n2[:, k * 128:(k + 1) * 128],
                                  c.sid_bf[:])
            S["mm"].inc(ins)
            c.mm_n += 1

    ldb3 = [0]

    @block.sync
    def _(sy):
        ldb3[0] = S["ld_ffw"].n
        for i in range(96):
            ic, tt = divmod(i, NTT)
            if i >= 2:
                sy.wait_ge(S["oq"].h, i - 1)
            S["ld_ffw"].inc(sy.dma_start(
                out=(c.ffn_a if i % 2 == 0 else c.ffn_b)[:, :],
                in_=c.up_sp[ic * 128:(ic + 1) * 128, tt * 512:(tt + 1) * 512]), 16)

    @block.vector
    def _(v):
        v.wait_ge(S["wdt"].h, wdt0 + 48)
        v.wait_ge(S["sc_bc"].h, 96)
        for _d in range(24):
            v.memset(c.gug[:, 0:4, :], 0.0)
        for i in range(96):
            ic, tt = divmod(i, NTT)
            gqs = c.gq[:, ic, tt * 512:(tt + 1) * 512]
            v.wait_ge(S["rq_s"].h, 96 + i + 1)
            v.tensor_scalar(out=c.ffn_p[:, :], in0=sc2[:, 512:1024],
                            scalar1=65536.0, scalar2=MAGIC, op0=OP.mult, op1=OP.add)
            v.tensor_scalar(out=c.ffn_c[:, :], in0=c.ffn_p[:, :], scalar1=MAGIC,
                            scalar2=65535.0, op0=OP.subtract, op1=OP.min)
            v.tensor_tensor(out=c.ffn_p[:, :], in0=gqs, in1=c.ffn_c[:, :], op=OP.mult)
            v.tensor_scalar(out=sc2[:, 0:512], in0=c.ffn_p[:, :],
                            scalar1=c.scbc[:, 5:6], scalar2=MAGIC,
                            op0=OP.mult, op1=OP.add)
            S["rq_d"].inc(v.tensor_scalar(out=gqs, in0=sc2[:, 0:512],
                                          scalar1=MAGIC, scalar2=None,
                                          op0=OP.subtract))
            v.wait_ge(S["ld_ffw"].h, ldb3[0] + (i + 1) * 16)
            slot = (c.ffn_a if i % 2 == 0 else c.ffn_b)[:, :]
            v.tensor_scalar(out=sc2[:, 0:512], in0=slot, scalar1=c.scbc[:, 2:3],
                            scalar2=MAGIC, op0=OP.mult, op1=OP.add)
            v.tensor_scalar(out=c.ffn_d[:, :], in0=sc2[:, 0:512], scalar1=MAGIC,
                            scalar2=None, op0=OP.subtract)
            v.tensor_tensor(out=sc2[:, 0:512], in0=gqs, in1=c.ffn_d[:, :],
                            op=OP.mult)
            S["oq"].inc(v.tensor_reduce(
                out=c.scr128[:, i + 1:i + 2], in_=sc2[:, 0:512], axis=AX.X,
                op=OP.max, apply_absolute_value=True))

    @block.scalar
    def _(sc_):
        sc_.wait_ge(S["wdg"].h, 192)
        for _d in range(8):
            sc_.copy(out=c.scr128[0:1, 124:125], in_=c.scr128[0:1, 123:124])
        for i in range(96):
            ic, tt = divmod(i, NTT)
            if i >= 1:
                sc_.wait_ge(S["rq_d"].h, 96 + i)
            S["rq_s"].inc(sc_.activation(
                out=sc2[:, 512:1024], in_=c.gq[:, ic, tt * 512:(tt + 1) * 512],
                func=ACT.Sigmoid, scale=c.scbc[:, 4:5]))

    @block.vector
    def _(v):
        v.wait_ge(S["oq"].h, 96)
        S["ar5s"].inc(v.tensor_reduce(out=c.scr128[:, 0:1],
                                      in_=c.scr128[:, 1:97], axis=AX.X,
                                      op=OP.max))

    @block.tensor
    def _(t):
        t.wait_ge(S["ar5s"].h, 1)
        S["mm"].inc(t.transpose(c.pU[:, 0:128], c.scr128[:, :], c.sid_f[:]))
        c.mm_n += 1
    mm_t5 = c.mm_n

    @block.vector
    def _(v):
        v.wait_ge(S["mm"].h, mm_t5)
        v.tensor_copy(out=c.scr128[0:1, 126:127], in_=c.scr128[0:1, 125:126])
        v.tensor_copy(out=c.scr128[0:1, 126:127], in_=c.scr128[0:1, 125:126])
        S["ar5s"].inc(v.tensor_reduce(out=c.red[0:1, 6:7], in_=c.pU[0:1, 0:128],
                                      axis=AX.X, op=OP.max))

    @block.gpsimd
    def _(g):
        g.wait_ge(S["ar5s"].h, 2)
        S["sw"].inc(g.dma_start(out=c.cc_in[3][0:1, 0:1], in_=c.red[0:1, 6:7]), 16)
        S["sw"].inc(g.dma_start(out=c.cc_in[3][0:1, 1:8], in_=c.scal[0:1, 1:8]), 16)
        g.wait_ge(S["sw"].h, S["sw"].n)
        S["ar5s"].inc_raw_cc(g.collective_compute(
            "AllReduce", OP.max, replica_groups=[list(range(8))],
            ins=[c.cc_in[3][:, :].opt()], outs=[c.cc_out[3][:, :].opt()]))
        g.wait_ge(S["ar5s"].h, 3)
        S["sw"].inc(g.dma_start(out=c.scal[0:1, 22:23], in_=c.cc_out[3][0:1, 0:1]), 16)
        g.wait_ge(S["sw"].h, S["sw"].n)
        S["ar5s"].inc(g.engine_nop())

    @block.scalar
    def _(sc_):
        sc_.wait_ge(S["ar5s"].h, 4)
        sc_.activation(out=c.scal[0:1, 23:24], in_=c.scal[0:1, 22:23], func=ACT.Ln)
        sc_.activation(out=c.scal[0:1, 24:25], in_=c.scal[0:1, 23:24], func=ACT.Exp,
                       scale=-1.0)
        sc_.mul(out=c.scal[0:1, 26:27], in_=c.scal[0:1, 24:25], mul=QA)
        sc_.mul(out=c.scal[0:1, 27:28], in_=c.scal[0:1, 22:23], mul=1.0 / QA)
        sc_.mul(out=c.scal[0:1, 28:29], in_=c.scal[0:1, 27:28],
                mul=c.scal[0:1, 21:22])
        S["ar5s"].inc(sc_.mul(out=c.scal[0:1, 29:30], in_=c.scal[0:1, 28:29],
                              mul=c.scal[0:1, 14:15]))

    @block.sync
    def _(sy):
        sy.wait_ge(S["ar5s"].h, 5)
        S["sws"].inc(sy.dma_start(out=c.sc_dram[0:1, 7:8], in_=c.scal[0:1, 26:27]), 16)
        S["sws"].inc(sy.dma_start(out=c.sc_dram[0:1, 8:9], in_=c.scal[0:1, 29:30]), 16)
        sy.wait_ge(S["sws"].h, S["sws"].n)
        S["sc_bc"].inc(sy.dma_start(out=c.scbc[:, 6:7],
                                    in_=c.sc_dram[0:1, 7:8].broadcast_to([128, 1])), 16)
        S["sc_bc"].inc(sy.dma_start(out=c.scbc[:, 7:8],
                                    in_=c.sc_dram[0:1, 8:9].broadcast_to([128, 1])), 16)


def _p12_final(c):
    S, block = c.S, c.block
    sc2 = c.wdq_scr2
    ldb4 = [0]
    mm12 = c.mm_n
    c.spl2_p12 = c.S["spl2"].n

    @block.sync
    def _(sy):
        ldb4[0] = S["ld_ffw"].n
        sy.wait_ge(S["oq"].h, 96)
        for ht in range(8):
            for ic in range(NCF):
                i = ht * NCF + ic
                if i >= 2:
                    sy.wait_ge(S["oq2"].h, i - 1)
                S["ld_ffw"].inc(sy.dma_start(
                    out=(c.ffn_a if i % 2 == 0 else c.ffn_b)[:, 0:256],
                    in_=c.up_sp[ic * 128:(ic + 1) * 128,
                                ht * 256:(ht + 1) * 256]), 16)
            for j in range(NKD):
                sj = ht * NKD + j
                if sj >= 2:
                    sy.wait_ge(S["fsg"].h, sj - 1)
                S["spl2"].inc(sy.dma_start(
                    out=c.snbuf[:, j % 2, :],
                    in_=c.sn_sp[j * 128:(j + 1) * 128, ht * 256:(ht + 1) * 256]), 16)
            for ts in range(2):
                g_ts = ht * 2 + ts
                if g_ts >= 1:
                    sy.wait_ge(S["fx"].h, g_ts)
                S["ld_x"].inc(sy.dma_start(
                    out=c.fin_x[:, :],
                    in_=c.x_hbm[g_ts * 128:(g_ts + 1) * 128, :]), 16)
                sy.wait_ge(S["fx"].h, g_ts + 1)
                S["fout"].inc(sy.dma_start(
                    out=c.out_hbm[g_ts * 128:(g_ts + 1) * 128, :],
                    in_=c.fin_o[:, :]), 16)

    @block.vector
    def _(v):
        v.wait_ge(S["sc_bc"].h, 128)
        for _d in range(24):
            v.memset(c.gug[:, 0:4, :], 0.0)
        snb = 0
        for ht in range(8):
            if ht >= 1:
                v.wait_ge(S["mm"].h, mm12 + (ht - 1) * 16 + 8)
            for ic in range(NCF):
                i = ht * NCF + ic
                gqs = c.gq[:, ic, ht * 256:(ht + 1) * 256]
                v.wait_ge(S["ld_ffw"].h, ldb4[0] + (i + 1) * 16)
                slot = (c.ffn_a if i % 2 == 0 else c.ffn_b)[:, 0:256]
                v.tensor_scalar(out=sc2[:, 0:256], in0=slot, scalar1=c.scbc[:, 2:3],
                                scalar2=MAGIC, op0=OP.mult, op1=OP.add)
                v.tensor_scalar(out=c.ffn_d[:, 0:256], in0=sc2[:, 0:256],
                                scalar1=MAGIC, scalar2=None, op0=OP.subtract)
                v.tensor_tensor(out=c.ffn_p[:, 0:256], in0=gqs,
                                in1=c.ffn_d[:, 0:256], op=OP.mult)
                v.tensor_scalar(out=sc2[:, 256:512], in0=c.ffn_p[:, 0:256],
                                scalar1=c.scbc[:, 6:7], scalar2=MAGIC,
                                op0=OP.mult, op1=OP.add)
                v.tensor_scalar(out=c.ffn_c[:, 0:256], in0=sc2[:, 256:512],
                                scalar1=MAGIC, scalar2=None, op0=OP.subtract)
                S["oq2"].inc(v.tensor_scalar(out=c.ofm[:, ic, :],
                                             in0=c.ffn_c[:, 0:256],
                                             scalar1=c.scbc[:, 7:8],
                                             scalar2=None, op0=OP.mult))
            for j in range(NKD):
                sj = ht * NKD + j
                v.wait_ge(S["fg"].h, sj + 1)
                v.wait_ge(S["spl2"].h, c.spl2_p12 + (sj + 1) * 16)
                if sj >= 1:
                    prev_pair = (ht * 16 + 8 + j) if j >= 1 else ((ht - 1) * 16 + 16)
                    v.wait_ge(S["mm"].h, mm12 + prev_pair)
                S["fsg"].inc(v.tensor_tensor(
                    out=c.fin_a[:, 0:256],
                    in0=c.snbuf[:, j % 2, :], in1=c.gug[:, j, :],
                    op=OP.mult))
            for ts in range(2):
                g_ts = ht * 2 + ts
                v.wait_ge(S["mm"].h, mm12 + ht * 16 + 16)
                if g_ts >= 1:
                    v.wait_ge(S["fout"].h, g_ts * 16)
                bA = (c.pU, c.pV) if ts == 0 else (c.pW, c.pD)
                v.wait_ge(S["ld_x"].h, 256 + (g_ts + 1) * 16)
                v.tensor_copy(out=c.scr128[0:1, 126:127], in_=c.scr128[0:1, 125:126])
                v.tensor_copy(out=c.scr128[0:1, 126:127], in_=c.scr128[0:1, 125:126])
                v.tensor_tensor(out=c.fin_o[:, 0:512], in0=bA[0][0:128, 0:512],
                                in1=c.fin_x[:, 0:512], op=OP.add)
                S["fx"].inc(v.tensor_tensor(out=c.fin_o[:, 512:1024],
                                            in0=bA[1][0:128, 0:512],
                                            in1=c.fin_x[:, 512:1024],
                                            op=OP.add))

    @block.tensor
    def _(t):
        assert c.mm_n == mm12
        t.wait_ge(S["wdt"].h, 96)
        for ht in range(8):
            for j in range(NKD):
                sj = ht * NKD + j
                t.wait_ge(S["oq2"].h, (ht + 1) * NCF)
                if sj >= 2:
                    t.wait_ge(S["fg"].h, sj - 1)
                bi = sj % 2
                for k in range(NCF):
                    ins = t.matmul(c.banks[bi][0:128, 0:256],
                                   c.wq_d[:, k, j * 128:(j + 1) * 128],
                                   c.ofm[:, k, :], start=(k == 0),
                                   stop=(k == NCF - 1))
                S["mm"].inc(ins)
                c.mm_n += 1
            for j in range(NKD):
                sj = ht * NKD + j
                t.wait_ge(S["fsg"].h, sj + 1)
                if ht >= 1:
                    t.wait_ge(S["fx"].h, 2 * ht)
                dst0 = c.pU if j < 4 else c.pV
                dst1 = c.pW if j < 4 else c.pD
                jc = (j % 4) * 128
                t.transpose(dst0[0:128, jc:jc + 128], c.fin_a[:, 0:128], c.sid_f[:])
                ins = t.transpose(dst1[0:128, jc:jc + 128], c.fin_a[:, 128:256],
                                  c.sid_f[:])
                S["mm"].inc(ins)
                c.mm_n += 1

    @block.scalar
    def _(sc_):
        for ht in range(8):
            for j in range(NKD):
                sj = ht * NKD + j
                sc_.wait_ge(S["mm"].h, mm12 + ht * 16 + j + 1)
                if ht >= 1:
                    sc_.wait_ge(S["fsg"].h, (ht - 1) * NKD + j + 1)
                sc_.copy(out=c.scr128[0:1, 124:125], in_=c.scr128[0:1, 123:124])
                sc_.copy(out=c.scr128[0:1, 124:125], in_=c.scr128[0:1, 123:124])
                S["fg"].inc(sc_.activation(out=c.gug[:, j, :],
                                           in_=c.banks[sj % 2][0:128, 0:256],
                                           func=ACT.Silu))


def _taps(c):
    if c.stage_limit >= 99:
        return
    S, block = c.S, c.block

    @block.gpsimd
    def _(g):
        if c.stage_limit >= 7:
            g.wait_ge(S["fout"].h, S["fout"].n)
            S["dbgs"].inc(g.dma_start(out=c.dbg["scal2"][:, :], in_=c.scal[0:1, :]), 16)
            for k in range(NCF):
                S["dbgs"].inc(g.dma_start(
                    out=c.dbg["wqd"][k * 128:(k + 1) * 128, :],
                    in_=c.wq_d[:, k, :]), 16)
            for ic in range(NCF):
                S["dbgs"].inc(g.dma_start(
                    out=c.dbg["g2d"][ic * 128:(ic + 1) * 128, :],
                    in_=c.gq[:, ic, :]), 16)
        g.wait_ge(S["tp_e"].h, NTP)
        g.wait_ge(S["sc_bc"].h, 32)
        if c.stage_limit == 1:
            for i in range(NTP):
                S["dbgs"].inc(g.dma_start(out=c.dbg["xn_tm"][i * 128:(i + 1) * 128, :],
                                          in_=c.xn_tm[:, i, :]), 16)
        S["dbgs"].inc(g.dma_start(out=c.dbg["gmax"][:, :], in_=c.scal[0:1, 0:16]), 16)
        S["dbgs"].inc(g.dma_start(out=c.dbg["lnst"][:, :], in_=c.ln_st[:, :]), 16)
        S["dbgs"].inc(g.dma_start(out=c.dbg["red"][:, :], in_=c.red[:, :]), 16)
        if c.stage_limit >= 7:
            S["dbgs"].inc(g.dma_start(out=c.dbg["s128"][:, :], in_=c.scr128[:, :]), 16)
        if c.stage_limit >= 2:
            g.wait_ge(S["ip_e"].h, len(c.ip_groups))
            g.wait_ge(S["spl"].h, S["spl"].n)
            if c.stage_limit == 2:
                for i in range(16):
                    S["dbgs"].inc(g.dma_start(
                        out=c.dbg["xbc"][i * 128:(i + 1) * 128, :],
                        in_=c.xbc[:, i, :]), 16)
                S["dbgs"].inc(g.dma_start(
                    out=c.dbg["xbc"][16 * 128:17 * 128, :], in_=c.bc_tmp[:, :]), 16)
            S["dbgs"].inc(g.dma_start(out=c.dbg["z"][:, :], in_=c.z_sp[:, :]), 16)
            S["dbgs"].inc(g.dma_start(out=c.dbg["dtr"][:, :], in_=c.dt_dram[:, :]), 16)
        if c.stage_limit >= 5:
            g.wait_ge(S["yev"].h, NH * NCH)
            g.wait_ge(S["sup"].h, NH * NCH)
            for h in range(NH):
                S["dbgs"].inc(g.dma_start(
                    out=c.dbg["y"][h * 128:(h + 1) * 128, :], in_=c.yb[:, h, :]), 16)
        if c.stage_limit in (5, 6):
            for i in range(NTP):
                S["dbgs"].inc(g.dma_start(
                    out=c.dbg["xsc2"][i * 128:(i + 1) * 128, :], in_=c.xsc[:, i, :]), 16)
        if c.stage_limit == 4:
            g.wait_ge(S["spl2"].h, 192 * 16)
            g.wait_ge(S["wdt"].h, 48)
            for k in range(NKD):
                S["dbgs"].inc(g.dma_start(
                    out=c.dbg["xqv"][k * 128:(k + 1) * 128, :],
                    in_=c.xn_fm[:, k, :]), 16)
                S["dbgs"].inc(g.dma_start(
                    out=c.dbg["wq0"][k * 128:(k + 1) * 128, :],
                    in_=c.wq_h[1][:, k, :]), 16)
        elif c.stage_limit >= 5:
            g.wait_ge(S["spl2"].h, 192 * 16)
        if c.stage_limit >= 6:
            g.wait_ge(S["spl2"].h, 224 * 16)
            S["dbgs"].inc(g.dma_start(out=c.dbg["sn"][:, :], in_=c.sn_sp[:, :]), 16)
            S["dbgs"].inc(g.dma_start(out=c.dbg["rsq"][:, :], in_=c.rsq[:, :]), 16)
            S["dbgs"].inc(g.dma_start(out=c.dbg["rrow"][:, :], in_=c.row_dram[0:1, :]), 16)
            g.wait_ge(S["ar2"].h, 6)
            g.wait_ge(S["sc_bc"].h, S["sc_bc"].n)
            S["dbgs"].inc(g.dma_start(out=c.dbg["upt"][:, :], in_=c.up_sp[:, :]), 16)
            S["dbgs"].inc(g.dma_start(out=c.dbg["gt"][:, :], in_=c.gate_sp[:, :]), 16)
        if c.stage_limit == 3:
            g.wait_ge(S["cvs"].h, 17)
            g.wait_ge(S["prep"].h, 1)
            g.wait_ge(S["dts"].h, 4)
            for i in range(16):
                S["dbgs"].inc(g.dma_start(
                    out=c.dbg["conv"][i * 128:(i + 1) * 128, :],
                    in_=c.xbc[:, i, :]), 16)
            S["dbgs"].inc(g.dma_start(
                out=c.dbg["conv"][16 * 128:17 * 128, :], in_=c.bcf[:, :]), 16)
            S["dbgs"].inc(g.dma_start(out=c.dbg["acs"][:, :], in_=c.acs_fm[:, :]), 16)
            S["dbgs"].inc(g.dma_start(out=c.dbg["dtv"][:, :], in_=c.dt_fm[:, :]), 16)
        g.wait_ge(S["dbgs"].h, S["dbgs"].n)


def _finish(c):
    S, block = c.S, c.block

    @block.sync
    def _(sy):
        if S["fout"].n:
            sy.wait_ge(S["fout"].h, S["fout"].n)


def make_in_maps(inputs):
    u = np.ascontiguousarray(inputs["L0_attn_u"], dtype=np.float32)
    conv_w = np.ascontiguousarray(inputs["L0_attn_conv_w"][:, 0, :], dtype=np.float32)
    shared = {
        "w_in": np.ascontiguousarray(inputs["L0_attn_W_in"], dtype=np.float32),
        "conv_w": conv_w,
        "conv_b": np.ascontiguousarray(inputs["L0_attn_conv_b"], dtype=np.float32),
        "dt_bias": np.ascontiguousarray(inputs["L0_attn_dt_bias"], dtype=np.float32),
        "a_log": np.ascontiguousarray(inputs["L0_attn_A_log"], dtype=np.float32),
        "dvec": np.ascontiguousarray(inputs["L0_attn_D"], dtype=np.float32),
        "norm_scale": np.ascontiguousarray(inputs["L0_attn_norm_scale"], dtype=np.float32),
        "w_out": np.ascontiguousarray(inputs["L0_attn_W_out"], dtype=np.float32),
        "w_gate": np.ascontiguousarray(inputs["L0_ffn_w_gate"], dtype=np.float32),
        "w_up": np.ascontiguousarray(inputs["L0_ffn_w_up"], dtype=np.float32),
        "w_down": np.ascontiguousarray(inputs["L0_ffn_w_down"], dtype=np.float32),
    }
    return [dict(shared, x=np.ascontiguousarray(u[i])) for i in range(8)]


def kernel(**inputs):
    nc, _ = build(stage_limit=99)
    in_maps = make_in_maps(inputs)
    res = run_bass_kernel_spmd(nc, in_maps, core_ids=list(range(8)))
    return np.stack([res.results[i]["out"] for i in range(8)], axis=0)


# version 12
# speedup vs baseline: 1.0035x; 1.0035x over previous
import numpy as np
from contextlib import ExitStack

import concourse.bass as bass
import concourse.mybir as mybir
from concourse.bass_utils import run_bass_kernel_spmd

F32 = mybir.dt.float32
BF16 = mybir.dt.bfloat16
I8 = mybir.dt.int8
I16 = mybir.dt.int16
I32 = mybir.dt.int32
OP = mybir.AluOpType
ACT = mybir.ActivationFunctionType
AX = mybir.AxisListType
MAGIC = 12582912.0

T = 2048
D = 1024
DIP = 4240
DI = 2048
CD = 2176
NH = 16
HD = 128
DS = 64
CH = 256
NCH = 8
DFF = 3072
NKD = 8
NKI = 16
NCF = 24
NTT = 4
NTP = 16
EPS = 1e-5
QA = 32767.0
QW = 7.0

SBUF_BYTES = 229376
BASE = 27648


class Sem:
    __slots__ = ("h", "n", "nc")

    def __init__(self, nc, stack, name):
        self.h = stack.enter_context(nc.semaphore(name))
        self.n = 0
        self.nc = nc

    def inc_raw_cc(self, instr, k=1):
        instr.then_inc(self.h, k)
        self.n += k
        return instr

    def inc(self, instr, k=1, drain=True):
        if k == 1 and drain and instr.ins.engine not in (
                mybir.EngineType.PE, mybir.EngineType.Activation):
            eng = self.nc.engines[instr.ins.engine]
            eng.maybe_drain_then_inc((self.h, k), fusable=True)
        else:
            instr.then_inc(self.h, k)
        self.n += k
        return instr


class Ctx:
    pass


class Arena:

    def __init__(self, nc):
        self.nc = nc
        self.items = []

    def at(self, name, off, shape, dt, p0, p1):
        per = int(np.prod(shape[1:])) * mybir.dt.size(dt)
        assert BASE <= off and off + per <= SBUF_BYTES, (name, off, per)
        for (n2, o2, s2, q0, q1) in self.items:
            if not (p1 < q0 or q1 < p0):
                if not (off + per <= o2 or o2 + s2 <= off):
                    raise AssertionError(f"overlap {name} vs {n2}")
        self.items.append((name, off, per, p0, p1))
        return self.nc.alloc_sbuf_tensor_at(name, shape, dt, offset=off).ap()


def build(stage_limit=99):
    nc = bass.Bass("TRN2", target_bir_lowering=False, detect_race_conditions=False)
    c = Ctx()
    c.nc = nc
    c.stage_limit = stage_limit

    c.x_hbm = nc.dram_tensor("x", [T, D], F32, kind="ExternalInput").ap()
    c.w_in_hbm = nc.dram_tensor("w_in", [D, DIP], F32, kind="ExternalInput").ap()
    c.conv_w_hbm = nc.dram_tensor("conv_w", [4, CD], F32, kind="ExternalInput").ap()
    c.conv_b_hbm = nc.dram_tensor("conv_b", [CD], F32, kind="ExternalInput").ap()
    c.dt_bias_hbm = nc.dram_tensor("dt_bias", [NH], F32, kind="ExternalInput").ap()
    c.a_log_hbm = nc.dram_tensor("a_log", [NH], F32, kind="ExternalInput").ap()
    c.dvec_hbm = nc.dram_tensor("dvec", [NH], F32, kind="ExternalInput").ap()
    c.ns_hbm = nc.dram_tensor("norm_scale", [DI], F32, kind="ExternalInput").ap()
    c.w_out_hbm = nc.dram_tensor("w_out", [DI, D], F32, kind="ExternalInput").ap()
    c.w_gate_hbm = nc.dram_tensor("w_gate", [DFF, D], F32, kind="ExternalInput").ap()
    c.w_up_hbm = nc.dram_tensor("w_up", [DFF, D], F32, kind="ExternalInput").ap()
    c.w_down_hbm = nc.dram_tensor("w_down", [D, DFF], F32, kind="ExternalInput").ap()
    c.out_hbm = nc.dram_tensor("out", [T, D], F32, kind="ExternalOutput").ap()

    c.z_sp = nc.dram_tensor("z_sp", [DI, T], BF16)
    c.up_sp = nc.dram_tensor("up_sp", [DFF, T], F32)
    c.gate_sp = nc.dram_tensor("gate_sp", [DFF, T], F32)
    c.sn_sp = nc.dram_tensor("sn_sp", [D, T], BF16)
    c.dt_dram = nc.dram_tensor("dt_dram", [NH, T], F32)
    c.acs_dram = nc.dram_tensor("acs_dram", [NH, T], F32)
    c.e_dram = nc.dram_tensor("e_dram", [NH, T], F32)
    c.eal_dram = nc.dram_tensor("eal_dram", [NH, NCH], F32)
    c.row_dram = nc.dram_tensor("row_dram", [4, T], F32)
    c.sc_dram = nc.dram_tensor("sc_dram", [1, 16], F32)
    c.cc_in = [nc.dram_tensor(f"cc_in{i}", [1, 8], F32) for i in range(4)]
    c.cc_out = [nc.dram_tensor(f"cc_out{i}", [1, 8], F32) for i in range(4)]

    c.dbg = {}

    def tap(name, shape):
        c.dbg[name] = nc.dram_tensor("dbg_" + name, shape, F32,
                                     kind="ExternalOutput").ap()

    if stage_limit < 99:
        tap("s128", [128, 128])
        tap("xn_tm", [T, D])
        tap("gmax", [1, 16])
        tap("lnst", [128, 12])
        tap("red", [128, 8])
        if stage_limit >= 2:
            tap("z", [DI, T])
            tap("xbc", [CD, T])
            tap("dtr", [NH, T])
        if stage_limit >= 3:
            tap("conv", [CD, T])
            tap("acs", [NH, T])
            tap("dtv", [NH, T])
        if stage_limit >= 4:
            tap("upt", [DFF, T])
            tap("gt", [DFF, T])
            tap("xqv", [D, T])
            tap("wq0", [D, 1536])
        if stage_limit >= 5:
            tap("y", [DI, T])
            tap("xsc2", [T, DI])
        if stage_limit >= 6:
            tap("sn", [D, T])
            tap("rsq", [128, T])
            tap("rrow", [1, T])
        if stage_limit >= 7:
            tap("scal2", [1, 32])
            tap("wqd", [DFF, D])
            tap("g2d", [DFF, T])

    with ExitStack() as ST:
        E = ST.enter_context
        c.ST = ST

        def sb(name, shape, dt=F32):
            t = E(nc.sbuf_tensor(name, shape, dt))
            setattr(c, name, t)
            return t

        sb("sid_bf", [128, 128], BF16)
        sb("sid_f", [128, 128], F32)
        sb("small", [NH, 16])
        sb("dcol", [128, NH])
        sb("cwt", [128, 17 * 5])
        sb("scal", [1, 32])
        sb("scbc", [128, 16])
        sb("red", [128, 8])
        sb("scr128", [128, 128])
        sb("rowt", [1, 128])
        sb("dt_tm", [128, NTP, NH])
        sb("acs_tm", [128, NTP, NH])
        sb("df_tm", [128, NTP, NH])
        sb("rsq", [128, T], BF16)
        sb("ln_st", [128, 12])
        sb("epsc", [128, 1])
        sb("wbm", [128, 32])
        sb("wrs", [128, 32])
        sb("nwt1", [128, 32])
        sb("nwt2", [128, 32])
        sb("wsc", [128, 32])
        sb("wdmx", [128, 32])
        sb("wlna", [128, 32])
        sb("wlnb", [128, 32])
        sb("c_e12", [128, 1])
        sb("onescol_bf", [128, 1], BF16)
        sb("c_e7", [128, 1])

        A = Arena(nc)
        S1 = BASE
        S2 = S1 + 67840
        S3 = S2 + 65536
        GQ0 = SBUF_BYTES - 98304 - 32
        STG = S2
        assert S3 + 65536 <= SBUF_BYTES

        def cur_alloc(base):
            cur = [base]

            def nxt(name, shape, dt, p0, p1):
                t = A.at(name, cur[0], shape, dt, p0, p1)
                setattr(c, name, t)
                cur[0] += int(np.prod(shape[1:])) * mybir.dt.size(dt)
                return cur[0]
            return nxt

        c.w_in_bf = A.at("w_in_bf", S1, [128, NKD, DIP], BF16, 0, 3)
        n1 = cur_alloc(S1)
        n1("conv_acc", [128, 2, T], F32, 4, 4)
        n1("dt_fm", [NH, T], F32, 4, 4)
        n1("acs_fm", [NH, T], F32, 4, 4)
        n1("scr16", [NH, T], F32, 4, 4)
        n1("scr16b", [NH, T], F32, 4, 4)
        n1("ones16", [NH, CH], F32, 4, 4)
        n1("scr16c", [NH, T], F32, 4, 4)
        c.wq_h = [A.at(f"wq_h{i}", S1 + i * 24576, [128, NKD, DFF // 2], BF16, 5, 6)
                  for i in range(2)]
        n2 = cur_alloc(S1 + 49152)
        n2("wdq_scr", [128, 2, D], F32, 5, 6)
        n2("wdq_t", [128, D], F32, 5, 6)
        c.bcf = A.at("bcf", S1 + 61440, [128, T], BF16, 4, 8)
        c.btm = A.at("btm", S1 + 65536, [128, NTP, DS], BF16, 4, 8)
        n3 = cur_alloc(S1)
        n3("acsbc", [128, 2, T], F32, 7, 8)
        n3("zbuf", [128, 2, T], BF16, 7, 8)
        n3("s_st", [64, NH * HD], F32, 7, 8)
        n3("gtm", [128, NCH, 2, CH], BF16, 7, 8)
        n3("d_i", [128, NH * 128], BF16, 7, 8)
        n3("wt", [128, 2, 2, CH], BF16, 7, 8)
        n3("dscr", [128, 2, CH], F32, 7, 8)
        n3("escr", [128, 2, CH], BF16, 7, 8)
        n3("csc", [64, 2, CH], BF16, 7, 8)
        n3("bsc", [128, 2, DS], BF16, 7, 8)
        n3("sbf", [64, HD], BF16, 7, 8)
        n3("e2", [64, NH * NCH], F32, 7, 8)
        n3("cfm", [64, T], BF16, 7, 8)
        n3("e64f", [64, CH], F32, 7, 8)
        n4 = cur_alloc(S1)
        n4("w_out_bf", [128, NKI, D], BF16, 9, 9)
        n4("wo_scr", [128, 2, D], F32, 9, 9)
        n4("ns_col", [128, NKI], F32, 9, 9)
        n4("ysq", [128, 2, 512], BF16, 9, 9)
        n4("snst", [128, 2, 512], BF16, 9, 9)
        n4("rrow", [1, T], F32, 9, 9)
        n4("rrow2", [1, T], F32, 9, 9)
        n5 = cur_alloc(S1)
        n5("wq_d", [128, NCF, D], BF16, 11, 12)
        n5("ofm", [128, NCF, 256], BF16, 11, 12)
        n5("gug", [128, 8, 256], BF16, 12, 12)

        c.ln_a = A.at("ln_a", S2, [128, 2, D], F32, 1, 1)
        c.ln_scr = A.at("ln_scr", S2 + 8192, [128, D], F32, 1, 1)
        c.cw5 = A.at("cw5", S2 + 12288, [5, CD], F32, 0, 2)
        c.xn_tm = A.at("xn_tm", S2 + 21504, [128, NTP, D], BF16, 1, 2)
        c.xbc = A.at("xbc", S2, [128, 16, T], BF16, 3, 7)
        c.yb = A.at("yb", S2, [128, NH, T], BF16, 8, 9)
        n6 = cur_alloc(STG)
        n6("ffn_a", [128, 512], F32, 10, 12)
        n6("ffn_b", [128, 512], F32, 10, 12)
        n6("ffn_c", [128, 512], I32, 10, 12)
        n6("ffn_d", [128, 512], I32, 10, 12)
        n6("wdq_scr2", [128, D], F32, 10, 12)
        n6("wdq_q2", [128, D], I8, 10, 12)
        n6("wdq_n2", [128, D], BF16, 10, 12)
        n6("ffn_p", [128, 512], F32, 10, 12)
        n6("fin_a", [128, 512], F32, 12, 12)
        n6("fin_x", [128, D], F32, 12, 12)
        n6("fin_o", [128, D], F32, 12, 12)
        end6 = n6("snbuf", [128, 2, 256], BF16, 12, 12)
        assert end6 <= GQ0, end6

        c.xn_fm = A.at("xn_fm", S3, [128, NKD, T], BF16, 2, 6)
        c.bc_tmp = A.at("bc_tmp", S3 + 32768, [128, T], BF16, 3, 6)
        c.st16 = A.at("st16", S3 + 36864, [16, 512], F32, 3, 3)
        c.zst = A.at("zst", S3 + 38912, [128, 2, 512], BF16, 3, 3)
        c.ffn_st = A.at("ffn_st", S3 + 36864, [128, 2, 512], F32, 5, 6)
        c.wdq_q = A.at("wdq_q", S3 + 40960, [128, D], I8, 5, 6)
        c.wdq_n = A.at("wdq_n", S3 + 41984, [128, D], BF16, 5, 6)
        c.xsc = A.at("xsc", S3, [128, NTP, DI], BF16, 7, 8)

        c.gq = A.at("gq", GQ0, [128, NCF, T], I16, 10, 12)

        c.pA = E(nc.psum_tensor("pA", [128, 512], F32))
        c.pB = E(nc.psum_tensor("pB", [128, 512], F32))
        c.pC = E(nc.psum_tensor("pC", [128, 512], F32))
        c.pD = E(nc.psum_tensor("pD", [128, 512], F32))
        c.pT = E(nc.psum_tensor("pT", [128, 1024], BF16))
        c.pU = E(nc.psum_tensor("pU", [128, 512], F32))
        c.pV = E(nc.psum_tensor("pV", [128, 512], F32))
        c.pW = E(nc.psum_tensor("pW", [128, 512], F32))

        names = ["ld_x", "ld_w", "ld_sm", "idn", "mz", "ln_sum", "ln_ap",
                 "xn_mx", "mm", "tp_e", "tp2", "ar_l", "ar_d", "sc_bc",
                 "cv", "dtp", "dts", "xq_v", "wdv", "wdg", "wdt",
                 "spl", "fmx", "ar2", "prep", "gt_e", "bc_d", "zr",
                 "ssd_p", "yev", "sup", "sq", "rs_r", "wos", "snv",
                 "rq", "sg2", "ar4", "oq", "ar5", "dwn", "fg", "fx",
                 "fout", "sw", "dbgs", "ip_e", "ln_iv", "cvs", "dtw", "ld_ffw", "spl2", "fev", "sws", "wds", "ssd_d", "ssd_e", "ysqs", "ld_wo", "rsqd", "rq_d", "rq_s", "fsg", "ar4s", "ar5s", "oq2", "rqr_a", "rqr_b", "rqr"]
        c.S = {n: Sem(nc, ST, n) for n in names}
        c.bank_ev = [Sem(nc, ST, f"bk{i}") for i in range(4)]
        c.banks = [c.pA, c.pB, c.pC, c.pD]
        c.bank_uses = [0, 0, 0, 0]
        c.mm_n = 0

        c.block = E(nc.Block())

        _p0_consts(c)
        _p1_ln(c)
        _p2_xn_transpose(c)
        _p2b_ar1(c)
        if stage_limit >= 2:
            _p3_in_proj(c)
        if stage_limit >= 3:
            _p4_conv_dtprep(c)
        if stage_limit >= 4:
            _p5_xq(c)
            _p6_upgate(c)
        if stage_limit >= 5:
            _p7_ssdprep(c)
            _p8_ssd(c)
        if stage_limit >= 6:
            _p9_rsqrt_outproj(c)
        if stage_limit >= 7:
            _p10_requant(c)
            _p11_ophase(c)
            _p12_final(c)
        _taps(c)
        _finish(c)
        left = nc._sbuf_addr_for_side("left")
        assert left <= BASE, f"auto sbuf {left} > BASE {BASE}"
        return nc, c.dbg


def _p0_consts(c):
    S, block = c.S, c.block

    @block.vector
    def _(v):
        v.memset(c.sid_bf[:], 1.0)
        v.memset(c.sid_f[:], 1.0)
        v.memset(c.epsc[:], EPS)
        v.memset(c.c_e12[:], 1e-12)
        v.memset(c.onescol_bf[:], 1.0)
        v.memset(c.c_e7[:], 7e-12)
        v.memset(c.scal[:], 0.0)
        v.memset(c.ln_st[:], 0.0)
        v.memset(c.red[:], 0.0)
        S["mz"].inc(v.memset(c.scr128[:], 0.0))

    @block.gpsimd
    def _(g):
        g.wait_ge(S["mz"].h, 1)
        g.affine_select(out=c.sid_bf[:], in_=c.sid_bf[:], pattern=[[-1, 128]],
                        compare_op=OP.is_equal, fill=0.0, base=0,
                        channel_multiplier=1)
        g.affine_select(out=c.sid_f[:], in_=c.sid_f[:], pattern=[[-1, 128]],
                        compare_op=OP.is_equal, fill=0.0, base=0,
                        channel_multiplier=1)
        S["idn"].inc(g.engine_nop())
        for k in range(NKD):
            S["ld_w"].inc(g.dma_start(out=c.w_in_bf[:, k, :],
                                      in_=c.w_in_hbm[k * 128:(k + 1) * 128, :]), 16)

    @block.sync
    def _(sy):
        S["ld_sm"].inc(sy.dma_start(out=c.small[:, 0:1],
                                    in_=c.dt_bias_hbm.unsqueeze(1)), 16)
        S["ld_sm"].inc(sy.dma_start(out=c.small[:, 2:3],
                                    in_=c.a_log_hbm.unsqueeze(1)), 16)
        S["ld_sm"].inc(sy.dma_start(out=c.cw5[0:4, :], in_=c.conv_w_hbm[:, :]), 16)
        S["ld_sm"].inc(sy.dma_start(out=c.cw5[4:5, :],
                                    in_=c.conv_b_hbm.unsqueeze(0)), 16)
        S["ld_sm"].inc(sy.dma_start(out=c.row_dram[3:4, 0:NH],
                                    in_=c.dvec_hbm.unsqueeze(0)), 16)
        sy.wait_ge(S["ld_sm"].h, S["ld_sm"].n)
        S["ld_sm"].inc(sy.dma_start(
            out=c.dcol[:, :],
            in_=c.row_dram[3:4, 0:NH].broadcast_to([128, NH])), 16)

    @block.scalar
    def _(s):
        s.wait_ge(S["ld_sm"].h, 96)
        s.activation(out=c.small[:, 1:2], in_=c.small[:, 2:3], func=ACT.Exp)
        S["ld_sm"].inc(s.mul(out=c.small[:, 1:2], in_=c.small[:, 1:2], mul=-1.0), 1)

    @block.vector
    def _(v):
        v.wait_ge(S["idn"].h, 1)
        S["idn"].inc(v.engine_nop())

    @block.tensor
    def _(t):
        t.wait_ge(S["ld_sm"].h, 64)
        t.wait_ge(S["idn"].h, 1)
        for cc in range(17):
            ins = t.transpose(c.pU[0:128, cc * 5:cc * 5 + 5],
                              c.cw5[0:5, cc * 128:(cc + 1) * 128],
                              c.sid_f[0:5, 0:5])
        S["dtp"].inc(ins)

    @block.vector
    def _(v):
        v.wait_ge(S["dtp"].h, 1)
        v.tensor_copy(out=c.scr128[0:1, 126:127], in_=c.scr128[0:1, 125:126])
        v.tensor_copy(out=c.scr128[0:1, 126:127], in_=c.scr128[0:1, 125:126])
        S["dtp"].inc(v.tensor_copy(out=c.cwt[:, 0:85], in_=c.pU[:, 0:85]))


def _p1_ln(c):
    S, block = c.S, c.block

    @block.sync
    def _(sy):
        for i in range(NTP):
            if i >= 2:
                sy.wait_ge(S["ln_ap"].h, i - 1)
            S["ld_x"].inc(sy.dma_start(out=c.ln_a[:, i % 2, :],
                                       in_=c.x_hbm[i * 128:(i + 1) * 128, :]), 16)

    @block.vector
    def _(v):
        for i in range(NTP):
            v.wait_ge(S["ld_x"].h, (i + 1) * 16)
            S["ln_sum"].inc(v.reduce_sum(out=c.ln_st[:, 0:1],
                                         in_=c.ln_a[:, i % 2, :], axis=AX.X))
            v.wait_ge(S["ln_iv"].h, i + 1)
            S["ln_ap"].inc(v.tensor_scalar(out=c.xn_tm[:, i, :],
                                           in0=c.ln_a[:, i % 2, :],
                                           scalar1=c.ln_st[:, 2:3],
                                           scalar2=c.ln_st[:, 8:9],
                                           op0=OP.subtract, op1=OP.mult))
            v.tensor_reduce(out=c.red[:, 0:1], in_=c.xn_tm[:, i, :], axis=AX.X,
                            op=OP.max, apply_absolute_value=True)
            if i == 0:
                ins = v.tensor_copy(out=c.red[:, 1:2], in_=c.red[:, 0:1])
            else:
                ins = v.tensor_tensor(out=c.red[:, 1:2], in0=c.red[:, 1:2],
                                      in1=c.red[:, 0:1], op=OP.max)
            S["xn_mx"].inc(ins)

    @block.scalar
    def _(s):
        for i in range(NTP):
            s.wait_ge(S["ld_x"].h, (i + 1) * 16)
            if i >= 1:
                s.wait_ge(S["ln_ap"].h, i)
            s.activation(out=c.ln_scr[:], in_=c.ln_a[:, i % 2, :],
                         func=ACT.Square, accum_out=c.ln_st[:, 1:2])
            s.wait_ge(S["ln_sum"].h, i + 1)
            s.mul(out=c.ln_st[:, 2:3], in_=c.ln_st[:, 0:1], mul=1.0 / D)
            s.wait_ge(S["mz"].h, 1)
            s.activation(out=c.ln_st[:, 3:4], in_=c.ln_st[:, 1:2], func=ACT.Identity,
                         scale=1.0 / D, bias=c.epsc[:, 0:1])
            s.activation(out=c.ln_st[:, 4:5], in_=c.ln_st[:, 2:3], func=ACT.Square)
            s.activation(out=c.ln_st[:, 5:6], in_=c.ln_st[:, 4:5], func=ACT.Identity,
                         scale=-1.0, bias=c.ln_st[:, 3:4])
            s.activation(out=c.ln_st[:, 7:8], in_=c.ln_st[:, 5:6], func=ACT.Ln)
            S["ln_iv"].inc(s.activation(out=c.ln_st[:, 8:9], in_=c.ln_st[:, 7:8],
                                        func=ACT.Exp, scale=-0.5))


def _p2_xn_transpose(c):
    S, block = c.S, c.block

    @block.tensor
    def _(t):
        t.wait_ge(S["idn"].h, 2)
        for i in range(NTP):
            t.wait_ge(S["ln_ap"].h, i + 1)
            if i >= 1:
                t.wait_ge(S["tp_e"].h, i)
            for j in range(NKD):
                ins = t.transpose(c.pT[:, j * 128:(j + 1) * 128],
                                  c.xn_tm[:, i, j * 128:(j + 1) * 128],
                                  c.sid_bf[:])
            S["mm"].inc(ins)
            c.mm_n += 1

    @block.vector
    def _(v):
        for i in range(NTP):
            v.wait_ge(S["mm"].h, i + 1)
            v.tensor_copy(out=c.scr128[0:1, 126:127], in_=c.scr128[0:1, 125:126])
            v.tensor_copy(out=c.scr128[0:1, 126:127], in_=c.scr128[0:1, 125:126])
            S["tp_e"].inc(v.tensor_copy(
                out=c.xn_fm[:, :, i * 128:(i + 1) * 128],
                in_=c.pT[:].rearrange("p (a b) -> p a b", a=NKD)))


def _p2b_ar1(c):
    S, block = c.S, c.block

    @block.vector
    def _(v):
        v.wait_ge(S["xn_mx"].h, NTP)
        S["xn_mx"].inc(v.tensor_copy(out=c.scr128[:, 0:1], in_=c.red[:, 1:2]))

    @block.tensor
    def _(t):
        t.wait_ge(S["xn_mx"].h, NTP + 1)
        S["tp2"].inc(t.transpose(c.pU[:, 0:128], c.scr128[:, :], c.sid_f[:]))

    @block.scalar
    def _(s):
        s.wait_ge(S["tp2"].h, 1)
        s.copy(out=c.scr128[0:1, 124:125], in_=c.scr128[0:1, 123:124])
        s.copy(out=c.scr128[0:1, 124:125], in_=c.scr128[0:1, 123:124])
        S["tp2"].inc(s.copy(out=c.rowt[0:1, :], in_=c.pU[0:1, 0:128]))

    @block.vector
    def _(v):
        v.wait_ge(S["tp2"].h, 2)
        S["ar_l"].inc(v.tensor_reduce(out=c.scal[0:1, 0:1], in_=c.rowt[0:1, :],
                                      axis=AX.X, op=OP.max))

    @block.gpsimd
    def _(g):
        g.wait_ge(S["ar_l"].h, 1)
        S["sw"].inc(g.dma_start(out=c.cc_in[0][:, :], in_=c.scal[0:1, 0:8]), 16)
        g.wait_ge(S["sw"].h, S["sw"].n)
        S["ar_d"].inc_raw_cc(g.collective_compute(
            "AllReduce", OP.max, replica_groups=[list(range(8))],
            ins=[c.cc_in[0][:, :].opt()], outs=[c.cc_out[0][:, :].opt()]))
        g.wait_ge(S["ar_d"].h, 1)
        S["sw"].inc(g.dma_start(out=c.scal[0:1, 8:9], in_=c.cc_out[0][0:1, 0:1]), 16)
        g.wait_ge(S["sw"].h, S["sw"].n)
        S["ar_d"].inc(g.engine_nop())

    @block.vector
    def _(v):
        v.wait_ge(S["ar_d"].h, 2)
        S["ar_d"].inc(v.tensor_scalar(out=c.scal[0:1, 9:10], in0=c.scal[0:1, 8:9],
                                      scalar1=1.0 / QA, scalar2=c.c_e12[0:1, 0:1],
                                      op0=OP.mult, op1=OP.max))

    @block.scalar
    def _(s):
        s.wait_ge(S["ar_d"].h, 3)
        s.activation(out=c.scal[0:1, 11:12], in_=c.scal[0:1, 9:10], func=ACT.Ln)
        S["ar_d"].inc(s.activation(out=c.scal[0:1, 10:11], in_=c.scal[0:1, 11:12],
                                   func=ACT.Exp, scale=-1.0))

    @block.sync
    def _(sy):
        sy.wait_ge(S["ar_d"].h, 4)
        S["sws"].inc(sy.dma_start(out=c.sc_dram[0:1, 0:2], in_=c.scal[0:1, 9:11]), 16)
        sy.wait_ge(S["sws"].h, S["sws"].n)
        S["sc_bc"].inc(sy.dma_start(out=c.scbc[:, 0:1],
                                    in_=c.sc_dram[0:1, 1:2].broadcast_to([128, 1])), 16)
        S["sc_bc"].inc(sy.dma_start(out=c.scbc[:, 1:2],
                                    in_=c.sc_dram[0:1, 0:1].broadcast_to([128, 1])), 16)


def _p3_in_proj(c):
    S, block = c.S, c.block
    groups = [(m, tt) for m in range(34) for tt in range(NTT)]
    c.ip_groups = groups
    mm_base = c.mm_n

    @block.tensor
    def _(t):
        t.wait_ge(S["ld_w"].h, NKD * 16)
        t.wait_ge(S["tp_e"].h, NTP)
        for gi, (m, tt) in enumerate(groups):
            bi = gi % 4
            if gi >= 4:
                t.wait_ge(S["ip_e"].h, gi - 3)
            mw = 16 if m == 33 else 128
            for k in range(NKD):
                ins = t.matmul(c.banks[bi][0:mw, 0:512],
                               c.w_in_bf[:, k, m * 128:m * 128 + mw],
                               c.xn_fm[:, k, tt * 512:(tt + 1) * 512],
                               start=(k == 0), stop=(k == NKD - 1))
            S["mm"].inc(ins)
            c.mm_n += 1

    @block.vector
    def _(v):
        zi = 0
        for gi, (m, tt) in enumerate(groups):
            bi = gi % 4
            v.wait_ge(S["mm"].h, mm_base + gi + 1)
            v.tensor_copy(out=c.scr128[0:1, 126:127], in_=c.scr128[0:1, 125:126])
            v.tensor_copy(out=c.scr128[0:1, 126:127], in_=c.scr128[0:1, 125:126])
            if m < 16:
                if zi >= 2:
                    v.wait_ge(S["spl"].h, 16 * (zi - 1))
                ins = v.tensor_copy(out=c.zst[:, zi % 2, :],
                                    in_=c.banks[bi][0:128, 0:512])
                zi += 1
            elif m < 32:
                ins = v.tensor_copy(out=c.xbc[:, m - 16, tt * 512:(tt + 1) * 512],
                                    in_=c.banks[bi][0:128, 0:512])
            elif m == 32:
                ins = v.tensor_copy(out=c.bc_tmp[:, tt * 512:(tt + 1) * 512],
                                    in_=c.banks[bi][0:128, 0:512])
            else:
                if tt >= 1:
                    v.wait_ge(S["spl"].h, 16 * (64 + tt))
                ins = v.tensor_copy(out=c.st16[0:16, :],
                                    in_=c.banks[bi][0:16, 0:512])
            S["ip_e"].inc(ins)

    @block.sync
    def _(sy):
        zi = 0
        for gi, (m, tt) in enumerate(groups):
            if m < 16:
                sy.wait_ge(S["ip_e"].h, gi + 1)
                S["spl"].inc(sy.dma_start(
                    out=c.z_sp[m * 128:(m + 1) * 128, tt * 512:(tt + 1) * 512],
                    in_=c.zst[:, zi % 2, :]), 16)
                zi += 1
            elif m == 33:
                sy.wait_ge(S["ip_e"].h, gi + 1)
                S["spl"].inc(sy.dma_start(
                    out=c.dt_dram[0:16, tt * 512:(tt + 1) * 512],
                    in_=c.st16[0:16, :]), 16)


def _p4_conv_dtprep(c):
    S, block = c.S, c.block

    def src_tile(cc):
        return c.xbc[:, cc, :] if cc < 16 else c.bc_tmp[:, :]

    @block.sync
    def _(sy):
        sy.wait_ge(S["spl"].h, S["spl"].n)
        S["spl"].inc(sy.dma_start(out=c.dt_fm[:, :], in_=c.dt_dram[:, :]), 16)

    dt_ld = S["spl"].n

    @block.vector
    def _(v):
        v.wait_ge(S["dtp"].h, 2)
        for cc in range(17):
            need = min((17 + cc) * 4, 132) if cc < 16 else 132
            v.wait_ge(S["ip_e"].h, need)
            if cc >= 2:
                v.wait_ge(S["cvs"].h, cc - 1)
            src = src_tile(cc)
            acc = c.conv_acc[:, cc % 2, :]
            w = lambda k: c.cwt[:, cc * 5 + k:cc * 5 + k + 1]
            v.tensor_scalar(out=acc, in0=src, scalar1=w(3), scalar2=w(4),
                            op0=OP.mult, op1=OP.add)
            for sh in (1, 2, 3):
                v.scalar_tensor_tensor(out=acc[:, sh:T], in0=src[:, 0:T - sh],
                                       scalar=w(3 - sh), in1=acc[:, sh:T],
                                       op0=OP.mult, op1=OP.add)
            S["cv"].inc(v.engine_nop())

    @block.scalar
    def _(s):
        for cc in range(17):
            s.wait_ge(S["cv"].h, cc + 1)
            S["cvs"].inc(s.activation(out=src_tile(cc), in_=c.conv_acc[:, cc % 2, :],
                                      func=ACT.Silu))

    @block.vector
    def _(v):
        v.wait_ge(S["spl"].h, dt_ld)
        S["dtw"].inc(v.tensor_scalar(out=c.scr16[:, :], in0=c.dt_fm[:, :],
                                     scalar1=c.small[:, 0:1], scalar2=None,
                                     op0=OP.add))

    @block.scalar
    def _(s):
        s.wait_ge(S["dtw"].h, 1)
        s.activation(out=c.acs_fm[:, :], in_=c.scr16[:, :], func=ACT.Abs)
        S["dts"].inc(s.activation(out=c.scr16b[:, :], in_=c.acs_fm[:, :],
                                  func=ACT.Exp, scale=-1.0))

    @block.vector
    def _(v):
        v.wait_ge(S["dts"].h, 1)
        S["dtw"].inc(v.tensor_scalar(out=c.scr16b[:, :], in0=c.scr16b[:, :],
                                     scalar1=1.0, scalar2=None, op0=OP.add))

    @block.scalar
    def _(s):
        s.wait_ge(S["dtw"].h, 2)
        S["dts"].inc(s.activation(out=c.scr16b[:, :], in_=c.scr16b[:, :],
                                  func=ACT.Ln))

    @block.vector
    def _(v):
        v.wait_ge(S["dts"].h, 2)
        v.memset(c.ones16[:], 1.0)
        v.scalar_tensor_tensor(out=c.dt_fm[:, :], in0=c.scr16[:, :], scalar=0.0,
                               in1=c.scr16b[:, :], op0=OP.max, op1=OP.add)
        v.tensor_scalar(out=c.scr16[:, :], in0=c.dt_fm[:, :],
                        scalar1=c.small[:, 1:2], scalar2=None, op0=OP.mult)
        for ch in range(NCH):
            sl = slice(ch * CH, (ch + 1) * CH)
            v.tensor_tensor_scan(out=c.acs_fm[:, sl], data0=c.ones16[:, :],
                                 data1=c.scr16[:, sl], initial=0.0,
                                 op0=OP.mult, op1=OP.add)
        S["dtw"].inc(v.engine_nop())

    @block.scalar
    def _(s):
        s.wait_ge(S["dtw"].h, 3)
        s.activation(out=c.scr16b[:, :], in_=c.acs_fm[:, :], func=ACT.Exp)
        S["dts"].inc(s.activation(out=c.small[:, 3:11],
                                  in_=c.acs_fm[:, CH - 1::CH], func=ACT.Exp))

    @block.sync
    def _(sy):
        sy.wait_ge(S["dts"].h, 3)
        S["spl"].inc(sy.dma_start(out=c.acs_dram[:, :], in_=c.acs_fm[:, :]), 16)
        S["spl"].inc(sy.dma_start(out=c.e_dram[:, :], in_=c.scr16b[:, :]), 16)
        S["spl"].inc(sy.dma_start(out=c.eal_dram[:, :], in_=c.small[:, 3:11]), 16)

    @block.vector
    def _(v):
        v.wait_ge(S["dtw"].h, 3)
        for ch in range(NCH):
            sl = slice(ch * CH, (ch + 1) * CH)
            last = c.acs_fm[:, ch * CH + CH - 1:ch * CH + CH]
            v.tensor_scalar(out=c.scr16[:, sl], in0=c.acs_fm[:, sl],
                            scalar1=last, scalar2=None, op0=OP.subtract)
        S["dtw"].inc(v.engine_nop())

    @block.scalar
    def _(s):
        s.wait_ge(S["dtw"].h, 4)
        S["dts"].inc(s.activation(out=c.scr16c[:, :], in_=c.scr16[:, :],
                                  func=ACT.Exp, scale=-1.0))

    @block.tensor
    def _(t):
        t.wait_ge(S["dtw"].h, 3)
        for i in range(NTP):
            t.transpose(c.pU[0:128, i * 16:(i + 1) * 16],
                        c.dt_fm[0:16, i * 128:(i + 1) * 128], c.sid_f[0:16, 0:16])
        for i in range(NTP):
            ins = t.transpose(c.pU[0:128, 256 + i * 16:256 + (i + 1) * 16],
                              c.acs_fm[0:16, i * 128:(i + 1) * 128],
                              c.sid_f[0:16, 0:16])
        S["tp2"].inc(ins)
        t.wait_ge(S["dts"].h, 4)
        for i in range(NTP):
            ins = t.transpose(c.pV[0:128, i * 16:(i + 1) * 16],
                              c.scr16c[0:16, i * 128:(i + 1) * 128],
                              c.sid_f[0:16, 0:16])
        S["tp2"].inc(ins)
        t.wait_ge(S["cvs"].h, 17)
        for i in range(NTP):
            ins = t.transpose(c.pT[:, i * 64:(i + 1) * 64],
                              c.bc_tmp[0:64, i * 128:(i + 1) * 128],
                              c.sid_bf[0:64, 0:64])
        S["tp2"].inc(ins)

    tp_base = 2

    @block.vector
    def _(v):
        v.wait_ge(S["tp2"].h, tp_base + 1)
        v.tensor_copy(out=c.scr128[0:1, 126:127], in_=c.scr128[0:1, 125:126])
        v.tensor_copy(out=c.scr128[0:1, 126:127], in_=c.scr128[0:1, 125:126])
        v.tensor_copy(out=c.dt_tm[:].rearrange("p a b -> p (a b)"),
                      in_=c.pU[:, 0:256])
        v.tensor_copy(out=c.acs_tm[:].rearrange("p a b -> p (a b)"),
                      in_=c.pU[:, 256:512])
        v.wait_ge(S["tp2"].h, tp_base + 2)
        v.tensor_copy(out=c.scr128[0:1, 126:127], in_=c.scr128[0:1, 125:126])
        v.tensor_copy(out=c.scr128[0:1, 126:127], in_=c.scr128[0:1, 125:126])
        v.tensor_copy(out=c.df_tm[:].rearrange("p a b -> p (a b)"),
                      in_=c.pV[:, 0:256])
        v.wait_ge(S["tp2"].h, tp_base + 3)
        v.tensor_copy(out=c.scr128[0:1, 126:127], in_=c.scr128[0:1, 125:126])
        v.tensor_copy(out=c.scr128[0:1, 126:127], in_=c.scr128[0:1, 125:126])
        v.tensor_copy(out=c.btm[:].rearrange("p a b -> p (a b)"),
                      in_=c.pT[:, 0:1024])
        S["prep"].inc(v.tensor_copy(out=c.bcf[:, :], in_=c.bc_tmp[:, :]))


def _p5_xq(c):
    S, block = c.S, c.block
    mm_after_p3 = c.mm_n
    qi16 = c.ffn_st[:, 0, :].bitcast(I16)

    @block.vector
    def _(v):
        v.wait_ge(S["sc_bc"].h, 32)
        v.wait_ge(S["mm"].h, mm_after_p3)
        v.wait_ge(S["dts"].h, 4)
        v.wait_ge(S["cvs"].h, 17)
        for k in range(NKD):
            for hf in range(2):
                sl = slice(hf * 1024, (hf + 1) * 1024)
                v.tensor_scalar(out=c.wdq_t[:, :], in0=c.xn_fm[:, k, sl],
                                scalar1=c.scbc[:, 0:1], scalar2=MAGIC,
                                op0=OP.mult, op1=OP.add)
                v.tensor_scalar(out=qi16, in0=c.wdq_t[:, :],
                                scalar1=MAGIC, scalar2=None, op0=OP.subtract)
                v.tensor_scalar(out=c.xn_fm[:, k, sl], in0=qi16,
                                scalar1=c.scbc[:, 1:2], scalar2=None, op0=OP.mult)
        S["xq_v"].inc(v.engine_nop())


def _p6_upgate(c):
    S, block = c.S, c.block
    HALVES = [("up", 0), ("up", 1), ("gate", 0), ("gate", 1)]
    whbm = {"up": c.w_up_hbm, "gate": c.w_gate_hbm}
    wsp = {"up": c.up_sp, "gate": c.gate_sp}

    items = []
    for ch in range(12):
        items.append(("deq", 0, ch))
    for H in range(4):
        mmq = [("mm", H, m, tt) for m in range(12) for tt in range(NTT)]
        deq = [("deq", H + 1, ch) for ch in range(12)] if H < 3 else []
        di = 0
        for i, it in enumerate(mmq):
            items.append(it)
            if i % 4 == 3 and di < len(deq):
                items.append(deq[di]); di += 1
        items.extend(deq[di:])
    c.p6_items = items

    mm_base = c.mm_n
    nitems = len(items)

    def ld_chunk(sy, ci):
        H, ch = divmod(ci, 12)
        w, hf = HALVES[H]
        if ci >= 2:
            sy.wait_ge(S["wdv"].h, 2 * (ci - 2) + 2)
        S["ld_ffw"].inc(sy.dma_start(
            out=c.wdq_scr[:, ci % 2, :],
            in_=whbm[w][hf * 1536 + ch * 128: hf * 1536 + (ch + 1) * 128, :]), 16)

    @block.sync
    def _(sy):
        sy.wait_ge(S["tp2"].h, 5)
        sy.wait_ge(S["dts"].h, 4)
        sy.wait_ge(S["cvs"].h, 17)
        ld_chunk(sy, 0)
        ld_chunk(sy, 1)
        seq = 0
        for it in items:
            if it[0] == "deq":
                ci = it[1] * 12 + it[2]
                if ci + 2 < 48:
                    ld_chunk(sy, ci + 2)
            else:
                _, H, m, tt = it
                w, hf = HALVES[H]
                sy.wait_ge(S["fev"].h, seq + 1)
                S["spl2"].inc(sy.dma_start(
                    out=wsp[w][hf * 1536 + m * 128: hf * 1536 + (m + 1) * 128,
                               tt * 512:(tt + 1) * 512],
                    in_=c.ffn_st[:, seq % 2, :]), 16)
                seq += 1

    @block.vector
    def _(v):
        v.wait_ge(S["xq_v"].h, 1)
        seq = 0
        idx = 0
        for it in items:
            idx += 1
            if it[0] == "deq":
                H, ch = it[1], it[2]
                ci = H * 12 + ch
                scr = c.wdq_scr[:, ci % 2, :]
                v.wait_ge(S["ld_ffw"].h, (ci + 1) * 16)
                if ci >= 1:
                    v.wait_ge(S["wdg"].h, 2 * ci)
                    v.wait_ge(S["wds"].h, ci)
                v.tensor_reduce(out=c.wbm[:].unsqueeze(2),
                                in_=scr.rearrange("p (b k) -> p b k", k=32),
                                axis=AX.X, op=OP.max, apply_absolute_value=True)
                S["wdv"].inc(v.tensor_scalar(out=c.wdmx[:], in0=c.wbm[:],
                                             scalar1=c.c_e7[:, 0:1],
                                             scalar2=None, op0=OP.max))
                v.wait_ge(S["wdg"].h, 2 * ci + 1)
                v.tensor_scalar(out=scr, in0=c.wdq_t[:, :], scalar1=MAGIC,
                                scalar2=None, op0=OP.add)
                S["wdv"].inc(v.tensor_scalar(out=c.wdq_q[:, :], in0=scr,
                                             scalar1=MAGIC, scalar2=None,
                                             op0=OP.subtract))
                v.wait_ge(S["mm"].h, mm_base + idx)
                v.tensor_copy(out=c.scr128[0:1, 126:127], in_=c.scr128[0:1, 125:126])
                v.tensor_copy(out=c.scr128[0:1, 126:127], in_=c.scr128[0:1, 125:126])
                S["wdt"].inc(v.tensor_copy(
                    out=c.wq_h[H % 2][:, :, ch * 128:(ch + 1) * 128],
                    in_=c.pT[:].rearrange("p (a b) -> p a b", a=NKD)))
            else:
                v.wait_ge(S["mm"].h, mm_base + idx)
                v.tensor_copy(out=c.scr128[0:1, 126:127], in_=c.scr128[0:1, 125:126])
                v.tensor_copy(out=c.scr128[0:1, 126:127], in_=c.scr128[0:1, 125:126])
                slot = seq % 2
                if seq >= 2:
                    v.wait_ge(S["spl2"].h, (seq - 1) * 16)
                v.tensor_copy(out=c.ffn_st[:, slot, :],
                              in_=c.banks[seq % 4][0:128, 0:512])
                col = 2 if it[1] < 2 else 3
                v.tensor_reduce(out=c.red[:, 4:5], in_=c.ffn_st[:, slot, :],
                                axis=AX.X, op=OP.max, apply_absolute_value=True)
                if seq == 0 or (col == 3 and seq == 96):
                    ins = v.tensor_copy(out=c.red[:, col:col + 1], in_=c.red[:, 4:5])
                else:
                    ins = v.tensor_tensor(out=c.red[:, col:col + 1],
                                          in0=c.red[:, col:col + 1],
                                          in1=c.red[:, 4:5], op=OP.max)
                S["fev"].inc(ins)
                S["fmx"].inc(v.engine_nop())
                seq += 1

    @block.gpsimd
    def _(g):
        seq = 0
        for it in items:
            if it[0] == "deq":
                H, ch = it[1], it[2]
                ci = H * 12 + ch
                scr = c.wdq_scr[:, ci % 2, :]
                g.wait_ge(S["wds"].h, ci + 1)
                g.tensor_tensor(out=c.nwt1[:], in0=c.wdmx[:], in1=c.wlnb[:],
                                op=OP.mult)
                g.tensor_scalar(out=c.nwt2[:], in0=c.nwt1[:], scalar1=-1.0,
                                scalar2=2.0, op0=OP.mult, op1=OP.add)
                g.tensor_tensor(out=c.nwt1[:], in0=c.wlnb[:], in1=c.nwt2[:],
                                op=OP.mult)
                g.tensor_scalar(out=c.wrs[:], in0=c.nwt1[:], scalar1=QW,
                                scalar2=None, op0=OP.mult)
                S["wdg"].inc(g.tensor_tensor(
                    out=c.wdq_t[:].rearrange("p (b k) -> p b k", k=32),
                    in0=scr.rearrange("p (b k) -> p b k", k=32),
                    in1=c.wrs[:].unsqueeze(2).broadcast_to([128, 32, 32]),
                    op=OP.mult))
                g.wait_ge(S["wdv"].h, 2 * ci + 2)
                if ci >= 1:
                    g.wait_ge(S["wdt"].h, ci)
                S["wdg"].inc(g.tensor_tensor(
                    out=c.wdq_n[:].rearrange("p (b k) -> p b k", k=32),
                    in0=c.wdq_q[:].rearrange("p (b k) -> p b k", k=32),
                    in1=c.wsc[:].unsqueeze(2).broadcast_to([128, 32, 32]),
                    op=OP.mult))
            else:
                seq += 1

    @block.scalar
    def _(sc_):
        for it in items:
            if it[0] != "deq":
                continue
            H, ch = it[1], it[2]
            ci = H * 12 + ch
            sc_.wait_ge(S["wdv"].h, 2 * ci + 1)
            if ci >= 1:
                sc_.wait_ge(S["wdg"].h, 2 * ci)
            sc_.activation(out=c.wlna[:], in_=c.wdmx[:], func=ACT.Ln)
            sc_.activation(out=c.wlnb[:], in_=c.wlna[:], func=ACT.Exp, scale=-1.0)
            sc_.mul(out=c.wrs[:], in_=c.wlnb[:], mul=QW)
            S["wds"].inc(sc_.mul(out=c.wsc[:], in_=c.wdmx[:], mul=1.0 / QW))

    @block.tensor
    def _(t):
        t.wait_ge(S["ip_e"].h, 136)
        seq = 0
        for it in items:
            if it[0] == "deq":
                H, ch = it[1], it[2]
                ci = H * 12 + ch
                t.wait_ge(S["wdg"].h, 2 * ci + 2)
                for k in range(NKD):
                    ins = t.transpose(c.pT[:, k * 128:(k + 1) * 128],
                                      c.wdq_n[:, k * 128:(k + 1) * 128],
                                      c.sid_bf[:])
                S["mm"].inc(ins)
            else:
                _, H, m, tt = it
                w, hf = HALVES[H]
                slab = c.wq_h[H % 2]
                t.wait_ge(S["wdt"].h, (H + 1) * 12)
                bi = seq % 4
                if seq >= 4:
                    t.wait_ge(S["fev"].h, seq - 3)
                for k in range(NKD):
                    ins = t.matmul(c.banks[bi][0:128, 0:512],
                                   slab[:, k, m * 128:(m + 1) * 128],
                                   c.xn_fm[:, k, tt * 512:(tt + 1) * 512],
                                   start=(k == 0), stop=(k == NKD - 1))
                S["mm"].inc(ins)
                seq += 1
            c.mm_n += 1

    @block.vector
    def _(v):
        v.wait_ge(S["fmx"].h, 192)
        v.tensor_copy(out=c.scr128[:, 0:1], in_=c.red[:, 2:3])
        S["ar2"].inc(v.tensor_copy(out=c.scr128[:, 1:2], in_=c.red[:, 3:4]))

    @block.tensor
    def _(t):
        t.wait_ge(S["ar2"].h, 1)
        S["mm"].inc(t.transpose(c.pU[:, 0:128], c.scr128[:, :], c.sid_f[:]))
        c.mm_n += 1

    mm_tp = c.mm_n

    @block.vector
    def _(v):
        v.wait_ge(S["mm"].h, mm_tp)
        v.tensor_copy(out=c.scr128[0:1, 126:127], in_=c.scr128[0:1, 125:126])
        v.tensor_copy(out=c.scr128[0:1, 126:127], in_=c.scr128[0:1, 125:126])
        S["ar2"].inc(v.tensor_reduce(out=c.red[0:2, 5:6], in_=c.pU[0:2, 0:128],
                                     axis=AX.X, op=OP.max))

    @block.gpsimd
    def _(g):
        g.wait_ge(S["ar2"].h, 2)
        S["sw"].inc(g.dma_start(out=c.cc_in[1][0:1, 0:2], in_=c.red[0:2, 5:6]), 16)
        S["sw"].inc(g.dma_start(out=c.cc_in[1][0:1, 2:8], in_=c.scal[0:1, 2:8]), 16)
        g.wait_ge(S["sw"].h, S["sw"].n)
        S["ar2"].inc_raw_cc(g.collective_compute(
            "AllReduce", OP.max, replica_groups=[list(range(8))],
            ins=[c.cc_in[1][:, :].opt()], outs=[c.cc_out[1][:, :].opt()]))
        g.wait_ge(S["ar2"].h, 3)
        S["sw"].inc(g.dma_start(out=c.scal[0:1, 12:14], in_=c.cc_out[1][0:1, 0:2]), 16)
        g.wait_ge(S["sw"].h, S["sw"].n)
        S["ar2"].inc(g.engine_nop())

    @block.vector
    def _(v):
        v.wait_ge(S["ar2"].h, 4)
        v.tensor_scalar(out=c.scal[0:1, 14:15], in0=c.scal[0:1, 12:13],
                        scalar1=1.0 / QA, scalar2=c.c_e12[0:1, 0:1],
                        op0=OP.mult, op1=OP.max)
        S["ar2"].inc(v.tensor_scalar(out=c.scal[0:1, 15:16], in0=c.scal[0:1, 13:14],
                                     scalar1=1.0 / QA, scalar2=c.c_e12[0:1, 0:1],
                                     op0=OP.mult, op1=OP.max))

    @block.scalar
    def _(s):
        s.wait_ge(S["ar2"].h, 5)
        s.activation(out=c.scal[0:1, 18:19], in_=c.scal[0:1, 14:15], func=ACT.Ln)
        s.activation(out=c.scal[0:1, 16:17], in_=c.scal[0:1, 18:19], func=ACT.Exp,
                     scale=-1.0)
        s.activation(out=c.scal[0:1, 18:19], in_=c.scal[0:1, 15:16], func=ACT.Ln)
        S["ar2"].inc(s.activation(out=c.scal[0:1, 17:18], in_=c.scal[0:1, 18:19],
                                  func=ACT.Exp, scale=-1.0))

    @block.sync
    def _(sy):
        sy.wait_ge(S["ar2"].h, 6)
        S["sws"].inc(sy.dma_start(out=c.sc_dram[0:1, 2:4], in_=c.scal[0:1, 16:18]), 16)
        S["sws"].inc(sy.dma_start(out=c.sc_dram[0:1, 4:5], in_=c.scal[0:1, 15:16]), 16)
        S["sws"].inc(sy.dma_start(out=c.sc_dram[0:1, 5:6], in_=c.scal[0:1, 14:15]), 16)
        sy.wait_ge(S["sws"].h, S["sws"].n)
        S["sc_bc"].inc(sy.dma_start(out=c.scbc[:, 2:3],
                                    in_=c.sc_dram[0:1, 2:3].broadcast_to([128, 1])), 16)
        S["sc_bc"].inc(sy.dma_start(out=c.scbc[:, 3:4],
                                    in_=c.sc_dram[0:1, 3:4].broadcast_to([128, 1])), 16)
        S["sc_bc"].inc(sy.dma_start(out=c.scbc[:, 4:5],
                                    in_=c.sc_dram[0:1, 4:5].broadcast_to([128, 1])), 16)


def _p7_ssdprep(c):
    S, block = c.S, c.block
    mm_b = c.mm_n
    c.bc_ready = {}
    c.cfm_ready = [0]

    @block.sync
    def _(sy):
        sy.wait_ge(S["fev"].h, 192)
        S["bc_d"].inc(sy.dma_start(out=c.cfm[0:64, :], in_=c.bcf[64:128, :]), 16)
        c.cfm_ready[0] = S["bc_d"].n
        for h in range(NH):
            S["bc_d"].inc(sy.dma_start(
                out=c.e2[:, h * NCH:(h + 1) * NCH],
                in_=c.eal_dram[h:h + 1, :].broadcast_to([64, NCH])), 16)
        for h in range(2):
            S["bc_d"].inc(sy.dma_start(
                out=c.acsbc[:, h, :],
                in_=c.acs_dram[h:h + 1, :].broadcast_to([128, T])), 16)
            S["bc_d"].inc(sy.dma_start(
                out=c.zbuf[:, h, :],
                in_=c.z_sp[h * 128:(h + 1) * 128, :]), 16)
            c.bc_ready[h] = S["bc_d"].n

    @block.tensor
    def _(t):
        for b in range(32):
            i, jj = divmod(b, 2)
            if b >= 1:
                t.wait_ge(S["tp_e"].h, NTP + b)
            for j in range(8):
                ins = t.transpose(c.pT[:, j * 128:(j + 1) * 128],
                                  c.xbc[:, jj * 8 + j, i * 128:(i + 1) * 128],
                                  c.sid_bf[:])
            S["mm"].inc(ins)
            c.mm_n += 1

    @block.vector
    def _(v):
        for b in range(32):
            i, jj = divmod(b, 2)
            v.wait_ge(S["mm"].h, mm_b + b + 1)
            v.tensor_copy(out=c.scr128[0:1, 126:127], in_=c.scr128[0:1, 125:126])
            v.tensor_copy(out=c.scr128[0:1, 126:127], in_=c.scr128[0:1, 125:126])
            S["tp_e"].inc(v.tensor_copy(
                out=c.xsc[:, i, jj * 1024:(jj + 1) * 1024]
                    .rearrange("p (a b) -> p a b", b=128),
                in_=c.pT[:].rearrange("p (a b) -> p a b", a=8)))
        for i in range(NTP):
            for h in range(NH):
                ins = v.tensor_scalar(out=c.xsc[:, i, h * 128:(h + 1) * 128],
                                      in0=c.xsc[:, i, h * 128:(h + 1) * 128],
                                      scalar1=c.dt_tm[:, i, h:h + 1],
                                      scalar2=None, op0=OP.mult)
        v.memset(c.s_st[:], 0.0)
        for h in range(NH):
            ins = v.tensor_scalar(out=c.d_i[:, h * 128:(h + 1) * 128],
                                  in0=c.sid_bf[:], scalar1=c.dcol[:, h:h + 1],
                                  scalar2=None, op0=OP.mult)
        S["prep"].inc(ins)

    mm_g = [0]

    @block.tensor
    def _(t):
        t.wait_ge(S["bc_d"].h, c.cfm_ready[0])
        t.wait_ge(S["tp_e"].h, NTP + 32)
        for ch in range(NCH):
            bank = c.pU if ch % 2 == 0 else c.pV
            if ch >= 2:
                t.wait_ge(S["gt_e"].h, 3 * (ch - 1))
            for sg in range(2):
                ins = t.matmul(bank[0:128, sg * 256:(sg + 1) * 256],
                               c.bcf[0:64, ch * 256 + sg * 128:ch * 256 + (sg + 1) * 128],
                               c.cfm[0:64, ch * 256:(ch + 1) * 256],
                               start=(sg == 0), stop=(sg == 1))
            S["mm"].inc(ins)
            c.mm_n += 1
    mm_g[0] = c.mm_n

    @block.vector
    def _(v):
        for ch in range(NCH):
            bank = c.pU if ch % 2 == 0 else c.pV
            v.wait_ge(S["mm"].h, mm_g[0] - NCH + ch + 1)
            v.tensor_copy(out=c.scr128[0:1, 126:127], in_=c.scr128[0:1, 125:126])
            v.tensor_copy(out=c.scr128[0:1, 126:127], in_=c.scr128[0:1, 125:126])
            S["gt_e"].inc(v.tensor_copy(
                out=c.gtm[:, ch, :, :].rearrange("p a b -> p (a b)"),
                in_=bank[0:128, 0:512]))

    @block.gpsimd
    def _(g):
        for ch in range(NCH):
            g.wait_ge(S["gt_e"].h, 3 * ch + 1)
            for sg in range(2):
                ins = g.affine_select(out=c.gtm[:, ch, sg, :],
                                      in_=c.gtm[:, ch, sg, :],
                                      pattern=[[1, 256]], compare_op=OP.is_ge,
                                      fill=0.0, base=-128 * sg,
                                      channel_multiplier=-1)
                S["gt_e"].inc(ins)


def _p8_ssd(c):
    S, block = c.S, c.block
    NIT = NH * NCH
    mm_b = c.mm_n

    @block.sync
    def _(sy):
        for h in range(2, NH):
            sy.wait_ge(S["ssd_p"].h, (h - 1) * NCH)
            sy.wait_ge(S["yev"].h, (h - 1) * NCH)
            S["bc_d"].inc(sy.dma_start(
                out=c.acsbc[:, h % 2, :],
                in_=c.acs_dram[h:h + 1, :].broadcast_to([128, T])), 16)
            S["bc_d"].inc(sy.dma_start(
                out=c.zbuf[:, h % 2, :],
                in_=c.z_sp[h * 128:(h + 1) * 128, :]), 16)
            c.bc_ready[h] = S["bc_d"].n

    @block.vector
    def _(v):
        v.wait_ge(S["prep"].h, S["prep"].n)
        v.wait_ge(S["gt_e"].h, 3 * NCH)
        for idx in range(NIT + 1):
            if idx >= 1:
                pidx = idx - 1
                h, ch = divmod(pidx, NCH)
                ybank = c.banks[pidx % 2]
                v.wait_ge(S["mm"].h, mm_b + 2 * pidx + 1)
                v.tensor_copy(out=c.scr128[0:1, 126:127], in_=c.scr128[0:1, 125:126])
                v.tensor_copy(out=c.scr128[0:1, 126:127], in_=c.scr128[0:1, 125:126])
                S["yev"].inc(v.tensor_tensor(
                    out=c.yb[:, h, ch * 256:(ch + 1) * 256],
                    in0=ybank[0:128, 0:256],
                    in1=c.zbuf[:, h % 2, ch * 256:(ch + 1) * 256], op=OP.mult))
                sbank = c.banks[2 + pidx % 2]
                v.wait_ge(S["mm"].h, mm_b + 2 * pidx + 2)
                v.tensor_copy(out=c.scr128[0:1, 126:127], in_=c.scr128[0:1, 125:126])
                v.tensor_copy(out=c.scr128[0:1, 126:127], in_=c.scr128[0:1, 125:126])
                S["sup"].inc(v.scalar_tensor_tensor(
                    out=c.s_st[:, h * HD:(h + 1) * HD],
                    in0=c.s_st[:, h * HD:(h + 1) * HD],
                    scalar=c.e2[:, h * NCH + ch:h * NCH + ch + 1],
                    in1=sbank[0:64, 0:128], op0=OP.mult, op1=OP.add))
            if idx < NIT:
                h, ch = divmod(idx, NCH)
                if ch == 0:
                    v.wait_ge(S["bc_d"].h, c.bc_ready[h])
                for sg in range(2):
                    ins = v.tensor_scalar(
                        out=c.dscr[:, sg, :],
                        in0=c.acsbc[:, h % 2, ch * 256:(ch + 1) * 256],
                        scalar1=c.acs_tm[:, ch * 2 + sg, h:h + 1],
                        scalar2=0.0, op0=OP.subtract, op1=OP.min)
                S["ssd_d"].inc(ins)
                v.wait_ge(S["ssd_e"].h, idx + 1)
                for sg in range(2):
                    v.tensor_tensor(out=c.wt[:, idx % 2, sg, :],
                                    in0=c.escr[:, sg, :],
                                    in1=c.gtm[:, ch, sg, :], op=OP.mult)
                v.tensor_tensor(out=c.csc[0:64, idx % 2, :],
                                in0=c.cfm[0:64, ch * 256:(ch + 1) * 256],
                                in1=c.e64f[0:64, :], op=OP.mult)
                for sg in range(2):
                    v.tensor_scalar(out=c.bsc[:, sg, :],
                                    in0=c.btm[:, ch * 2 + sg, :],
                                    scalar1=c.df_tm[:, ch * 2 + sg, h:h + 1],
                                    scalar2=None, op0=OP.mult)
                S["ssd_p"].inc(v.tensor_copy(out=c.sbf[:, :],
                                             in_=c.s_st[:, h * HD:(h + 1) * HD]))

    @block.scalar
    def _(sc_):
        for idx in range(NIT):
            h, ch = divmod(idx, NCH)
            sc_.wait_ge(S["ssd_d"].h, idx + 1)
            if idx >= 1:
                sc_.wait_ge(S["ssd_p"].h, idx)
            for sg in range(2):
                sc_.activation(out=c.escr[:, sg, :], in_=c.dscr[:, sg, :],
                               func=ACT.Exp)
            if ch == 0:
                sc_.wait_ge(S["bc_d"].h, c.bc_ready[h])
            S["ssd_e"].inc(sc_.activation(
                out=c.e64f[0:64, :], in_=c.acsbc[0:64, h % 2, ch * 256:(ch + 1) * 256],
                func=ACT.Exp))

    @block.tensor
    def _(t):
        for idx in range(NIT):
            h, ch = divmod(idx, NCH)
            t.wait_ge(S["ssd_p"].h, idx + 1)
            ybank = c.banks[idx % 2]
            if idx >= 2:
                t.wait_ge(S["yev"].h, idx - 1)
            xs = [c.xsc[:, ch * 2 + sg, h * 128:(h + 1) * 128] for sg in range(2)]
            t.matmul(ybank[0:128, 0:256], xs[0], c.wt[:, idx % 2, 0, :],
                     start=True, stop=False)
            t.matmul(ybank[0:128, 0:256], xs[1], c.wt[:, idx % 2, 1, :],
                     start=False, stop=False)
            t.matmul(ybank[0:128, 0:128], xs[0], c.d_i[:, h * 128:(h + 1) * 128],
                     start=False, stop=False)
            t.matmul(ybank[0:128, 128:256], xs[1], c.d_i[:, h * 128:(h + 1) * 128],
                     start=False, stop=False)
            ins = t.matmul(ybank[0:128, 0:256], c.sbf[:, :],
                           c.csc[0:64, idx % 2, :], start=False, stop=True)
            S["mm"].inc(ins)
            sbank = c.banks[2 + idx % 2]
            if idx >= 2:
                t.wait_ge(S["sup"].h, idx - 1)
            for sg in range(2):
                ins = t.matmul(sbank[0:64, 0:128], c.bsc[:, sg, :], xs[sg],
                               start=(sg == 0), stop=(sg == 1))
            S["mm"].inc(ins)
            c.mm_n += 2


def _p9_rsqrt_outproj(c):
    S, block = c.S, c.block
    NIT = NH * NCH

    @block.sync
    def _(sy):
        sy.wait_ge(S["yev"].h, NIT)
        sy.wait_ge(S["sup"].h, NIT)
        S["ld_wo"].inc(sy.dma_start(
            out=c.ns_col[:, :], in_=c.ns_hbm.rearrange("(a p) -> p a", p=128),
            allow_slow_non_contiguous=True), 16)
        for k in range(NKI):
            if k >= 2:
                sy.wait_ge(S["wos"].h, k - 1)
            S["ld_wo"].inc(sy.dma_start(
                out=c.wo_scr[:, k % 2, :],
                in_=c.w_out_hbm[k * 128:(k + 1) * 128, :]), 16)

    @block.vector
    def _(v):
        v.wait_ge(S["ld_wo"].h, 16)
        for k in range(NKI):
            v.wait_ge(S["ld_wo"].h, (k + 2) * 16)
            S["wos"].inc(v.tensor_scalar(out=c.w_out_bf[:, k, :],
                                         in0=c.wo_scr[:, k % 2, :],
                                         scalar1=c.ns_col[:, k:k + 1],
                                         scalar2=None, op0=OP.mult))

    mm_b = [c.mm_n]

    @block.scalar
    def _(sc_):
        sc_.wait_ge(S["yev"].h, NIT)
        for tt in range(NTT):
            for h in range(NH):
                i = tt * NH + h
                if i >= 2:
                    sc_.wait_ge(S["mm"].h, mm_b[0] + i - 1)
                S["ysqs"].inc(sc_.activation(
                    out=c.ysq[:, i % 2, :], in_=c.yb[:, h, tt * 512:(tt + 1) * 512],
                    func=ACT.Square))

    @block.tensor
    def _(t):
        assert c.mm_n == mm_b[0]
        for tt in range(NTT):
            if tt >= 1:
                t.wait_ge(S["rs_r"].h, tt)
            for h in range(NH):
                i = tt * NH + h
                t.wait_ge(S["ysqs"].h, i + 1)
                ins = t.matmul(c.pW[0:1, 0:512], c.onescol_bf[:, :],
                               c.ysq[:, i % 2, :], start=(h == 0), stop=(h == NH - 1))
                S["mm"].inc(ins)
                c.mm_n += 1

    @block.vector
    def _(v):
        for tt in range(NTT):
            v.wait_ge(S["mm"].h, mm_b[0] + (tt + 1) * NH)
            v.tensor_copy(out=c.scr128[0:1, 126:127], in_=c.scr128[0:1, 125:126])
            v.tensor_copy(out=c.scr128[0:1, 126:127], in_=c.scr128[0:1, 125:126])
            S["rs_r"].inc(v.tensor_copy(out=c.rrow[0:1, tt * 512:(tt + 1) * 512],
                                        in_=c.pW[0:1, 0:512]))
        S["rs_r"].inc(v.tensor_scalar(out=c.rrow2[0:1, :], in0=c.rrow[0:1, :],
                                      scalar1=1.0 / DI, scalar2=c.epsc[0:1, 0:1],
                                      op0=OP.mult, op1=OP.add))

    @block.scalar
    def _(sc_):
        sc_.wait_ge(S["rs_r"].h, 5)
        sc_.activation(out=c.rrow[0:1, :], in_=c.rrow2[0:1, :], func=ACT.Ln)
        S["rs_r"].inc(sc_.activation(out=c.rrow2[0:1, :], in_=c.rrow[0:1, :],
                                     func=ACT.Exp, scale=-0.5))

    @block.sync
    def _(sy):
        sy.wait_ge(S["rs_r"].h, 6)
        S["sws"].inc(sy.dma_start(out=c.row_dram[0:1, :], in_=c.rrow2[0:1, :]), 16)

    @block.gpsimd
    def _(g):
        g.wait_ge(S["sws"].h, S["sws"].n)
        S["rsqd"].inc(g.dma_start(out=c.rsq[:, :],
                                  in_=c.row_dram[0:1, :].broadcast_to([128, T])), 16)

    mm_o = [0]

    @block.tensor
    def _(t):
        mm_o[0] = c.mm_n
        t.wait_ge(S["wos"].h, NKI)
        seq = 0
        for j in range(NKD):
            for tt in range(NTT):
                if seq >= 4:
                    t.wait_ge(S["snv"].h, seq - 3)
                bi = seq % 4
                for k in range(NKI):
                    ins = t.matmul(c.banks[bi][0:128, 0:512],
                                   c.w_out_bf[:, k, j * 128:(j + 1) * 128],
                                   c.yb[:, k, tt * 512:(tt + 1) * 512],
                                   start=(k == 0), stop=(k == NKI - 1))
                S["mm"].inc(ins)
                c.mm_n += 1
                seq += 1

    @block.vector
    def _(v):
        v.wait_ge(S["rsqd"].h, 16)
        seq = 0
        for j in range(NKD):
            for tt in range(NTT):
                v.wait_ge(S["mm"].h, mm_o[0] + seq + 1)
                v.tensor_copy(out=c.scr128[0:1, 126:127], in_=c.scr128[0:1, 125:126])
                v.tensor_copy(out=c.scr128[0:1, 126:127], in_=c.scr128[0:1, 125:126])
                if seq >= 2:
                    v.wait_ge(S["spl2"].h, (192 + seq - 1) * 16)
                ins = v.tensor_tensor(out=c.snst[:, seq % 2, :],
                                      in0=c.banks[seq % 4][0:128, 0:512],
                                      in1=c.rsq[:, tt * 512:(tt + 1) * 512],
                                      op=OP.mult)
                S["snv"].inc(ins)
                seq += 1

    @block.sync
    def _(sy):
        seq = 0
        for j in range(NKD):
            for tt in range(NTT):
                sy.wait_ge(S["snv"].h, seq + 1)
                S["spl2"].inc(sy.dma_start(
                    out=c.sn_sp[j * 128:(j + 1) * 128, tt * 512:(tt + 1) * 512],
                    in_=c.snst[:, seq % 2, :]), 16)
                seq += 1


def _p10_requant(c):
    S, block = c.S, c.block
    sc2 = c.wdq_scr2

    @block.sync
    def _(sy):
        for i in range(96):
            ic, tt = divmod(i, NTT)
            if i >= 2:
                sy.wait_ge(S["rq"].h, i - 1)
            S["ld_ffw"].inc(sy.dma_start(
                out=(c.ffn_a if i % 2 == 0 else c.ffn_b)[:, :],
                in_=c.gate_sp[ic * 128:(ic + 1) * 128, tt * 512:(tt + 1) * 512]), 16)

    ldb = S["ld_ffw"].n - 96 * 16

    @block.vector
    def _(v):
        v.wait_ge(S["sc_bc"].h, 80)
        for _d in range(24):
            v.memset(c.gug[:, 0:4, :], 0.0)
        for i in range(96):
            ic, tt = divmod(i, NTT)
            gqs = c.gq[:, ic, tt * 512:(tt + 1) * 512]
            v.wait_ge(S["ld_ffw"].h, ldb + (i + 1) * 16)
            slot = (c.ffn_a if i % 2 == 0 else c.ffn_b)[:, :]
            v.tensor_scalar(out=sc2[:, 0:512], in0=slot, scalar1=c.scbc[:, 3:4],
                            scalar2=MAGIC, op0=OP.mult, op1=OP.add)
            S["rq_d"].inc(v.tensor_scalar(out=gqs, in0=sc2[:, 0:512],
                                          scalar1=MAGIC, scalar2=None,
                                          op0=OP.subtract))
            v.wait_ge(S["rq_s"].h, i + 1)
            v.tensor_scalar(out=c.ffn_p[:, :], in0=sc2[:, 512:1024],
                            scalar1=65536.0, scalar2=MAGIC, op0=OP.mult, op1=OP.add)
            v.tensor_scalar(out=c.ffn_c[:, :], in0=c.ffn_p[:, :], scalar1=MAGIC,
                            scalar2=65535.0, op0=OP.subtract, op1=OP.min)
            v.tensor_tensor(out=c.ffn_p[:, :], in0=gqs, in1=c.ffn_c[:, :],
                            op=OP.mult)
            v.tensor_reduce(out=c.red[:, 6:7], in_=c.ffn_p[:, :], axis=AX.X,
                            op=OP.max, apply_absolute_value=True)
            if i == 0:
                ins = v.tensor_copy(out=c.red[:, 5:6], in_=c.red[:, 6:7])
            else:
                ins = v.tensor_tensor(out=c.red[:, 5:6], in0=c.red[:, 5:6],
                                      in1=c.red[:, 6:7], op=OP.max)
            S["rq"].inc(ins)

    @block.scalar
    def _(sc_):
        for _d in range(8):
            sc_.copy(out=c.scr128[0:1, 124:125], in_=c.scr128[0:1, 123:124])
        for i in range(96):
            ic, tt = divmod(i, NTT)
            sc_.wait_ge(S["rq_d"].h, i + 1)
            if i >= 1:
                sc_.wait_ge(S["rq"].h, i)
            S["rq_s"].inc(sc_.activation(
                out=sc2[:, 512:1024], in_=c.gq[:, ic, tt * 512:(tt + 1) * 512],
                func=ACT.Sigmoid, scale=c.scbc[:, 4:5]))

    @block.vector
    def _(v):
        v.wait_ge(S["rq"].h, 96)
        S["ar4s"].inc(v.tensor_copy(out=c.scr128[:, 0:1], in_=c.red[:, 5:6]))

    @block.tensor
    def _(t):
        t.wait_ge(S["ar4s"].h, 1)
        S["mm"].inc(t.transpose(c.pU[:, 0:128], c.scr128[:, :], c.sid_f[:]))
        c.mm_n += 1
    mm_t4 = c.mm_n

    @block.vector
    def _(v):
        v.wait_ge(S["mm"].h, mm_t4)
        v.tensor_copy(out=c.scr128[0:1, 126:127], in_=c.scr128[0:1, 125:126])
        v.tensor_copy(out=c.scr128[0:1, 126:127], in_=c.scr128[0:1, 125:126])
        S["ar4s"].inc(v.tensor_reduce(out=c.red[0:1, 6:7], in_=c.pU[0:1, 0:128],
                                      axis=AX.X, op=OP.max))

    @block.gpsimd
    def _(g):
        g.wait_ge(S["ar4s"].h, 2)
        S["sw"].inc(g.dma_start(out=c.cc_in[2][0:1, 0:1], in_=c.red[0:1, 6:7]), 16)
        S["sw"].inc(g.dma_start(out=c.cc_in[2][0:1, 1:8], in_=c.scal[0:1, 1:8]), 16)
        g.wait_ge(S["sw"].h, S["sw"].n)
        S["ar4s"].inc_raw_cc(g.collective_compute(
            "AllReduce", OP.max, replica_groups=[list(range(8))],
            ins=[c.cc_in[2][:, :].opt()], outs=[c.cc_out[2][:, :].opt()]))
        g.wait_ge(S["ar4s"].h, 3)
        S["sw"].inc(g.dma_start(out=c.scal[0:1, 19:20], in_=c.cc_out[2][0:1, 0:1]), 16)
        g.wait_ge(S["sw"].h, S["sw"].n)
        S["ar4s"].inc(g.engine_nop())

    @block.scalar
    def _(sc_):
        sc_.wait_ge(S["ar4s"].h, 4)
        sc_.activation(out=c.scal[0:1, 23:24], in_=c.scal[0:1, 19:20], func=ACT.Ln)
        sc_.activation(out=c.scal[0:1, 24:25], in_=c.scal[0:1, 23:24], func=ACT.Exp,
                       scale=-1.0)
        sc_.mul(out=c.scal[0:1, 20:21], in_=c.scal[0:1, 24:25], mul=QA)
        sc_.mul(out=c.scal[0:1, 25:26], in_=c.scal[0:1, 19:20],
                mul=1.0 / (65536.0 * QA))
        S["ar4s"].inc(sc_.mul(out=c.scal[0:1, 21:22], in_=c.scal[0:1, 25:26],
                              mul=c.scal[0:1, 15:16]))

    @block.sync
    def _(sy):
        sy.wait_ge(S["ar4s"].h, 5)
        S["sws"].inc(sy.dma_start(out=c.sc_dram[0:1, 6:7], in_=c.scal[0:1, 20:21]), 16)
        sy.wait_ge(S["sws"].h, S["sws"].n)
        S["sc_bc"].inc(sy.dma_start(out=c.scbc[:, 5:6],
                                    in_=c.sc_dram[0:1, 6:7].broadcast_to([128, 1])), 16)


def _p11_ophase(c):
    S, block = c.S, c.block
    sc2 = c.wdq_scr2
    wdv0, wds0, wdg0, wdt0 = 96, 48, 96, 48
    p11_mm0 = c.mm_n
    ldb2 = [0]

    @block.sync
    def _(sy):
        ldb2[0] = S["ld_ffw"].n
        sy.wait_ge(S["rq"].h, 96)
        for ci in range(48):
            oc, c4 = divmod(ci, 6)
            if ci >= 2:
                sy.wait_ge(S["wdv"].h, wdv0 + 2 * (ci - 2) + 2)
            S["ld_ffw"].inc(sy.dma_start(
                out=sc2[:, (ci % 2) * 512:(ci % 2) * 512 + 512],
                in_=c.w_down_hbm[oc * 128:(oc + 1) * 128,
                                 c4 * 512:(c4 + 1) * 512]), 16)

    @block.vector
    def _(v):
        v.wait_ge(S["rq"].h, 96)
        for ci in range(48):
            slot = sc2[:, (ci % 2) * 512:(ci % 2) * 512 + 512]
            v.wait_ge(S["ld_ffw"].h, ldb2[0] + (ci + 1) * 16)
            if ci >= 1:
                v.wait_ge(S["wdg"].h, wdg0 + 2 * ci)
                v.wait_ge(S["wds"].h, wds0 + ci)
            v.tensor_reduce(out=c.wbm[:, 0:16].unsqueeze(2),
                            in_=slot.rearrange("p (b k) -> p b k", k=32),
                            axis=AX.X, op=OP.max, apply_absolute_value=True)
            S["wdv"].inc(v.tensor_scalar(out=c.wdmx[:, 0:16], in0=c.wbm[:, 0:16],
                                         scalar1=c.c_e7[:, 0:1], scalar2=None,
                                         op0=OP.max))
            v.wait_ge(S["wdg"].h, wdg0 + 2 * ci + 1)
            v.tensor_scalar(out=c.ffn_p[:, :], in0=c.ffn_p[:, :], scalar1=MAGIC,
                            scalar2=None, op0=OP.add)
            S["wdv"].inc(v.tensor_scalar(out=c.wdq_q2[:, 0:512], in0=c.ffn_p[:, :],
                                         scalar1=MAGIC, scalar2=None,
                                         op0=OP.subtract))
            oc, c4 = divmod(ci, 6)
            v.wait_ge(S["mm"].h, p11_mm0 + ci + 1)
            v.tensor_copy(out=c.scr128[0:1, 126:127], in_=c.scr128[0:1, 125:126])
            v.tensor_copy(out=c.scr128[0:1, 126:127], in_=c.scr128[0:1, 125:126])
            S["wdt"].inc(v.tensor_copy(
                out=c.wq_d[:, c4 * 4:(c4 + 1) * 4, oc * 128:(oc + 1) * 128],
                in_=c.pT[:, 0:512].rearrange("p (a b) -> p a b", a=4)))

    @block.scalar
    def _(sc_):
        for ci in range(48):
            sc_.wait_ge(S["wdv"].h, wdv0 + 2 * ci + 1)
            if ci >= 1:
                sc_.wait_ge(S["wdg"].h, wdg0 + 2 * ci)
            sc_.activation(out=c.wlna[:, 0:16], in_=c.wdmx[:, 0:16], func=ACT.Ln)
            sc_.activation(out=c.wlnb[:, 0:16], in_=c.wlna[:, 0:16], func=ACT.Exp,
                           scale=-1.0)
            sc_.mul(out=c.wrs[:, 0:16], in_=c.wlnb[:, 0:16], mul=QW)
            S["wds"].inc(sc_.mul(out=c.wsc[:, 0:16], in_=c.wdmx[:, 0:16],
                                 mul=1.0 / QW))

    @block.gpsimd
    def _(g):
        for ci in range(48):
            slot = sc2[:, (ci % 2) * 512:(ci % 2) * 512 + 512]
            g.wait_ge(S["wds"].h, wds0 + ci + 1)
            g.tensor_tensor(out=c.nwt1[:, 0:16], in0=c.wdmx[:, 0:16],
                            in1=c.wlnb[:, 0:16], op=OP.mult)
            g.tensor_scalar(out=c.nwt2[:, 0:16], in0=c.nwt1[:, 0:16],
                            scalar1=-1.0, scalar2=2.0, op0=OP.mult, op1=OP.add)
            g.tensor_tensor(out=c.nwt1[:, 0:16], in0=c.wlnb[:, 0:16],
                            in1=c.nwt2[:, 0:16], op=OP.mult)
            g.tensor_scalar(out=c.wrs[:, 0:16], in0=c.nwt1[:, 0:16],
                            scalar1=QW, scalar2=None, op0=OP.mult)
            S["wdg"].inc(g.tensor_tensor(
                out=c.ffn_p[:].rearrange("p (b k) -> p b k", k=32),
                in0=slot.rearrange("p (b k) -> p b k", k=32),
                in1=c.wrs[:, 0:16].unsqueeze(2).broadcast_to([128, 16, 32]),
                op=OP.mult))
            g.wait_ge(S["wdv"].h, wdv0 + 2 * ci + 2)
            if ci >= 1:
                g.wait_ge(S["wdt"].h, wdt0 + ci)
            S["wdg"].inc(g.tensor_tensor(
                out=c.wdq_n2[:, 0:512].rearrange("p (b k) -> p b k", k=32),
                in0=c.wdq_q2[:, 0:512].rearrange("p (b k) -> p b k", k=32),
                in1=c.wsc[:, 0:16].unsqueeze(2).broadcast_to([128, 16, 32]),
                op=OP.mult))

    @block.tensor
    def _(t):
        assert c.mm_n == p11_mm0
        for ci in range(48):
            t.wait_ge(S["wdg"].h, wdg0 + 2 * ci + 2)
            if ci >= 1:
                t.wait_ge(S["wdt"].h, wdt0 + ci)
            for k in range(4):
                ins = t.transpose(c.pT[:, k * 128:(k + 1) * 128],
                                  c.wdq_n2[:, k * 128:(k + 1) * 128],
                                  c.sid_bf[:])
            S["mm"].inc(ins)
            c.mm_n += 1

    ldb3 = [0]

    @block.sync
    def _(sy):
        ldb3[0] = S["ld_ffw"].n
        for i in range(96):
            ic, tt = divmod(i, NTT)
            if i >= 2:
                sy.wait_ge(S["oq"].h, i - 1)
            S["ld_ffw"].inc(sy.dma_start(
                out=(c.ffn_a if i % 2 == 0 else c.ffn_b)[:, :],
                in_=c.up_sp[ic * 128:(ic + 1) * 128, tt * 512:(tt + 1) * 512]), 16)

    @block.vector
    def _(v):
        v.wait_ge(S["wdt"].h, wdt0 + 48)
        v.wait_ge(S["sc_bc"].h, 96)
        for _d in range(24):
            v.memset(c.gug[:, 0:4, :], 0.0)
        for i in range(96):
            ic, tt = divmod(i, NTT)
            gqs = c.gq[:, ic, tt * 512:(tt + 1) * 512]
            v.wait_ge(S["rq_s"].h, 96 + i + 1)
            v.tensor_scalar(out=c.ffn_p[:, :], in0=sc2[:, 512:1024],
                            scalar1=65536.0, scalar2=MAGIC, op0=OP.mult, op1=OP.add)
            v.tensor_scalar(out=c.ffn_c[:, :], in0=c.ffn_p[:, :], scalar1=MAGIC,
                            scalar2=65535.0, op0=OP.subtract, op1=OP.min)
            v.tensor_tensor(out=c.ffn_p[:, :], in0=gqs, in1=c.ffn_c[:, :], op=OP.mult)
            v.tensor_scalar(out=sc2[:, 0:512], in0=c.ffn_p[:, :],
                            scalar1=c.scbc[:, 5:6], scalar2=MAGIC,
                            op0=OP.mult, op1=OP.add)
            S["rq_d"].inc(v.tensor_scalar(out=gqs, in0=sc2[:, 0:512],
                                          scalar1=MAGIC, scalar2=None,
                                          op0=OP.subtract))
            v.wait_ge(S["ld_ffw"].h, ldb3[0] + (i + 1) * 16)
            slot = (c.ffn_a if i % 2 == 0 else c.ffn_b)[:, :]
            v.tensor_scalar(out=sc2[:, 0:512], in0=slot, scalar1=c.scbc[:, 2:3],
                            scalar2=MAGIC, op0=OP.mult, op1=OP.add)
            v.tensor_scalar(out=c.ffn_d[:, :], in0=sc2[:, 0:512], scalar1=MAGIC,
                            scalar2=None, op0=OP.subtract)
            v.tensor_tensor(out=sc2[:, 0:512], in0=gqs, in1=c.ffn_d[:, :],
                            op=OP.mult)
            S["oq"].inc(v.tensor_reduce(
                out=c.scr128[:, i + 1:i + 2], in_=sc2[:, 0:512], axis=AX.X,
                op=OP.max, apply_absolute_value=True))

    @block.scalar
    def _(sc_):
        sc_.wait_ge(S["wdg"].h, 192)
        for _d in range(8):
            sc_.copy(out=c.scr128[0:1, 124:125], in_=c.scr128[0:1, 123:124])
        for i in range(96):
            ic, tt = divmod(i, NTT)
            if i >= 1:
                sc_.wait_ge(S["rq_d"].h, 96 + i)
            S["rq_s"].inc(sc_.activation(
                out=sc2[:, 512:1024], in_=c.gq[:, ic, tt * 512:(tt + 1) * 512],
                func=ACT.Sigmoid, scale=c.scbc[:, 4:5]))

    @block.vector
    def _(v):
        v.wait_ge(S["oq"].h, 96)
        S["ar5s"].inc(v.tensor_reduce(out=c.scr128[:, 0:1],
                                      in_=c.scr128[:, 1:97], axis=AX.X,
                                      op=OP.max))

    @block.tensor
    def _(t):
        t.wait_ge(S["ar5s"].h, 1)
        S["mm"].inc(t.transpose(c.pU[:, 0:128], c.scr128[:, :], c.sid_f[:]))
        c.mm_n += 1
    mm_t5 = c.mm_n

    @block.vector
    def _(v):
        v.wait_ge(S["mm"].h, mm_t5)
        v.tensor_copy(out=c.scr128[0:1, 126:127], in_=c.scr128[0:1, 125:126])
        v.tensor_copy(out=c.scr128[0:1, 126:127], in_=c.scr128[0:1, 125:126])
        S["ar5s"].inc(v.tensor_reduce(out=c.red[0:1, 6:7], in_=c.pU[0:1, 0:128],
                                      axis=AX.X, op=OP.max))

    @block.gpsimd
    def _(g):
        g.wait_ge(S["ar5s"].h, 2)
        S["sw"].inc(g.dma_start(out=c.cc_in[3][0:1, 0:1], in_=c.red[0:1, 6:7]), 16)
        S["sw"].inc(g.dma_start(out=c.cc_in[3][0:1, 1:8], in_=c.scal[0:1, 1:8]), 16)
        g.wait_ge(S["sw"].h, S["sw"].n)
        S["ar5s"].inc_raw_cc(g.collective_compute(
            "AllReduce", OP.max, replica_groups=[list(range(8))],
            ins=[c.cc_in[3][:, :].opt()], outs=[c.cc_out[3][:, :].opt()]))
        g.wait_ge(S["ar5s"].h, 3)
        S["sw"].inc(g.dma_start(out=c.scal[0:1, 22:23], in_=c.cc_out[3][0:1, 0:1]), 16)
        g.wait_ge(S["sw"].h, S["sw"].n)
        S["ar5s"].inc(g.engine_nop())

    @block.scalar
    def _(sc_):
        sc_.wait_ge(S["ar5s"].h, 4)
        sc_.activation(out=c.scal[0:1, 23:24], in_=c.scal[0:1, 22:23], func=ACT.Ln)
        sc_.activation(out=c.scal[0:1, 24:25], in_=c.scal[0:1, 23:24], func=ACT.Exp,
                       scale=-1.0)
        sc_.mul(out=c.scal[0:1, 26:27], in_=c.scal[0:1, 24:25], mul=QA)
        sc_.mul(out=c.scal[0:1, 27:28], in_=c.scal[0:1, 22:23], mul=1.0 / QA)
        sc_.mul(out=c.scal[0:1, 28:29], in_=c.scal[0:1, 27:28],
                mul=c.scal[0:1, 21:22])
        S["ar5s"].inc(sc_.mul(out=c.scal[0:1, 29:30], in_=c.scal[0:1, 28:29],
                              mul=c.scal[0:1, 14:15]))

    @block.sync
    def _(sy):
        sy.wait_ge(S["ar5s"].h, 5)
        S["sws"].inc(sy.dma_start(out=c.sc_dram[0:1, 7:8], in_=c.scal[0:1, 26:27]), 16)
        S["sws"].inc(sy.dma_start(out=c.sc_dram[0:1, 8:9], in_=c.scal[0:1, 29:30]), 16)
        sy.wait_ge(S["sws"].h, S["sws"].n)
        S["sc_bc"].inc(sy.dma_start(out=c.scbc[:, 6:7],
                                    in_=c.sc_dram[0:1, 7:8].broadcast_to([128, 1])), 16)
        S["sc_bc"].inc(sy.dma_start(out=c.scbc[:, 7:8],
                                    in_=c.sc_dram[0:1, 8:9].broadcast_to([128, 1])), 16)


def _p12_final(c):
    S, block = c.S, c.block
    sc2 = c.wdq_scr2
    ldb4 = [0]
    mm12 = c.mm_n
    c.spl2_p12 = c.S["spl2"].n

    @block.sync
    def _(sy):
        ldb4[0] = S["ld_ffw"].n
        sy.wait_ge(S["oq"].h, 96)
        for ht in range(8):
            for ic in range(NCF):
                i = ht * NCF + ic
                if i >= 2:
                    sy.wait_ge(S["oq2"].h, i - 1)
                S["ld_ffw"].inc(sy.dma_start(
                    out=(c.ffn_a if i % 2 == 0 else c.ffn_b)[:, 0:256],
                    in_=c.up_sp[ic * 128:(ic + 1) * 128,
                                ht * 256:(ht + 1) * 256]), 16)
            for j in range(NKD):
                sj = ht * NKD + j
                if sj >= 2:
                    sy.wait_ge(S["fsg"].h, sj - 1)
                S["spl2"].inc(sy.dma_start(
                    out=c.snbuf[:, j % 2, :],
                    in_=c.sn_sp[j * 128:(j + 1) * 128, ht * 256:(ht + 1) * 256]), 16)
            for ts in range(2):
                g_ts = ht * 2 + ts
                if g_ts >= 1:
                    sy.wait_ge(S["fx"].h, g_ts)
                S["ld_x"].inc(sy.dma_start(
                    out=c.fin_x[:, :],
                    in_=c.x_hbm[g_ts * 128:(g_ts + 1) * 128, :]), 16)
                sy.wait_ge(S["fx"].h, g_ts + 1)
                S["fout"].inc(sy.dma_start(
                    out=c.out_hbm[g_ts * 128:(g_ts + 1) * 128, :],
                    in_=c.fin_o[:, :]), 16)

    @block.vector
    def _(v):
        v.wait_ge(S["sc_bc"].h, 128)
        for _d in range(24):
            v.memset(c.gug[:, 0:4, :], 0.0)
        snb = 0
        for ht in range(8):
            if ht >= 1:
                v.wait_ge(S["mm"].h, mm12 + (ht - 1) * 16 + 8)
            for ic in range(NCF):
                i = ht * NCF + ic
                gqs = c.gq[:, ic, ht * 256:(ht + 1) * 256]
                v.wait_ge(S["ld_ffw"].h, ldb4[0] + (i + 1) * 16)
                slot = (c.ffn_a if i % 2 == 0 else c.ffn_b)[:, 0:256]
                v.tensor_scalar(out=sc2[:, 0:256], in0=slot, scalar1=c.scbc[:, 2:3],
                                scalar2=MAGIC, op0=OP.mult, op1=OP.add)
                v.tensor_scalar(out=c.ffn_d[:, 0:256], in0=sc2[:, 0:256],
                                scalar1=MAGIC, scalar2=None, op0=OP.subtract)
                v.tensor_tensor(out=c.ffn_p[:, 0:256], in0=gqs,
                                in1=c.ffn_d[:, 0:256], op=OP.mult)
                v.tensor_scalar(out=sc2[:, 256:512], in0=c.ffn_p[:, 0:256],
                                scalar1=c.scbc[:, 6:7], scalar2=MAGIC,
                                op0=OP.mult, op1=OP.add)
                v.tensor_scalar(out=c.ffn_c[:, 0:256], in0=sc2[:, 256:512],
                                scalar1=MAGIC, scalar2=None, op0=OP.subtract)
                S["oq2"].inc(v.tensor_scalar(out=c.ofm[:, ic, :],
                                             in0=c.ffn_c[:, 0:256],
                                             scalar1=c.scbc[:, 7:8],
                                             scalar2=None, op0=OP.mult))
            for j in range(NKD):
                sj = ht * NKD + j
                v.wait_ge(S["fg"].h, sj + 1)
                v.wait_ge(S["spl2"].h, c.spl2_p12 + (sj + 1) * 16)
                if sj >= 1:
                    prev_pair = (ht * 16 + 8 + j) if j >= 1 else ((ht - 1) * 16 + 16)
                    v.wait_ge(S["mm"].h, mm12 + prev_pair)
                S["fsg"].inc(v.tensor_tensor(
                    out=c.fin_a[:, 0:256],
                    in0=c.snbuf[:, j % 2, :], in1=c.gug[:, j, :],
                    op=OP.mult))
            for ts in range(2):
                g_ts = ht * 2 + ts
                v.wait_ge(S["mm"].h, mm12 + ht * 16 + 16)
                if g_ts >= 1:
                    v.wait_ge(S["fout"].h, g_ts * 16)
                bA = (c.pU, c.pV) if ts == 0 else (c.pW, c.pD)
                v.wait_ge(S["ld_x"].h, 256 + (g_ts + 1) * 16)
                v.tensor_copy(out=c.scr128[0:1, 126:127], in_=c.scr128[0:1, 125:126])
                v.tensor_copy(out=c.scr128[0:1, 126:127], in_=c.scr128[0:1, 125:126])
                v.tensor_tensor(out=c.fin_o[:, 0:512], in0=bA[0][0:128, 0:512],
                                in1=c.fin_x[:, 0:512], op=OP.add)
                S["fx"].inc(v.tensor_tensor(out=c.fin_o[:, 512:1024],
                                            in0=bA[1][0:128, 0:512],
                                            in1=c.fin_x[:, 512:1024],
                                            op=OP.add))

    @block.tensor
    def _(t):
        assert c.mm_n == mm12
        t.wait_ge(S["wdt"].h, 96)
        for ht in range(8):
            for j in range(NKD):
                sj = ht * NKD + j
                t.wait_ge(S["oq2"].h, (ht + 1) * NCF)
                if sj >= 2:
                    t.wait_ge(S["fg"].h, sj - 1)
                bi = sj % 2
                for k in range(NCF):
                    ins = t.matmul(c.banks[bi][0:128, 0:256],
                                   c.wq_d[:, k, j * 128:(j + 1) * 128],
                                   c.ofm[:, k, :], start=(k == 0),
                                   stop=(k == NCF - 1))
                S["mm"].inc(ins)
                c.mm_n += 1
            for j in range(NKD):
                sj = ht * NKD + j
                t.wait_ge(S["fsg"].h, sj + 1)
                if ht >= 1:
                    t.wait_ge(S["fx"].h, 2 * ht)
                dst0 = c.pU if j < 4 else c.pV
                dst1 = c.pW if j < 4 else c.pD
                jc = (j % 4) * 128
                t.transpose(dst0[0:128, jc:jc + 128], c.fin_a[:, 0:128], c.sid_f[:])
                ins = t.transpose(dst1[0:128, jc:jc + 128], c.fin_a[:, 128:256],
                                  c.sid_f[:])
                S["mm"].inc(ins)
                c.mm_n += 1

    @block.scalar
    def _(sc_):
        for ht in range(8):
            for j in range(NKD):
                sj = ht * NKD + j
                sc_.wait_ge(S["mm"].h, mm12 + ht * 16 + j + 1)
                if ht >= 1:
                    sc_.wait_ge(S["fsg"].h, (ht - 1) * NKD + j + 1)
                sc_.copy(out=c.scr128[0:1, 124:125], in_=c.scr128[0:1, 123:124])
                sc_.copy(out=c.scr128[0:1, 124:125], in_=c.scr128[0:1, 123:124])
                S["fg"].inc(sc_.activation(out=c.gug[:, j, :],
                                           in_=c.banks[sj % 2][0:128, 0:256],
                                           func=ACT.Silu))


def _taps(c):
    if c.stage_limit >= 99:
        return
    S, block = c.S, c.block

    @block.gpsimd
    def _(g):
        if c.stage_limit >= 7:
            g.wait_ge(S["fout"].h, S["fout"].n)
            S["dbgs"].inc(g.dma_start(out=c.dbg["scal2"][:, :], in_=c.scal[0:1, :]), 16)
            for k in range(NCF):
                S["dbgs"].inc(g.dma_start(
                    out=c.dbg["wqd"][k * 128:(k + 1) * 128, :],
                    in_=c.wq_d[:, k, :]), 16)
            for ic in range(NCF):
                S["dbgs"].inc(g.dma_start(
                    out=c.dbg["g2d"][ic * 128:(ic + 1) * 128, :],
                    in_=c.gq[:, ic, :]), 16)
        g.wait_ge(S["tp_e"].h, NTP)
        g.wait_ge(S["sc_bc"].h, 32)
        if c.stage_limit == 1:
            for i in range(NTP):
                S["dbgs"].inc(g.dma_start(out=c.dbg["xn_tm"][i * 128:(i + 1) * 128, :],
                                          in_=c.xn_tm[:, i, :]), 16)
        S["dbgs"].inc(g.dma_start(out=c.dbg["gmax"][:, :], in_=c.scal[0:1, 0:16]), 16)
        S["dbgs"].inc(g.dma_start(out=c.dbg["lnst"][:, :], in_=c.ln_st[:, :]), 16)
        S["dbgs"].inc(g.dma_start(out=c.dbg["red"][:, :], in_=c.red[:, :]), 16)
        if c.stage_limit >= 7:
            S["dbgs"].inc(g.dma_start(out=c.dbg["s128"][:, :], in_=c.scr128[:, :]), 16)
        if c.stage_limit >= 2:
            g.wait_ge(S["ip_e"].h, len(c.ip_groups))
            g.wait_ge(S["spl"].h, S["spl"].n)
            if c.stage_limit == 2:
                for i in range(16):
                    S["dbgs"].inc(g.dma_start(
                        out=c.dbg["xbc"][i * 128:(i + 1) * 128, :],
                        in_=c.xbc[:, i, :]), 16)
                S["dbgs"].inc(g.dma_start(
                    out=c.dbg["xbc"][16 * 128:17 * 128, :], in_=c.bc_tmp[:, :]), 16)
            S["dbgs"].inc(g.dma_start(out=c.dbg["z"][:, :], in_=c.z_sp[:, :]), 16)
            S["dbgs"].inc(g.dma_start(out=c.dbg["dtr"][:, :], in_=c.dt_dram[:, :]), 16)
        if c.stage_limit >= 5:
            g.wait_ge(S["yev"].h, NH * NCH)
            g.wait_ge(S["sup"].h, NH * NCH)
            for h in range(NH):
                S["dbgs"].inc(g.dma_start(
                    out=c.dbg["y"][h * 128:(h + 1) * 128, :], in_=c.yb[:, h, :]), 16)
        if c.stage_limit in (5, 6):
            for i in range(NTP):
                S["dbgs"].inc(g.dma_start(
                    out=c.dbg["xsc2"][i * 128:(i + 1) * 128, :], in_=c.xsc[:, i, :]), 16)
        if c.stage_limit == 4:
            g.wait_ge(S["spl2"].h, 192 * 16)
            g.wait_ge(S["wdt"].h, 48)
            for k in range(NKD):
                S["dbgs"].inc(g.dma_start(
                    out=c.dbg["xqv"][k * 128:(k + 1) * 128, :],
                    in_=c.xn_fm[:, k, :]), 16)
                S["dbgs"].inc(g.dma_start(
                    out=c.dbg["wq0"][k * 128:(k + 1) * 128, :],
                    in_=c.wq_h[1][:, k, :]), 16)
        elif c.stage_limit >= 5:
            g.wait_ge(S["spl2"].h, 192 * 16)
        if c.stage_limit >= 6:
            g.wait_ge(S["spl2"].h, 224 * 16)
            S["dbgs"].inc(g.dma_start(out=c.dbg["sn"][:, :], in_=c.sn_sp[:, :]), 16)
            S["dbgs"].inc(g.dma_start(out=c.dbg["rsq"][:, :], in_=c.rsq[:, :]), 16)
            S["dbgs"].inc(g.dma_start(out=c.dbg["rrow"][:, :], in_=c.row_dram[0:1, :]), 16)
            g.wait_ge(S["ar2"].h, 6)
            g.wait_ge(S["sc_bc"].h, S["sc_bc"].n)
            S["dbgs"].inc(g.dma_start(out=c.dbg["upt"][:, :], in_=c.up_sp[:, :]), 16)
            S["dbgs"].inc(g.dma_start(out=c.dbg["gt"][:, :], in_=c.gate_sp[:, :]), 16)
        if c.stage_limit == 3:
            g.wait_ge(S["cvs"].h, 17)
            g.wait_ge(S["prep"].h, 1)
            g.wait_ge(S["dts"].h, 4)
            for i in range(16):
                S["dbgs"].inc(g.dma_start(
                    out=c.dbg["conv"][i * 128:(i + 1) * 128, :],
                    in_=c.xbc[:, i, :]), 16)
            S["dbgs"].inc(g.dma_start(
                out=c.dbg["conv"][16 * 128:17 * 128, :], in_=c.bcf[:, :]), 16)
            S["dbgs"].inc(g.dma_start(out=c.dbg["acs"][:, :], in_=c.acs_fm[:, :]), 16)
            S["dbgs"].inc(g.dma_start(out=c.dbg["dtv"][:, :], in_=c.dt_fm[:, :]), 16)
        g.wait_ge(S["dbgs"].h, S["dbgs"].n)


def _finish(c):
    S, block = c.S, c.block

    @block.sync
    def _(sy):
        if S["fout"].n:
            sy.wait_ge(S["fout"].h, S["fout"].n)


def make_in_maps(inputs):
    u = np.ascontiguousarray(inputs["L0_attn_u"], dtype=np.float32)
    conv_w = np.ascontiguousarray(inputs["L0_attn_conv_w"][:, 0, :], dtype=np.float32)
    shared = {
        "w_in": np.ascontiguousarray(inputs["L0_attn_W_in"], dtype=np.float32),
        "conv_w": conv_w,
        "conv_b": np.ascontiguousarray(inputs["L0_attn_conv_b"], dtype=np.float32),
        "dt_bias": np.ascontiguousarray(inputs["L0_attn_dt_bias"], dtype=np.float32),
        "a_log": np.ascontiguousarray(inputs["L0_attn_A_log"], dtype=np.float32),
        "dvec": np.ascontiguousarray(inputs["L0_attn_D"], dtype=np.float32),
        "norm_scale": np.ascontiguousarray(inputs["L0_attn_norm_scale"], dtype=np.float32),
        "w_out": np.ascontiguousarray(inputs["L0_attn_W_out"], dtype=np.float32),
        "w_gate": np.ascontiguousarray(inputs["L0_ffn_w_gate"], dtype=np.float32),
        "w_up": np.ascontiguousarray(inputs["L0_ffn_w_up"], dtype=np.float32),
        "w_down": np.ascontiguousarray(inputs["L0_ffn_w_down"], dtype=np.float32),
    }
    return [dict(shared, x=np.ascontiguousarray(u[i])) for i in range(8)]


def kernel(**inputs):
    nc, _ = build(stage_limit=99)
    in_maps = make_in_maps(inputs)
    res = run_bass_kernel_spmd(nc, in_maps, core_ids=list(range(8)))
    return np.stack([res.results[i]["out"] for i in range(8)], axis=0)
